# version 13
# baseline (speedup 1.0000x reference)
from contextlib import ExitStack
import numpy as np
import ml_dtypes
import concourse.bass as bass
import concourse.mybir as mybir
from concourse.bass_utils import run_bass_kernel_spmd

F32 = mybir.dt.float32
BF16 = mybir.dt.bfloat16
AF = mybir.ActivationFunctionType
ALU = mybir.AluOpType
AX = mybir.AxisListType

D = 1024
NH = 16
HD = 64
NHG = 8
DFF = 4096
NE = 8
NDT = D // 128
FC = 512
NFT = FC // 128


def chunks(total, size):
    return [(i, min(size, total - i)) for i in range(0, total, size)]


class Sems:
    def __init__(self, nc, names):
        self.h = {}
        self.n = {}
        self.lastw = {}
        self.cms = []
        for name in names:
            cm = nc.semaphore(name)
            self.cms.append(cm)
            self.h[name] = cm.__enter__()
            self.n[name] = 0

    def inc(self, name, instr, amount=1):
        self.n[name] += amount
        instr.then_inc(self.h[name], amount)
        return (name, self.n[name])

    def wait(self, eng, who, tok):
        if tok is None:
            return
        name, val = tok
        assert val <= self.n[name], f"wait on future {name}: {val} > {self.n[name]}"
        key = (who, name)
        if self.lastw.get(key, 0) >= val:
            return
        eng.wait_ge(self.h[name], val)
        self.lastw[key] = val

    def close(self):
        for cm in reversed(self.cms):
            cm.__exit__(None, None, None)


def build_program(T_B, T_C, debug_taps=False):
    NT_B = T_B // 128
    NT_C = T_C // 128
    BLK = min(1024, T_B)
    NBLK = T_B // BLK
    NBT = BLK // 128
    QC = min(1024, T_C)
    NQC = T_C // QC

    nc = bass.Bass("TRN2", target_bir_lowering=False, debug=False)

    din = {}
    def DI(name, shape, dtype=F32):
        din[name] = nc.dram_tensor(name, shape, dtype, kind="ExternalInput").ap()
        return din[name]

    x_b = DI("x_b", [T_B, D])
    cosk = DI("cosk", [128, T_B])
    s1k = DI("s1k", [128, T_B])
    idn_i = DI("idn", [128, 128], BF16)
    perm_i = DI("permbd", [128, 128])
    wq_r = DI("wq_r", [NHG, D, 128])
    wk_r = DI("wk_r", [NHG, D, 128])
    wv_r = DI("wv_r", [NHG // 2, D, 256])
    bq_i = DI("bq", [D])
    bk_i = DI("bk", [D])
    bv_i = DI("bv", [D])
    wo_i = DI("wo", [D, D])
    bo_i = DI("bo", [D])
    g1_i = DI("g1", [D])
    be1_i = DI("be1", [D])
    g2_i = DI("g2", [D])
    be2_i = DI("be2", [D])
    w1_r = DI("w1_r", [DFF // FC, D, FC])
    b1_i = DI("b1_r", [DFF // FC, 128, FC // 128])
    w2_i = DI("w2", [DFF, D])
    b2_i = DI("b2", [D])
    rwt_i = DI("rwt", [D, NE])
    wg_r = DI("wg_r", [NE, DFF // FC, D, FC])
    wu_r = DI("wu_r", [NE, DFF // FC, D, FC])
    wd_i = DI("wd", [NE, DFF, D])

    out_t = nc.dram_tensor("out", [T_C, D], F32, kind="ExternalOutput").ap()
    spill = nc.dram_tensor("xn1t_spill", [NDT, 128, T_B], BF16, kind="Internal").ap()
    h1sp = nc.dram_tensor("h1_spill", [T_C, D], F32, kind="Internal").ap()
    vsp = nc.dram_tensor("v_spill", [128, T_B // 128, 128], BF16, kind="Internal").ap()

    taps = {}
    if debug_taps:
        for name, shape, dt in [
            ("tap_xn1", [128, D], BF16), ("tap_khat", [65, T_B], BF16),
            ("tap_vbuf", [128, 65], BF16), ("tap_qhat", [65, T_C], BF16),
            ("tap_attnT", [128, T_C], BF16), ("tap_xz", [128, D], F32),
            ("tap_xn2T", [128, T_C], BF16), ("tap_h1", [128, D], F32),
            ("tap_cw", [128, NE], F32),
        ]:
            taps[name] = nc.dram_tensor(name, shape, dt, kind="ExternalOutput").ap()

    ctx = []
    def ent(cm):
        ctx.append(cm)
        return cm.__enter__()

    sb = lambda n, s, d: ent(nc.sbuf_tensor(n, s, d))

    S = Sems(nc, ["sprow", "sptap", "spx0", "spx1", "spx2", "spx3", "spst0", "spst1",
              "spxb0", "spxb1", "sph1s", "sph1t", "spout", "spvs", "spvr",
              "gpw", "gpw10", "gpw11", "gpd0", "gpd1",
              "pe", "act", "dve", "pool"])

    s_idn = sb("s_idn", [128, 128], BF16)
    s_perm = sb("s_perm", [128, 128], F32)
    s_eps = sb("s_eps", [128, 1], F32)
    s_one1 = sb("s_one1", [1, 128], F32)
    s_ones64 = sb("s_ones64", [64, 1], BF16)
    s_bones = sb("s_bones", [128, 2], BF16)
    s_ln102 = sb("s_ln102", [1, 1], F32)
    s_mm1 = [sb("s_mm1a", [128, 1], F32), sb("s_mm1b", [128, 1], F32)]
    s_mm2 = [sb("s_mm2a", [128, 1], F32), sb("s_mm2b", [128, 1], F32)]
    s_ss = [sb("s_ssa", [128, 1], F32), sb("s_ssb", [128, 1], F32)]
    s_rstd = [sb("s_rstda", [128, 1], F32), sb("s_rstdb", [128, 1], F32)]
    s_cc = [sb("s_cca", [128, D], F32), sb("s_ccb", [128, D], F32)]
    s_cc2 = [sb("s_cc2a", [128, D], F32), sb("s_cc2b", [128, D], F32)]
    s_yy = [sb("s_yya", [128, D], F32), sb("s_yyb", [128, D], F32)]

    block = ent(nc.Block())
    T = {}

    def f(v):
        v.memset(s_eps[:], 1e-5)
        v.memset(s_one1[:], 1.0)
        v.memset(s_ones64[:], 1.0)
        ins = v.memset(s_bones[:], 0.0)
        t = S.inc("dve", ins)
        S.wait(v, "DVE", t)
        v.memset(s_bones[0:64, 0:1], 1.0)
        v.memset(s_bones[64:128, 1:2], 1.0)
        ins = v.memset(s_ln102[:], float(np.log(1.05)))
        T["init_dve"] = S.inc("dve", ins)
    block.vector(f)

    def load_act_set(a, set_id):
        a.add_instruction(mybir.InstLoadActFuncSet(
            name=nc.get_next_instruction_name(),
            act_func_set_id=set_id, ins=[], outs=[]))

    def f(a):
        load_act_set(a, 6)
    block.scalar(f)

    def f(sync):
        ins = sync.dma_start(s_idn[:], idn_i)
        S.inc("sprow", ins, 16)
        ins = sync.dma_start(s_perm[:], perm_i)
        T["init_sp"] = S.inc("sprow", ins, 16)
    block.sync(f)

    def emit_ln(src_ap, pre_toks, tag, par=0):
        def f(v, pre=list(pre_toks)):
            S.wait(v, "DVE", T.get(f"ln_guard{par}"))
            for t in pre:
                S.wait(v, "DVE", t)
            ins = v.reduce_sum(s_mm1[par][:], src_ap, axis=AX.X)
            T[tag + "_sum"] = S.inc("dve", ins)
        block.vector(f)

        def f(a):
            S.wait(a, "ACT", T[tag + "_sum"])
            ins = a.mul(s_mm2[par][:], s_mm1[par][:], -1.0 / D)
            T[tag + "_mn"] = S.inc("act", ins)
        block.scalar(f)

        def f(v):
            S.wait(v, "DVE", T[tag + "_mn"])
            ins = v.tensor_scalar(s_cc[par][:], src_ap, s_mm2[par][:], None, ALU.add)
            T[tag + "_c"] = S.inc("dve", ins)
        block.vector(f)

        def f(a):
            S.wait(a, "ACT", T[tag + "_c"])
            ins = a.activation(s_cc2[par][:], s_cc[par][:], AF.Square,
                               accum_out=s_ss[par][:])
            t = S.inc("act", ins)
            S.wait(a, "ACT", t)
            ins = a.activation(s_ss[par][:], s_ss[par][:], AF.Ln, bias=s_eps[:],
                               scale=1.0 / D)
            t = S.inc("act", ins)
            S.wait(a, "ACT", t)
            ins = a.activation(s_rstd[par][:], s_ss[par][:], AF.Exp, scale=-0.5)
            T[tag + "_rstd"] = S.inc("act", ins)
        block.scalar(f)
        return T[tag + "_rstd"], T[tag + "_c"]

    def emit_ln2(src_ap, gb_ap, bb_ap, out_ap, pre_toks, tag, par=0):
        t_r, _ = emit_ln(src_ap, pre_toks, tag + "a", par)

        def f(v):
            S.wait(v, "DVE", t_r)
            ins = v.tensor_scalar(s_yy[par][:], s_cc[par][:], s_rstd[par][:],
                                  None, ALU.mult)
            T[tag + "_y"] = S.inc("dve", ins)
        block.vector(f)

        t_r2, _ = emit_ln(s_yy[par][:], [T[tag + "_y"]], tag + "b", par)

        def f(v):
            S.wait(v, "DVE", t_r2)
            ins = v.scalar_tensor_tensor(
                s_cc2[par][:], s_cc[par][:], s_rstd[par][:], gb_ap,
                ALU.mult, ALU.mult)
            t = S.inc("dve", ins)
            S.wait(v, "DVE", t)
            S.wait(v, "DVE", T.get("tapguard"))
            ins = v.tensor_tensor(out_ap, s_cc2[par][:], bb_ap, ALU.add)
            T[tag + "_out"] = S.inc("dve", ins)
            T[f"ln_guard{par}"] = T[tag + "_out"]
        block.vector(f)
        return T[tag + "_out"]

    def emit_bcast(row_ap, dst_ap, p_bc, row_tok, tag):
        for half in range(2):
            def fpe(pe, half=half):
                S.wait(pe, "PE", row_tok)
                S.wait(pe, "PE", T["init_sp"])
                S.wait(pe, "PE", T["init_dve"])
                S.wait(pe, "PE", T.get("bc_ev"))
                ins = pe.matmul(
                    p_bc[:, 0:512], s_one1[:],
                    row_ap[:, half * 512:(half + 1) * 512],
                    start=True, stop=True)
                T["bc_mm"] = S.inc("pe", ins)
            block.tensor(fpe)

            def fv(v, half=half):
                S.wait(v, "DVE", T["bc_mm"])
                ins = v.tensor_copy(
                    dst_ap[:, half * 512:(half + 1) * 512], p_bc[:, 0:512])
                T["bc_ev"] = S.inc("dve", ins)
            block.vector(fv)
        T[tag + "_tok"] = T["bc_ev"]
        T["bcrow_used"] = T["bc_mm"]
        return T["bc_ev"]

    with ExitStack() as _es:
        s_x0 = _es.enter_context(nc.sbuf_tensor("a1_x0", [128, D], F32))
        s_x1 = _es.enter_context(nc.sbuf_tensor("a1_x1", [128, D], F32))
        s_x2 = _es.enter_context(nc.sbuf_tensor("a1_x2", [128, D], F32))
        s_x3 = _es.enter_context(nc.sbuf_tensor("a1_x3", [128, D], F32))
        s_xn1 = [_es.enter_context(
            nc.sbuf_tensor(f"a1_xn1{k}", [128, D], BF16)) for k in range(8)]
        s_g1b = _es.enter_context(nc.sbuf_tensor("a1_g1b", [128, D], F32))
        s_be1b = _es.enter_context(nc.sbuf_tensor("a1_be1b", [128, D], F32))
        s_row = _es.enter_context(nc.sbuf_tensor("a1_row", [1, D], F32))
        s_st0 = _es.enter_context(nc.sbuf_tensor("a1_st0", [128, NDT, BLK], BF16))
        s_st1 = _es.enter_context(nc.sbuf_tensor("a1_st1", [128, NDT, BLK], BF16))
        p_bc = _es.enter_context(nc.psum_tensor("a1_bc", [128, 512], F32))
        p_tr0 = _es.enter_context(nc.psum_tensor("a1_tr0", [128, 128], BF16))
        p_tr1 = _es.enter_context(nc.psum_tensor("a1_tr1", [128, 128], BF16))
        s_x = [s_x0, s_x1, s_x2, s_x3] + [
            _es.enter_context(nc.sbuf_tensor(f"a1_x{k}", [128, D], F32))
            for k in range(4, 8)]
        s_st = [s_st0, s_st1]
        p_tr = [p_tr0, p_tr1]
        for nm in ("c", "c2", "y"):
            pass
        extra = []
        for k in range(6):
            extra.append((
                _es.enter_context(nc.sbuf_tensor(f"a1mm1{k}", [128, 1], F32)),
                _es.enter_context(nc.sbuf_tensor(f"a1mm2{k}", [128, 1], F32)),
                _es.enter_context(nc.sbuf_tensor(f"a1ss{k}", [128, 1], F32)),
                _es.enter_context(nc.sbuf_tensor(f"a1rstd{k}", [128, 1], F32)),
                _es.enter_context(nc.sbuf_tensor(f"a1cc{k}", [128, D], F32)),
                _es.enter_context(nc.sbuf_tensor(f"a1cc2{k}", [128, D], F32)),
                _es.enter_context(nc.sbuf_tensor(f"a1yy{k}", [128, D], F32)),
            ))
        for (m1, m2, ss, rs, cc, cc2, yy) in extra:
            s_mm1.append(m1); s_mm2.append(m2); s_ss.append(ss)
            s_rstd.append(rs); s_cc.append(cc); s_cc2.append(cc2)
            s_yy.append(yy)

        for name, src, dst in [("g1b", g1_i, s_g1b), ("be1b", be1_i, s_be1b)]:
            def f(sync, src=src, name=name):
                S.wait(sync, "SP", T.get("bcrow_used"))
                ins = sync.dma_start(
                    s_row[:], src.rearrange("(one d) -> one d", one=1))
                T[name + "_row"] = S.inc("sprow", ins, 16)
            block.sync(f)
            emit_bcast(s_row, dst, p_bc, T[name + "_row"], name)

        stage_free = [None, None]
        xuse = [None] * 8
        trcp = [None, None]
        G = 8

        tokS = {}
        for g0 in range(0, NT_B, G):
            tiles = list(range(g0, g0 + G))

            for i in tiles:
                par = i % 8

                def f(sync, i=i, par=par):
                    S.wait(sync, "SP", xuse[par])
                    ins = sync.dma_start(
                        s_x[par][:], x_b[i * 128:(i + 1) * 128, :])
                    T[f"a1x{i}"] = S.inc(f"spx{par % 4}", ins, 16)
                block.sync(f)

            def f(v, tiles=tiles):
                S.wait(v, "DVE", tokS.get("mn2"))
                for i in tiles[len(tiles) // 2:]:
                    S.wait(v, "DVE", T[f"a1x{i}"])
                for i in tiles:
                    par = i % 8
                    ins = v.reduce_sum(s_mm1[par][:], s_x[par][:], axis=AX.X)
                tokS["sum"] = S.inc("dve", ins)
            block.vector(f)

            def f(a, tiles=tiles):
                S.wait(a, "ACT", tokS["sum"])
                S.wait(a, "ACT", tokS.get("c2"))
                for i in tiles:
                    par = i % 8
                    ins = a.mul(s_mm2[par][:], s_mm1[par][:], -1.0 / D)
                tokS["mn"] = S.inc("act", ins)
            block.scalar(f)

            def f(v, tiles=tiles):
                S.wait(v, "DVE", tokS["mn"])
                S.wait(v, "DVE", tokS.get("xn1"))
                for i in tiles:
                    par = i % 8
                    ins = v.tensor_scalar(
                        s_cc[par][:], s_x[par][:], s_mm2[par][:], None, ALU.add)
                tokS["c"] = S.inc("dve", ins)
            block.vector(f)

            def f(a, tiles=tiles):
                S.wait(a, "ACT", tokS["c"])
                S.wait(a, "ACT", tokS.get("xn1"))
                for i in tiles:
                    par = i % 8
                    ins = a.activation(s_cc2[par][:], s_cc[par][:], AF.Square,
                                       accum_out=s_ss[par][:])
                tokS["sq"] = S.inc("act", ins)
                S.wait(a, "ACT", tokS["sq"])
                for i in tiles:
                    par = i % 8
                    ins = a.activation(s_ss[par][:], s_ss[par][:], AF.Ln,
                                       bias=s_eps[:], scale=1.0 / D)
                t = S.inc("act", ins)
                S.wait(a, "ACT", t)
                for i in tiles:
                    par = i % 8
                    ins = a.activation(s_rstd[par][:], s_ss[par][:], AF.Exp,
                                       scale=-0.5)
                tokS["rstd"] = S.inc("act", ins)
            block.scalar(f)

            def f(v, tiles=tiles):
                S.wait(v, "DVE", tokS["rstd"])
                S.wait(v, "DVE", tokS.get("c2"))
                for i in tiles:
                    par = i % 8
                    ins = v.tensor_scalar(
                        s_yy[par][:], s_cc[par][:], s_rstd[par][:], None,
                        ALU.mult)
                tokS["y"] = S.inc("dve", ins)
                S.wait(v, "DVE", tokS["y"])
                for i in tiles:
                    par = i % 8
                    ins = v.reduce_sum(s_mm1[par][:], s_yy[par][:], axis=AX.X)
                tokS["sum2"] = S.inc("dve", ins)
            block.vector(f)

            def f(a, tiles=tiles):
                S.wait(a, "ACT", tokS["sum2"])
                for i in tiles:
                    par = i % 8
                    ins = a.mul(s_mm2[par][:], s_mm1[par][:], -1.0 / D)
                tokS["mn2"] = S.inc("act", ins)
            block.scalar(f)

            def f(v, tiles=tiles):
                S.wait(v, "DVE", tokS["mn2"])
                for i in tiles:
                    par = i % 8
                    ins = v.tensor_scalar(
                        s_cc[par][:], s_yy[par][:], s_mm2[par][:], None,
                        ALU.add)
                tokS["c2"] = S.inc("dve", ins)
            block.vector(f)

            def f(a, tiles=tiles):
                S.wait(a, "ACT", tokS["c2"])
                for i in tiles:
                    par = i % 8
                    ins = a.activation(s_cc2[par][:], s_cc[par][:], AF.Square,
                                       accum_out=s_ss[par][:])
                t = S.inc("act", ins)
                S.wait(a, "ACT", t)
                for i in tiles:
                    par = i % 8
                    ins = a.activation(s_ss[par][:], s_ss[par][:], AF.Ln,
                                       bias=s_eps[:], scale=1.0 / D)
                t = S.inc("act", ins)
                S.wait(a, "ACT", t)
                for i in tiles:
                    par = i % 8
                    ins = a.activation(s_rstd[par][:], s_ss[par][:], AF.Exp,
                                       scale=-0.5)
                tokS["rstd2"] = S.inc("act", ins)
            block.scalar(f)

            def f(v, tiles=tiles):
                S.wait(v, "DVE", tokS["rstd2"])
                S.wait(v, "DVE", T["g1b_tok"])
                S.wait(v, "DVE", T["be1b_tok"])
                S.wait(v, "DVE", T.get("a1_xn1_used"))
                for i in tiles:
                    par = i % 8
                    ins = v.scalar_tensor_tensor(
                        s_cc2[par][:], s_cc[par][:], s_rstd[par][:], s_g1b[:],
                        ALU.mult, ALU.mult)
                t = S.inc("dve", ins)
                S.wait(v, "DVE", t)
                S.wait(v, "DVE", T.get("tapguard"))
                for i in tiles:
                    par = i % 8
                    ins = v.tensor_tensor(
                        s_xn1[par][:], s_cc2[par][:], s_be1b[:], ALU.add)
                tokS["xn1"] = S.inc("dve", ins)
            block.vector(f)

            if debug_taps and g0 == 0:
                def f(sync):
                    S.wait(sync, "SP", tokS["xn1"])
                    S.wait(sync, "SP", T.get("lasttap"))
                    ins = sync.dma_start(taps["tap_xn1"], s_xn1[0][:])
                    T["tapguard"] = S.inc("sptap", ins, 16)
                    T["lasttap"] = T["tapguard"]
                block.sync(f)

            for i in tiles:
                par = i % 8
                sp = (i // NBT) % 2
                for j in range(NDT):
                    tp = j % 2

                    def f(pe, i=i, j=j, tp=tp, par=par):
                        S.wait(pe, "PE", tokS["xn1"])
                        S.wait(pe, "PE", T["init_sp"])
                        S.wait(pe, "PE", trcp[tp])
                        ins = pe.transpose(
                            p_tr[tp][:], s_xn1[par][:, j * 128:(j + 1) * 128],
                            s_idn[:])
                        T["a1t"] = S.inc("pe", ins)
                    block.tensor(f)

                    def f(a, i=i, j=j, tp=tp, sp=sp):
                        S.wait(a, "ACT", T["a1t"])
                        S.wait(a, "ACT", stage_free[sp])
                        c0 = (i % NBT) * 128
                        ins = a.copy(s_st[sp][:, j, c0:c0 + 128], p_tr[tp][:])
                        trcp[tp] = S.inc("act", ins)
                    block.scalar(f)
                xuse[par] = tokS["c"]
                T["a1_xn1_used"] = T["a1t"]

                if (i + 1) % NBT == 0:
                    blk = i // NBT

                    def f(sync, blk=blk, sp=sp):
                        S.wait(sync, "SP", trcp[0])
                        S.wait(sync, "SP", trcp[1])
                        ins = sync.dma_start(
                            spill[:, :, blk * BLK:(blk + 1) * BLK].rearrange(
                                "dt p c -> p dt c"),
                            s_st[sp][:])
                        T[f"a1sp{blk}"] = S.inc(f"spst{sp}", ins, 16)
                        stage_free[sp] = T[f"a1sp{blk}"]
                    block.sync(f)

        spill_done = T[f"a1sp{NBLK - 1}"]
        spill_done2 = T[f"a1sp{NBLK - 2}"] if NBLK >= 2 else None
        for lst in (s_mm1, s_mm2, s_ss, s_rstd, s_cc, s_cc2, s_yy):
            lst.pop(); lst.pop()

    attnT_cm = nc.sbuf_tensor("attnT", [128, NDT, T_C], BF16)
    attnT = attnT_cm.__enter__()
    with ExitStack() as _es:
        qhat = _es.enter_context(nc.sbuf_tensor("qhat", [65, 2, T_C], BF16))
        khat = _es.enter_context(nc.sbuf_tensor("khat", [65, 2, T_B], BF16))
        vbuf = _es.enter_context(nc.sbuf_tensor("vbuf", [128, NT_B, 2, 65], BF16))
        s_wq = _es.enter_context(nc.sbuf_tensor("s_wq", [128, NDT, 128], BF16))
        s_wk = _es.enter_context(nc.sbuf_tensor("s_wk", [128, NDT, 128], BF16))
        s_wvp = _es.enter_context(nc.sbuf_tensor("s_wvp", [128, NDT, 256], BF16))
        s_vst = _es.enter_context(nc.sbuf_tensor("s_vst", [128, NBT, 128], BF16))
        s_vrd0 = _es.enter_context(nc.sbuf_tensor("s_vrd0", [128, NBT, 128], BF16))
        s_vrd1 = _es.enter_context(nc.sbuf_tensor("s_vrd1", [128, NBT, 128], BF16))
        s_vrd = [s_vrd0, s_vrd1]
        s_bqb = _es.enter_context(nc.sbuf_tensor("s_bqb", [128, 1], F32))
        s_bkb = _es.enter_context(nc.sbuf_tensor("s_bkb", [128, 1], F32))
        s_xb0 = _es.enter_context(nc.sbuf_tensor("s_xb0", [128, NDT, BLK], BF16))
        s_xb1 = _es.enter_context(nc.sbuf_tensor("s_xb1", [128, NDT, BLK], BF16))
        s_ck0 = _es.enter_context(nc.sbuf_tensor("s_ck0", [128, BLK], F32))
        s_ck1 = _es.enter_context(nc.sbuf_tensor("s_ck1", [128, BLK], F32))
        s_sk0 = _es.enter_context(nc.sbuf_tensor("s_sk0", [128, BLK], F32))
        s_sk1 = _es.enter_context(nc.sbuf_tensor("s_sk1", [128, BLK], F32))
        s_bvrow = _es.enter_context(nc.sbuf_tensor("s_bvrow", [1, 256], F32))
        s_bvbb = _es.enter_context(nc.sbuf_tensor("s_bvbb", [128, 256], F32))
        s_rmaxv = _es.enter_context(nc.sbuf_tensor("s_rmaxv", [128, 1], F32))
        s_t1v = _es.enter_context(nc.sbuf_tensor("s_t1v", [128, 1], F32))
        s_t1b = _es.enter_context(nc.sbuf_tensor("s_t1b", [128, 1], BF16))
        s_ktmp = s_yy
        s_u = s_cc
        s_w = s_cc2
        s_sq2 = _es.enter_context(nc.sbuf_tensor("s_sq2", [64, 512], BF16))
        s_rmax = _es.enter_context(nc.sbuf_tensor("s_rmax", [1, 2], F32))
        s_t11 = _es.enter_context(nc.sbuf_tensor("s_t11", [1, 2], F32))
        s_bias2 = _es.enter_context(nc.sbuf_tensor("s_bias2", [1, 2], F32))
        s_lq = _es.enter_context(nc.sbuf_tensor("s_lq", [1, 512], F32))
        s_brow = _es.enter_context(nc.sbuf_tensor("s_brow", [1, 512], F32))
        s_p0 = _es.enter_context(nc.sbuf_tensor("s_p0", [128, QC], BF16))
        s_p1 = _es.enter_context(nc.sbuf_tensor("s_p1", [128, QC], BF16))
        s_tacc = s_cc2[0][0:65, 0:QC]
        s_rl = s_cc[0][0:1, 0:QC]
        s_xb = [s_xb0, s_xb1]
        s_ck = [s_ck0, s_ck1]
        s_sk = [s_sk0, s_sk1]
        s_p = [s_p0, s_p1]
        xb_use = [None, None]
        ck_use = [None, None]

        def f(v):
            S.wait(v, "DVE", spill_done)
            S.wait(v, "DVE", spill_done2)
            ins = v.memset(vbuf[:, :, :, 64:65], 1.0)
            T["init_vbuf"] = S.inc("dve", ins)
        block.vector(f)

        for hg in range(NHG):
            def f(gp, hg=hg):
                S.wait(gp, "GP", spill_done)
                S.wait(gp, "GP", T.get("tapguard"))
                S.wait(gp, "GP", T.get("attn_done"))
                srcs = [
                    (s_wq[:], wq_r[hg].rearrange("(dt p) c -> p dt c", p=128)),
                    (s_wk[:], wk_r[hg].rearrange("(dt p) c -> p dt c", p=128)),
                    (s_bqb[:], bq_i[hg * 128:(hg + 1) * 128].rearrange(
                        "(p one) -> p one", one=1)),
                    (s_bkb[:], bk_i[hg * 128:(hg + 1) * 128].rearrange(
                        "(p one) -> p one", one=1)),
                ]
                if hg % 2 == 0:
                    srcs.append(
                        (s_wvp[:],
                         wv_r[hg // 2].rearrange("(dt p) c -> p dt c", p=128)))
                for dst, src in srcs:
                    ins = gp.dma_start(dst, src)
                    T["wslices"] = S.inc("gpw", ins, 16)
            block.gpsimd(f)

            with ExitStack() as _es:
                p_k0 = _es.enter_context(nc.psum_tensor(f"a2_k0{hg}", [128, BLK], F32))
                p_k1 = _es.enter_context(nc.psum_tensor(f"a2_k1{hg}", [128, BLK], F32))
                p_rot = _es.enter_context(nc.psum_tensor(f"a2_rot{hg}", [128, 512], F32))
                p_vn0 = _es.enter_context(nc.psum_tensor(f"a2_vn0{hg}", [128, 256], F32))
                p_vn1 = _es.enter_context(nc.psum_tensor(f"a2_vn1{hg}", [128, 256], F32))
                p_nrm = _es.enter_context(nc.psum_tensor(f"a2_nrm{hg}", [1, 512], F32))
                p_kp = [p_k0, p_k1]
                p_vn = [p_vn0, p_vn1]
                vnT = [None, None]
                pkT = {}
                ktmpT = {}
                rotT = {}
                wopT = {}
                tuT = {}
                hatT = {}

                def f(v):
                    S.wait(v, "DVE", T.get("bias2"))
                    ins = v.memset(s_rmaxv[:], 0.0)
                    T["rmax0"] = S.inc("dve", ins)
                block.vector(f)

                if hg % 2 == 0:
                    def f(sync, hg=hg):
                        S.wait(sync, "SP", T.get("bvrow_used"))
                        ins = sync.dma_start(
                            s_bvrow[:],
                            bv_i[hg * 128:hg * 128 + 256].rearrange(
                                "(one c) -> one c", one=1))
                        T["bvrow"] = S.inc("sprow", ins, 16)
                    block.sync(f)

                    def f(pe):
                        S.wait(pe, "PE", T["bvrow"])
                        S.wait(pe, "PE", T["init_sp"])
                        S.wait(pe, "PE", T.get("wop"))
                        S.wait(pe, "PE", T.get("bvbb"))
                        for h0, hw in chunks(256, 512):
                            ins = pe.matmul(p_rot[:, 0:256], s_one1[:],
                                            s_bvrow[:], start=True, stop=True)
                        t = S.inc("pe", ins)
                        T["bvrow_used"] = t
                        T["bvmm"] = t
                    block.tensor(f)

                    def f(v):
                        S.wait(v, "DVE", T["bvmm"])
                        ins = v.tensor_copy(s_bvbb[:], p_rot[:, 0:256])
                        T["bvbb"] = S.inc("dve", ins)
                    block.vector(f)

                blocks = [("kv", b) for b in range(NBLK)] + \
                         [("q", b) for b in range(T_C // BLK if T_C >= BLK else 1)]
                for bi, (kind, b) in enumerate(blocks):
                    par = bi % 2
                    w = BLK if kind == "kv" or T_C >= BLK else T_C
                    c0 = b * BLK if kind == "kv" else b * w

                    def f(sync, par=par, c0=c0, w=w):
                        S.wait(sync, "SP", spill_done)
                        S.wait(sync, "SP", spill_done2)
                        S.wait(sync, "SP", xb_use[par])
                        S.wait(sync, "SP", ck_use[par])
                        ins = sync.dma_start(
                            s_xb[par][:, :, 0:w],
                            spill[:, :, c0:c0 + w].rearrange("dt p c -> p dt c"))
                        S.inc(f"spxb{par}", ins, 16)
                        ins = sync.dma_start(s_ck[par][:, 0:w], cosk[:, c0:c0 + w])
                        S.inc(f"spxb{par}", ins, 16)
                        ins = sync.dma_start(s_sk[par][:, 0:w], s1k[:, c0:c0 + w])
                        T["ckt"] = S.inc(f"spxb{par}", ins, 16)
                    block.sync(f)
                    xt = ct = T["ckt"]

                    wsel = s_wk if kind == "kv" else s_wq
                    bsel = s_bkb if kind == "kv" else s_bqb
                    dhat = khat if kind == "kv" else qhat

                    def f(pe, xt=xt, wsel=wsel, par=par, w=w):
                        S.wait(pe, "PE", T.get("attn_done"))
                        S.wait(pe, "PE", xt)
                        S.wait(pe, "PE", T["wslices"])
                        S.wait(pe, "PE", ktmpT.get(par))
                        for dt in range(NDT):
                            for h0, hw in chunks(w, 512):
                                ins = pe.matmul(
                                    p_kp[par][:, h0:h0 + hw],
                                    wsel[:, dt, :],
                                    s_xb[par][:, dt, h0:h0 + hw],
                                    start=(dt == 0), stop=(dt == NDT - 1))
                        pkT[par] = S.inc("pe", ins)
                    block.tensor(f)

                    def f(a, bsel=bsel, par=par, w=w):
                        S.wait(a, "ACT", pkT[par])
                        S.wait(a, "ACT", rotT.get(par))
                        S.wait(a, "ACT", T.get("nrm_red"))
                        S.wait(a, "ACT", tuT.get(par))
                        ins = a.activation(
                            s_ktmp[par][:, 0:w], p_kp[par][:, 0:w], AF.Identity,
                            bias=bsel[:])
                        ktmpT[par] = S.inc("act", ins)
                    block.scalar(f)

                    if kind == "kv" and hg % 2 == 0:
                        for j in range(NBT):
                            vp = j % 2

                            def f(pe, j=j, vp=vp, xt=xt, par=par):
                                S.wait(pe, "PE", xt)
                                S.wait(pe, "PE", T["wslices"])
                                S.wait(pe, "PE", vnT[vp])
                                for dt in range(NDT):
                                    ins = pe.matmul(
                                        p_vn[vp][:],
                                        s_xb[par][:, dt,
                                                  j * 128:(j + 1) * 128],
                                        s_wvp[:, dt, :],
                                        start=(dt == 0), stop=(dt == NDT - 1))
                                T["pv"] = S.inc("pe", ins)
                            block.tensor(f)

                            def f(v, j=j, vp=vp, b=b):
                                S.wait(v, "DVE", T["pv"])
                                S.wait(v, "DVE", T["bvbb"])
                                S.wait(v, "DVE", T.get("tapguard2"))
                                S.wait(v, "DVE", T["init_vbuf"])
                                S.wait(v, "DVE", T.get(f"vspw{b}"))
                                tt = b * NBT + j
                                ins = v.tensor_tensor(
                                    vbuf[:, tt, :, 0:64],
                                    p_vn[vp][:, 0:128].rearrange(
                                        "p (h c) -> p h c", h=2),
                                    s_bvbb[:, 0:128].rearrange(
                                        "p (h c) -> p h c", h=2),
                                    ALU.add)
                                t = S.inc("dve", ins)
                                ins = v.tensor_tensor(
                                    s_vst[:, j, :],
                                    p_vn[vp][:, 128:256],
                                    s_bvbb[:, 128:256], ALU.add)
                                vnT[vp] = S.inc("dve", ins)
                                T["vbuf_wr"] = vnT[vp]
                                T["vst_wr"] = vnT[vp]
                            block.vector(f)

                        def f(sync, b=b):
                            S.wait(sync, "SP", T["vst_wr"])
                            ins = sync.dma_start(
                                vsp[:, b * NBT:(b + 1) * NBT, :], s_vst[:])
                            T[f"vspw{b}"] = S.inc("spvs", ins, 16)
                        block.sync(f)

                    if kind == "kv" and hg % 2 == 1:
                        vrp = b % 2

                        def f(sync, b=b, vrp=vrp):
                            S.wait(sync, "SP", T[f"vspw{b}"])
                            S.wait(sync, "SP", T.get(f"vrd_used{vrp}"))
                            ins = sync.dma_start(
                                s_vrd[vrp][:], vsp[:, b * NBT:(b + 1) * NBT, :])
                            T["vrd"] = S.inc("spvr", ins, 16)
                        block.sync(f)

                        def f(v, b=b, vrp=vrp):
                            S.wait(v, "DVE", T["vrd"])
                            S.wait(v, "DVE", T.get("tapguard2"))
                            S.wait(v, "DVE", T["init_vbuf"])
                            for j in range(NBT):
                                tt = b * NBT + j
                                ins = v.tensor_copy(
                                    vbuf[:, tt, :, 0:64],
                                    s_vrd[vrp][:, j, :].rearrange(
                                        "p (h c) -> p h c", h=2))
                            T[f"vrd_used{vrp}"] = S.inc("dve", ins)
                            T["vbuf_wr"] = T[f"vrd_used{vrp}"]
                        block.vector(f)

                    for h0, hw in chunks(w, 512):
                        def f(pe, h0=h0, hw=hw, par=par):
                            S.wait(pe, "PE", ktmpT[par])
                            S.wait(pe, "PE", T.get("wop"))
                            ins = pe.matmul(
                                p_rot[:, 0:hw], s_perm[:],
                                s_ktmp[par][:, h0:h0 + hw],
                                start=True, stop=True)
                            rotT[par] = S.inc("pe", ins)
                            T["rot_roll"] = rotT[par]
                        block.tensor(f)

                        def f(v, h0=h0, hw=hw, par=par, ct=ct):
                            S.wait(v, "DVE", rotT[par])
                            S.wait(v, "DVE", ct)
                            ins = v.tensor_tensor(
                                s_w[par][:, h0:h0 + hw], p_rot[:, 0:hw],
                                s_sk[par][:, h0:h0 + hw], ALU.mult)
                            T["wop"] = S.inc("dve", ins)
                            wopT[par] = T["wop"]
                        block.vector(f)

                    def f(v, par=par, ct=ct, dhat=dhat, c0=c0, w=w, kind=kind):
                        S.wait(v, "DVE", ktmpT[par])
                        S.wait(v, "DVE", T.get("tapguard2"))
                        S.wait(v, "DVE", ct)
                        ins = v.tensor_tensor(
                            s_u[par][:, 0:w], s_ktmp[par][:, 0:w],
                            s_ck[par][:, 0:w], ALU.mult)
                        tuT[par] = S.inc("dve", ins)
                        S.wait(v, "DVE", tuT[par])
                        S.wait(v, "DVE", wopT[par])
                        ins = v.tensor_tensor(
                            dhat[0:64, 0, c0:c0 + w],
                            s_u[par][0:64, 0:w], s_w[par][0:64, 0:w], ALU.add)
                        S.inc("dve", ins)
                        ins = v.tensor_tensor(
                            dhat[0:64, 1, c0:c0 + w],
                            s_u[par][64:128, 0:w], s_w[par][64:128, 0:w],
                            ALU.add)
                        hatT[par] = S.inc("dve", ins)
                        T["hat"] = hatT[par]
                        if kind == "kv":
                            S.wait(v, "DVE", T.get("nrm_red"))
                            ins = v.reduce_max(
                                s_t1v[:], s_ktmp[par][:, 0:w], axis=AX.X,
                                apply_absolute_value=True)
                            t = S.inc("dve", ins)
                            S.wait(v, "DVE", t)
                            ins = v.tensor_tensor(
                                s_rmaxv[:], s_rmaxv[:], s_t1v[:], ALU.max)
                            T["nrm_red"] = S.inc("dve", ins)
                    block.vector(f)
                    if kind == "q" or hg % 2 == 1:
                        xb_use[par] = pkT[par]
                    else:
                        xb_use[par] = T["pv"]
                    ck_use[par] = hatT[par]

                def f(v):
                    S.wait(v, "DVE", T["nrm_red"])
                    ins = v.tensor_tensor(
                        s_t1b[:], s_rmaxv[:], s_rmaxv[:], ALU.mult)
                    T["rmx_bf"] = S.inc("dve", ins)
                block.vector(f)

                def f(pe):
                    S.wait(pe, "PE", T["rmx_bf"])
                    S.wait(pe, "PE", T.get("q2mm"))
                    for h01 in range(2):
                        ins = pe.matmul(
                            p_nrm[:, h01:h01 + 1],
                            s_bones[:, h01:h01 + 1], s_t1b[:],
                            start=True, stop=True)
                    T["mk2"] = S.inc("pe", ins)
                block.tensor(f)

                def f(a):
                    S.wait(a, "ACT", T["mk2"])
                    ins = a.activation(s_t11[:], p_nrm[:, 0:2], AF.Ln)
                    t = S.inc("act", ins)
                    S.wait(a, "ACT", t)
                    ins = a.activation(
                        s_bias2[:], s_t11[:], AF.Identity, scale=0.5,
                        bias=s_ln102[:])
                    T["bias2"] = S.inc("act", ins)
                block.scalar(f)

                for h01 in range(2):
                    for c0, cw in chunks(T_C, 512):
                        def f(v, h01=h01, c0=c0, cw=cw):
                            S.wait(v, "DVE", T["hat"])
                            S.wait(v, "DVE", T.get("q2mm"))
                            ins = v.tensor_tensor(
                                s_sq2[:, 0:cw],
                                qhat[0:64, h01, c0:c0 + cw],
                                qhat[0:64, h01, c0:c0 + cw], ALU.mult)
                            T["sq2"] = S.inc("dve", ins)
                        block.vector(f)

                        def f(pe, c0=c0, cw=cw):
                            S.wait(pe, "PE", T["sq2"])
                            S.wait(pe, "PE", T["bias2"])
                            S.wait(pe, "PE", T.get("lnq"))
                            for h0, hw in chunks(cw, 512):
                                ins = pe.matmul(
                                    p_nrm[:, h0:h0 + hw], s_ones64[:],
                                    s_sq2[:, h0:h0 + hw],
                                    start=True, stop=True)
                            T["q2mm"] = S.inc("pe", ins)
                        block.tensor(f)

                        def f(a, h01=h01, c0=c0, cw=cw):
                            S.wait(a, "ACT", T["q2mm"])
                            S.wait(a, "ACT", T["bias2"])
                            ins = a.activation(s_lq[:, 0:cw], p_nrm[:, 0:cw], AF.Ln)
                            T["lnq"] = S.inc("act", ins)
                            S.wait(a, "ACT", T["lnq"])
                            ins = a.activation(
                                s_brow[:, 0:cw], s_lq[:, 0:cw], AF.Exp,
                                scale=0.5, bias=s_bias2[:, h01:h01 + 1])
                            t = S.inc("act", ins)
                            S.wait(a, "ACT", t)
                            ins = a.mul(
                                qhat[64:65, h01, c0:c0 + cw], s_brow[:, 0:cw],
                                -1.0)
                            T["brow"] = S.inc("act", ins)
                        block.scalar(f)

                def f(v):
                    S.wait(v, "DVE", T["hat"])
                    S.wait(v, "DVE", T["brow"])
                    ins = v.memset(khat[64:65, :, :], 1.0)
                    T["khat_ones"] = S.inc("dve", ins)
                block.vector(f)

                if debug_taps and hg == 0:
                    def f(sync):
                        S.wait(sync, "SP", T["khat_ones"])
                        S.wait(sync, "SP", T["vbuf_wr"])
                        for name, srcap in [
                            ("tap_khat", khat[:, 0, :]),
                            ("tap_qhat", qhat[:, 0, :]),
                            ("tap_vbuf", vbuf[:, 0, 0, :]),
                        ]:
                            S.wait(sync, "SP", T.get("lasttap"))
                            ins = sync.dma_start(taps[name], srcap)
                            T["tapguard2"] = S.inc("sptap", ins, 16)
                            T["lasttap"] = T["tapguard2"]
                    block.sync(f)

            with ExitStack() as _es:
                p_s0 = _es.enter_context(nc.psum_tensor(f"b_s0{hg}", [128, QC], F32))
                p_s1 = _es.enter_context(nc.psum_tensor(f"b_s1{hg}", [128, QC], F32))
                p_acc = _es.enter_context(nc.psum_tensor(f"b_acc{hg}", [65, QC], F32))
                p_bl = _es.enter_context(nc.psum_tensor(f"b_bl{hg}", [64, QC], F32))
                p_s = [p_s0, p_s1]
                smm_par = {}
                exp_par = {}
                pv_par = {}
                pending = [None]

                def emit_s(h01, qc, kt):
                    par = kt % 2

                    def f(pe, h01=h01, kt=kt, par=par, qc=qc):
                        S.wait(pe, "PE", T["khat_ones"])
                        S.wait(pe, "PE", exp_par.get(par))
                        for h0, hw in chunks(QC, 512):
                            ins = pe.matmul(
                                p_s[par][:, h0:h0 + hw],
                                khat[0:65, h01, kt * 128:(kt + 1) * 128],
                                qhat[0:65, h01,
                                     qc * QC + h0:qc * QC + h0 + hw],
                                start=True, stop=True)
                        smm_par[par] = S.inc("pe", ins)
                    block.tensor(f)

                for h01 in range(2):
                    for qc in range(NQC):
                        emit_s(h01, qc, 0)
                        for kt in range(NT_B):
                            par = kt % 2
                            if kt + 1 < NT_B:
                                emit_s(h01, qc, kt + 1)
                            if kt == 4 and pending[0] is not None:
                                pending[0]()
                                pending[0] = None

                            def f(a, kt=kt, par=par):
                                S.wait(a, "ACT", smm_par[par])
                                S.wait(a, "ACT", pv_par.get(par))
                                ins = a.activation(
                                    s_p[par][:], p_s[par][:], AF.Exp,
                                    scale=0.125)
                                exp_par[par] = S.inc("act", ins)
                            block.scalar(f)

                            def f(pe, h01=h01, kt=kt, par=par):
                                S.wait(pe, "PE", exp_par[par])
                                if kt == 0:
                                    S.wait(pe, "PE", T.get("vbuf_wr"))
                                    S.wait(pe, "PE", T.get("tacc"))
                                for h0, hw in chunks(QC, 512):
                                    ins = pe.matmul(
                                        p_acc[:, h0:h0 + hw],
                                        vbuf[:, kt, h01, :],
                                        s_p[par][:, h0:h0 + hw],
                                        start=(kt == 0), stop=(kt == NT_B - 1))
                                pv_par[par] = S.inc("pe", ins)
                                T["pvlast"] = pv_par[par]
                            block.tensor(f)

                        def f(a):
                            S.wait(a, "ACT", T["pvlast"])
                            S.wait(a, "ACT", T["hat"])
                            S.wait(a, "ACT", T.get("attnT_wr"))
                            ins = a.copy(s_tacc, p_acc[:])
                            T["tacc"] = S.inc("act", ins)
                        block.scalar(f)

                        def evac(h01=h01, hg=hg, qc=qc, tacc_tok=None):
                            def f(v):
                                S.wait(v, "DVE", T["tacc"])
                                S.wait(v, "DVE", T["hat"])
                                S.wait(v, "DVE", T.get("bl"))
                                ins = v.reciprocal(s_rl, s_cc2[0][64:65, 0:QC])
                                T["rl"] = S.inc("dve", ins)
                            block.vector(f)

                            def f(pe):
                                S.wait(pe, "PE", T["rl"])
                                S.wait(pe, "PE", T.get("attnT_wr"))
                                for h0, hw in chunks(QC, 512):
                                    ins = pe.matmul(
                                        p_bl[:, h0:h0 + hw], s_one1[:, 0:64],
                                        s_cc[0][0:1, h0:h0 + hw],
                                        start=True, stop=True)
                                T["bl"] = S.inc("pe", ins)
                            block.tensor(f)

                            def f(v):
                                S.wait(v, "DVE", T["bl"])
                                ins = v.tensor_tensor(
                                    attnT[h01 * 64:(h01 + 1) * 64, hg,
                                          qc * QC:(qc + 1) * QC],
                                    s_cc2[0][0:64, 0:QC], p_bl[:], ALU.mult)
                                T["attnT_wr"] = S.inc("dve", ins)
                            block.vector(f)
                        pending[0] = evac

                if pending[0] is not None:
                    pending[0]()
                    pending[0] = None

            T["attn_done"] = T["attnT_wr"]

        attnT_done = T["attn_done"]

    if debug_taps:
        def f(sync):
            S.wait(sync, "SP", attnT_done)
            S.wait(sync, "SP", T.get("lasttap"))
            ins = sync.dma_start(taps["tap_attnT"], attnT[:, 0, :])
            T["tapguard5"] = S.inc("sptap", ins, 16)
            T["lasttap"] = T["tapguard5"]
        block.sync(f)

    xz_cm = nc.sbuf_tensor("xz", [128, NT_C, D], F32, side="right")
    xz = xz_cm.__enter__()
    xn2T_cm = nc.sbuf_tensor("xn2T", [128, NDT, T_C], BF16, side="right")
    xn2T = xn2T_cm.__enter__()
    if True:
        with ExitStack() as _es:
            s_g2b = _es.enter_context(nc.sbuf_tensor("c_g2b", [128, D], F32))
            s_be2b = _es.enter_context(nc.sbuf_tensor("c_be2b", [128, D], F32))
            s_bob = _es.enter_context(nc.sbuf_tensor("c_bob", [128, D], F32))
            s_crow = _es.enter_context(nc.sbuf_tensor("c_row", [1, D], F32))
            c_x0 = _es.enter_context(nc.sbuf_tensor("c_x0", [128, D], F32))
            c_x1 = _es.enter_context(nc.sbuf_tensor("c_x1", [128, D], F32))
            s_z = _es.enter_context(nc.sbuf_tensor("c_z", [128, D], F32))
            s_xn2 = _es.enter_context(nc.sbuf_tensor("c_xn2", [128, D], BF16))
            s_x = [c_x0, c_x1]
            s_wo = _es.enter_context(nc.sbuf_tensor("c_wo", [128, NDT, D], BF16))
            p_o = _es.enter_context(nc.psum_tensor("c_o", [128, D], F32))
            p_bc2 = _es.enter_context(nc.psum_tensor("c_bc", [128, 512], F32))
            p_tr0 = _es.enter_context(nc.psum_tensor("c_tr0", [128, 128], BF16))
            p_tr1 = _es.enter_context(nc.psum_tensor("c_tr1", [128, 128], BF16))
            p_tr = [p_tr0, p_tr1]

            def f(gp):
                S.wait(gp, "GP", attnT_done)
                ins = gp.dma_start(
                    s_wo[:], wo_i.rearrange("(dt p) c -> p dt c", p=128))
                T["wo"] = S.inc("gpw", ins, 16)
            block.gpsimd(f)

            for name, src, dst in [
                ("g2b", g2_i, s_g2b), ("be2b", be2_i, s_be2b),
                ("bob", bo_i, s_bob),
            ]:
                def f(sync, src=src, name=name):
                    S.wait(sync, "SP", attnT_done)
                    S.wait(sync, "SP", T.get("bcrow_used"))
                    ins = sync.dma_start(
                        s_crow[:], src.rearrange("(one d) -> one d", one=1))
                    T[name + "_row"] = S.inc("sprow", ins, 16)
                block.sync(f)
                emit_bcast(s_crow, dst, p_bc2, T[name + "_row"], name)

            xuse = [None, None]
            trcp = [None, None]
            for tt in range(NT_C):
                par = tt % 2

                def f(sync, tt=tt, par=par):
                    S.wait(sync, "SP", attnT_done)
                    S.wait(sync, "SP", xuse[par])
                    ins = sync.dma_start(
                        s_x[par][:], x_b[tt * 128:(tt + 1) * 128, :])
                    T["cx"] = S.inc(f"spx{par}", ins, 16)
                block.sync(f)
                cx = T["cx"]

                def f(pe, tt=tt):
                    S.wait(pe, "PE", attnT_done)
                    S.wait(pe, "PE", T["wo"])
                    S.wait(pe, "PE", T.get("z1"))
                    for dt in range(NDT):
                        for h0, hw in chunks(D, 512):
                            ins = pe.matmul(
                                p_o[:, h0:h0 + hw],
                                attnT[:, dt, tt * 128:(tt + 1) * 128],
                                s_wo[:, dt, h0:h0 + hw],
                                start=(dt == 0), stop=(dt == NDT - 1))
                    T["po"] = S.inc("pe", ins)
                block.tensor(f)

                t_r, _ = emit_ln(s_x[par][:], [cx], f"cy{tt}", par)

                def f(v, tt=tt, par=par):
                    S.wait(v, "DVE", t_r)
                    ins = v.tensor_scalar(
                        s_yy[par][:], s_cc[par][:], s_rstd[par][:], None,
                        ALU.mult)
                    t = S.inc("dve", ins)
                    S.wait(v, "DVE", t)
                    S.wait(v, "DVE", T["po"])
                    S.wait(v, "DVE", T["bob_tok"])
                    ins = v.tensor_tensor(s_z[:], p_o[:], s_yy[par][:], ALU.add)
                    T["z1"] = S.inc("dve", ins)
                    S.wait(v, "DVE", T["z1"])
                    ins = v.tensor_tensor(s_z[:], s_z[:], s_bob[:], ALU.add)
                    T["z2"] = S.inc("dve", ins)
                    S.wait(v, "DVE", T["z2"])
                    ins = v.tensor_tensor(
                        xz[:, tt, :], s_x[par][:], s_z[:], ALU.add)
                    T[f"xz{tt}"] = S.inc("dve", ins)
                    xuse[par] = T[f"xz{tt}"]
                    T[f"ln_guard{par}"] = T[f"xz{tt}"]
                block.vector(f)

                t_r2, _ = emit_ln(s_z[:], [T["z2"]], f"cn{tt}", par)

                def f(v, tt=tt, par=par):
                    S.wait(v, "DVE", t_r2)
                    S.wait(v, "DVE", T["g2b_tok"])
                    S.wait(v, "DVE", T["be2b_tok"])
                    S.wait(v, "DVE", T.get("xn2T_cp"))
                    ins = v.scalar_tensor_tensor(
                        s_cc2[par][:], s_cc[par][:], s_rstd[par][:], s_g2b[:],
                        ALU.mult, ALU.mult)
                    t = S.inc("dve", ins)
                    S.wait(v, "DVE", t)
                    ins = v.tensor_tensor(s_xn2[:], s_cc2[par][:], s_be2b[:],
                                          ALU.add)
                    T["xn2"] = S.inc("dve", ins)
                    T[f"ln_guard{par}"] = T["xn2"]
                block.vector(f)

                for j in range(NDT):
                    tp = j % 2

                    def f(pe, tt=tt, j=j, tp=tp):
                        S.wait(pe, "PE", T["xn2"])
                        S.wait(pe, "PE", trcp[tp])
                        ins = pe.transpose(
                            p_tr[tp][:], s_xn2[:, j * 128:(j + 1) * 128],
                            s_idn[:])
                        T["ctr"] = S.inc("pe", ins)
                    block.tensor(f)

                    def f(a, tt=tt, j=j, tp=tp):
                        S.wait(a, "ACT", T["ctr"])
                        ins = a.copy(
                            xn2T[:, j, tt * 128:(tt + 1) * 128], p_tr[tp][:])
                        trcp[tp] = S.inc("act", ins)
                        T["xn2T_cp"] = trcp[tp]
                    block.scalar(f)
            T["xn2T_done"] = T["xn2T_cp"]

            if debug_taps:
                def f(sync):
                    S.wait(sync, "SP", T["xn2T_done"])
                    S.wait(sync, "SP", T["xz0"])
                    S.wait(sync, "SP", T.get("lasttap"))
                    ins = sync.dma_start(taps["tap_xz"], xz[:, 0, :])
                    T["lasttap"] = S.inc("sptap", ins, 16)
                    S.wait(sync, "SP", T.get("lasttap"))
                    ins = sync.dma_start(taps["tap_xn2T"], xn2T[:, 0, :])
                    T["tapguard3"] = S.inc("sptap", ins, 16)
                    T["lasttap"] = T["tapguard3"]
                block.sync(f)

        attnT_cm.__exit__(None, None, None)

        with ExitStack() as _es:
            s_b2b = _es.enter_context(nc.sbuf_tensor("c_b2b", [128, D], F32))
            s_crow2 = _es.enter_context(nc.sbuf_tensor("c_row2", [1, D], F32))

            s_w1a = _es.enter_context(nc.sbuf_tensor("c_w1a", [128, NDT, FC], BF16))
            s_w1b = _es.enter_context(nc.sbuf_tensor("c_w1b", [128, NDT, FC], BF16))
            s_w2a = _es.enter_context(nc.sbuf_tensor("c_w2a", [128, NFT, D], BF16))
            s_w2b = _es.enter_context(nc.sbuf_tensor("c_w2b", [128, NFT, D], BF16))
            s_b1 = _es.enter_context(nc.sbuf_tensor("c_b1", [128, NFT], F32))
            s_b1x = _es.enter_context(nc.sbuf_tensor("c_b1x", [128, NFT], F32))
            s_gel = _es.enter_context(nc.sbuf_tensor("c_gel", [128, NFT, T_C], BF16))
            p_a0 = _es.enter_context(nc.psum_tensor("c_a0", [128, 1024], F32))
            p_a1 = _es.enter_context(nc.psum_tensor("c_a1", [128, 1024], F32))
            p_b0 = _es.enter_context(nc.psum_tensor("c_pb0", [128, D], F32))
            p_b1 = _es.enter_context(nc.psum_tensor("c_pb1", [128, D], F32))
            p_ap = [p_a0, p_a1]
            p_bp = [p_b0, p_b1]

            def f(sync):
                S.wait(sync, "SP", T["xn2T_done"])
                S.wait(sync, "SP", T.get("tapguard5"))
                S.wait(sync, "SP", T.get("bcrow_used"))
                ins = sync.dma_start(
                    s_crow2[:], b2_i.rearrange("(one d) -> one d", one=1))
                T["b2b_row"] = S.inc("sprow", ins, 16)
            block.sync(f)
            emit_bcast(s_crow2, s_b2b, p_a0, T["b2b_row"], "b2b")
            s_w1 = [s_w1a, s_w1b]
            s_w2 = [s_w2a, s_w2b]
            s_b1p = [s_b1, s_b1x]
            NFC_MLP = DFF // FC
            w_toks = {}
            w_used = {}

            def mlp_prefetch(fc):
                par = fc % 2

                def f(gp, fc=fc, par=par):
                    S.wait(gp, "GP", w_used.get(fc - 2))
                    if fc < 2:
                        S.wait(gp, "GP", T["xn2T_done"])
                    ins = gp.dma_start(
                        s_w1[par][:],
                        w1_r[fc].rearrange("(dt p) c -> p dt c", p=128))
                    S.inc(f"gpw1{par}", ins, 16)
                    ins = gp.dma_start(
                        s_w2[par][:],
                        w2_i[fc * FC:(fc + 1) * FC, :].rearrange(
                            "(ft p) c -> p ft c", p=128))
                    S.inc(f"gpw1{par}", ins, 16)
                    ins = gp.dma_start(
                        s_b1p[par][:, 0:NFT], b1_i[fc])
                    w_toks[fc] = S.inc(f"gpw1{par}", ins, 16)
                block.gpsimd(f)

            mlp_prefetch(0)
            if NFC_MLP > 1:
                mlp_prefetch(1)

            def f(a):
                load_act_set(a, 10)
            block.scalar(f)

            for fc in range(NFC_MLP):
                par = fc % 2
                for ft in range(NFT):
                    for h0, hw in chunks(T_C, 1024):
                        ap = (ft * (T_C // 1024) + h0 // 1024) % 2

                        def f(pe, fc=fc, ft=ft, h0=h0, hw=hw, par=par, ap=ap):
                            S.wait(pe, "PE", T["xn2T_done"])
                            S.wait(pe, "PE", w_toks[fc])
                            S.wait(pe, "PE", T.get("b2b_tok"))
                            S.wait(pe, "PE", T.get(f"gel{ap}"))
                            for dt in range(NDT):
                                for q0, qw in chunks(hw, 512):
                                    ins = pe.matmul(
                                        p_ap[ap][:, q0:q0 + qw],
                                        s_w1[par][:, dt, ft * 128:(ft + 1) * 128],
                                        xn2T[:, dt, h0 + q0:h0 + q0 + qw],
                                        start=(dt == 0), stop=(dt == NDT - 1))
                            T["pa"] = S.inc("pe", ins)
                        block.tensor(f)

                        def f(a, fc=fc, ft=ft, h0=h0, hw=hw, par=par, ap=ap):
                            S.wait(a, "ACT", T["pa"])
                            ins = a.activation(
                                s_gel[:, ft, h0:h0 + hw], p_ap[ap][:, 0:hw],
                                AF.Gelu, bias=s_b1p[par][:, ft:ft + 1])
                            T[f"gel{ap}"] = S.inc("act", ins)
                            T["gel"] = T[f"gel{ap}"]
                        block.scalar(f)

                for tt in range(NT_C):
                    bp = tt % 2

                    def f(pe, fc=fc, tt=tt, par=par, bp=bp):
                        S.wait(pe, "PE", T["gel"])
                        S.wait(pe, "PE", T.get(f"xzup{bp}"))
                        for ft in range(NFT):
                            for h0, hw in chunks(D, 512):
                                ins = pe.matmul(
                                    p_bp[bp][:, h0:h0 + hw],
                                    s_gel[:, ft, tt * 128:(tt + 1) * 128],
                                    s_w2[par][:, ft, h0:h0 + hw],
                                    start=(ft == 0), stop=(ft == NFT - 1))
                        T["pb"] = S.inc("pe", ins)
                    block.tensor(f)

                    def f(v, fc=fc, tt=tt, bp=bp):
                        S.wait(v, "DVE", T["pb"])
                        S.wait(v, "DVE", T.get("tapguard3"))
                        S.wait(v, "DVE", T[f"xz{tt}"])
                        ins = v.tensor_tensor(
                            xz[:, tt, :], xz[:, tt, :], p_bp[bp][:], ALU.add)
                        T[f"xz{tt}"] = S.inc("dve", ins)
                        if fc == NFC_MLP - 1:
                            S.wait(v, "DVE", T[f"xz{tt}"])
                            S.wait(v, "DVE", T["b2b_tok"])
                            ins = v.tensor_tensor(
                                xz[:, tt, :], xz[:, tt, :], s_b2b[:], ALU.add)
                            T[f"xz{tt}"] = S.inc("dve", ins)
                        T[f"xzup{bp}"] = T[f"xz{tt}"]
                        T["xzup"] = T[f"xz{tt}"]
                    block.vector(f)
                w_used[fc] = T["xzup"]
                if fc + 2 < NFC_MLP:
                    mlp_prefetch(fc + 2)

        xn2T_cm.__exit__(None, None, None)

        with ExitStack() as _es:
            xfT = _es.enter_context(nc.sbuf_tensor("d_xfT", [128, NDT, T_C], BF16))
            s_cw = _es.enter_context(nc.sbuf_tensor("d_cw", [128, NT_C, NE], F32))
            s_xf = _es.enter_context(nc.sbuf_tensor("d_xf", [128, D], BF16))
            s_rwt = _es.enter_context(nc.sbuf_tensor("d_rwt", [128, NDT, NE], BF16))
            s_rr = _es.enter_context(nc.sbuf_tensor("d_rr", [128, NT_C, NE], F32))
            s_er = _es.enter_context(nc.sbuf_tensor("d_er", [128, NT_C, NE], F32))
            s_mk1 = _es.enter_context(nc.sbuf_tensor("d_msk1", [128, NT_C, NE], F32))
            s_mk2 = _es.enter_context(nc.sbuf_tensor("d_msk2", [128, NT_C, NE], F32))
            s_m1 = _es.enter_context(nc.sbuf_tensor("d_m1", [128, NT_C], F32))
            s_m2 = _es.enter_context(nc.sbuf_tensor("d_m2", [128, NT_C], F32))
            s_den = _es.enter_context(nc.sbuf_tensor("d_den", [128, NT_C], F32))
            s_rden = _es.enter_context(nc.sbuf_tensor("d_rden", [128, NT_C], F32))
            s_negbig = _es.enter_context(nc.sbuf_tensor("d_negbig", [128, 1], F32))
            def f(gp):
                S.wait(gp, "GP", w_used.get(NFC_MLP - 1))
                ins = gp.dma_start(
                    s_rwt[:], rwt_i.rearrange("(dt p) e -> p dt e", p=128))
                T["rwt"] = S.inc("gpw", ins, 16)
            block.gpsimd(f)

            def f(v):
                S.wait(v, "DVE", w_used.get(NFC_MLP - 1))
                ins = v.memset(s_negbig[:], -1e30)
                T["negbig"] = S.inc("dve", ins)
            block.vector(f)

            def f(a):
                load_act_set(a, 6)
            block.scalar(f)

            with ExitStack() as _es:
                p_tr0 = _es.enter_context(nc.psum_tensor("d_tr0", [128, 128], BF16))
                p_tr1 = _es.enter_context(nc.psum_tensor("d_tr1", [128, 128], BF16))
                p_lg = _es.enter_context(nc.psum_tensor("d_lg", [128, NE], F32))
                p_tr = [p_tr0, p_tr1]
                trcp = [None, None]
                for tt in range(NT_C):
                    par = tt % 2
                    t_r, _ = emit_ln(xz[:, tt, :], [T[f"xz{tt}"]], f"dl{tt}", par)

                    def f(v, tt=tt, par=par):
                        S.wait(v, "DVE", t_r)
                        S.wait(v, "DVE", T.get("xfT_cp"))
                        ins = v.tensor_scalar(
                            s_xf[:], s_cc[par][:], s_rstd[par][:], None,
                            ALU.mult)
                        T["xf"] = S.inc("dve", ins)
                        T[f"ln_guard{par}"] = T["xf"]
                    block.vector(f)

                    for j in range(NDT):
                        tp = j % 2

                        def f(pe, tt=tt, j=j, tp=tp):
                            S.wait(pe, "PE", T["xf"])
                            S.wait(pe, "PE", trcp[tp])
                            ins = pe.transpose(
                                p_tr[tp][:], s_xf[:, j * 128:(j + 1) * 128],
                                s_idn[:])
                            T["dtr"] = S.inc("pe", ins)
                        block.tensor(f)

                        def f(a, tt=tt, j=j, tp=tp):
                            S.wait(a, "ACT", T["dtr"])
                            ins = a.copy(
                                xfT[:, j, tt * 128:(tt + 1) * 128], p_tr[tp][:])
                            trcp[tp] = S.inc("act", ins)
                            T["xfT_cp"] = trcp[tp]
                        block.scalar(f)

                    def f(pe, tt=tt):
                        S.wait(pe, "PE", T["xfT_cp"])
                        S.wait(pe, "PE", T["rwt"])
                        S.wait(pe, "PE", T.get("rr_ev"))
                        for dt in range(NDT):
                            ins = pe.matmul(
                                p_lg[:], xfT[:, dt, tt * 128:(tt + 1) * 128],
                                s_rwt[:, dt, :],
                                start=(dt == 0), stop=(dt == NDT - 1))
                        T["lg"] = S.inc("pe", ins)
                    block.tensor(f)

                    def f(a, tt=tt):
                        S.wait(a, "ACT", T["lg"])
                        ins = a.copy(s_rr[:, tt, :], p_lg[:])
                        T["rr_ev"] = S.inc("act", ins)
                        T[f"rr{tt}"] = T["rr_ev"]
                    block.scalar(f)

                    def f(sync, tt=tt):
                        S.wait(sync, "SP", T[f"xz{tt}"])
                        ins = sync.dma_start(
                            h1sp[tt * 128:(tt + 1) * 128, :], xz[:, tt, :])
                        T["h1sp_all"] = S.inc("sph1s", ins, 16)
                    block.sync(f)

                if debug_taps:
                    def f(sync):
                        S.wait(sync, "SP", T["xz0"])
                        S.wait(sync, "SP", T.get("lasttap"))
                        ins = sync.dma_start(taps["tap_h1"], xz[:, 0, :])
                        T["tapguard4"] = S.inc("sptap", ins, 16)
                        T["lasttap"] = T["tapguard4"]
                    block.sync(f)

            def f(v):
                for tt in range(NT_C):
                    S.wait(v, "DVE", T[f"rr{tt}"])
                ins = v.reduce_max(
                    s_m1[:].rearrange("p (t one) -> p t one", one=1),
                    s_rr[:], axis=AX.X)
                T["rmx"] = S.inc("dve", ins)
            block.vector(f)

            def f(a):
                S.wait(a, "ACT", T["rmx"])
                for tt in range(NT_C):
                    ins = a.mul(s_m2[:, tt:tt + 1], s_m1[:, tt:tt + 1], -1.0)
                    t = S.inc("act", ins)
                    S.wait(a, "ACT", t)
                    ins = a.activation(
                        s_er[:, tt, :], s_rr[:, tt, :], AF.Exp,
                        bias=s_m2[:, tt:tt + 1])
                    t = S.inc("act", ins)
                T["er"] = t
            block.scalar(f)

            def f(v):
                S.wait(v, "DVE", T["er"])
                ins = v.reduce_max(
                    s_m1[:].rearrange("p (t one) -> p t one", one=1),
                    s_er[:], axis=AX.X)
                t = S.inc("dve", ins)
                S.wait(v, "DVE", t)
                for tt in range(NT_C):
                    ins = v.tensor_scalar(
                        s_mk1[:, tt, :], s_er[:, tt, :], s_m1[:, tt:tt + 1],
                        None, ALU.is_ge)
                t = S.inc("dve", ins)
                S.wait(v, "DVE", t)
                S.wait(v, "DVE", T["negbig"])
                ins = v.scalar_tensor_tensor(
                    s_mk2[:], s_mk1[:], s_negbig[:], s_er[:], ALU.mult, ALU.add)
                t = S.inc("dve", ins)
                S.wait(v, "DVE", t)
                ins = v.reduce_max(
                    s_m2[:].rearrange("p (t one) -> p t one", one=1),
                    s_mk2[:], axis=AX.X)
                t = S.inc("dve", ins)
                S.wait(v, "DVE", t)
                for tt in range(NT_C):
                    ins = v.tensor_scalar(
                        s_mk2[:, tt, :], s_mk2[:, tt, :], s_m2[:, tt:tt + 1],
                        None, ALU.is_ge)
                t = S.inc("dve", ins)
                S.wait(v, "DVE", t)
                ins = v.tensor_tensor(s_den[:], s_m1[:], s_m2[:], ALU.add)
                t = S.inc("dve", ins)
                S.wait(v, "DVE", t)
                ins = v.reciprocal(s_rden[:], s_den[:])
                t = S.inc("dve", ins)
                S.wait(v, "DVE", t)
                ins = v.tensor_tensor(s_m1[:], s_m1[:], s_rden[:], ALU.mult)
                t = S.inc("dve", ins)
                ins = v.tensor_tensor(s_m2[:], s_m2[:], s_rden[:], ALU.mult)
                t = S.inc("dve", ins)
                S.wait(v, "DVE", t)
                for tt in range(NT_C):
                    ins = v.tensor_scalar(
                        s_mk1[:, tt, :], s_mk1[:, tt, :], s_m1[:, tt:tt + 1],
                        None, ALU.mult)
                    ins = v.tensor_scalar(
                        s_mk2[:, tt, :], s_mk2[:, tt, :], s_m2[:, tt:tt + 1],
                        None, ALU.mult)
                t = S.inc("dve", ins)
                S.wait(v, "DVE", t)
                ins = v.tensor_tensor(s_cw[:], s_mk1[:], s_mk2[:], ALU.add)
                T["cw"] = S.inc("dve", ins)
            block.vector(f)

            if debug_taps:
                def f(sync):
                    S.wait(sync, "SP", T["cw"])
                    S.wait(sync, "SP", T.get("lasttap"))
                    ins = sync.dma_start(taps["tap_cw"], s_cw[:, 0, :])
                    T["lasttap"] = S.inc("sptap", ins, 16)
                block.sync(f)

            with ExitStack() as _es:
                s_acc = xz
                s_wg0 = _es.enter_context(nc.sbuf_tensor("d_wg0", [128, NDT, FC], BF16))
                s_wg1 = _es.enter_context(nc.sbuf_tensor("d_wg1", [128, NDT, FC], BF16))
                s_wu0 = _es.enter_context(nc.sbuf_tensor("d_wu0", [128, NDT, FC], BF16))
                s_wu1 = _es.enter_context(nc.sbuf_tensor("d_wu1", [128, NDT, FC], BF16))
                s_wd0 = _es.enter_context(nc.sbuf_tensor("d_wd0", [128, NFT, D], BF16))
                s_wd1 = _es.enter_context(nc.sbuf_tensor("d_wd1", [128, NFT, D], BF16))
                s_h = _es.enter_context(nc.sbuf_tensor("d_h", [128, NFT, T_C], BF16))
                s_sg = _es.enter_context(nc.sbuf_tensor("d_sg", [128, 1024], F32))
                s_do = _es.enter_context(nc.sbuf_tensor("d_out", [128, D], F32))
                s_h1t = _es.enter_context(nc.sbuf_tensor("d_h1t", [128, D], F32))
                p_g = _es.enter_context(nc.psum_tensor("d_g", [128, 1024], F32))
                p_u = _es.enter_context(nc.psum_tensor("d_u", [128, 1024], F32))
                p_d0 = _es.enter_context(nc.psum_tensor("d_d0", [128, D], F32))
                p_d1 = _es.enter_context(nc.psum_tensor("d_d1", [128, D], F32))
                s_wg = [s_wg0, s_wg1]
                s_wu = [s_wu0, s_wu1]
                s_wd = [s_wd0, s_wd1]
                p_d = [p_d0, p_d1]

                def f(v):
                    S.wait(v, "DVE", T["cw"])
                    S.wait(v, "DVE", T["h1sp_all"])
                    S.wait(v, "DVE", T.get("tapguard4"))
                    for tt in range(NT_C):
                        ins = v.memset(s_acc[:, tt, :], 0.0)
                    T["acc0"] = S.inc("dve", ins)
                block.vector(f)

                NFC_D = DFF // FC
                steps = [(e, fc) for e in range(NE) for fc in range(NFC_D)]
                wd_toks = {}
                w_usedD = {}

                def d_prefetch(si):
                    e, fc = steps[si]
                    par = si % 2

                    def f(gp, e=e, fc=fc, si=si, par=par):
                        S.wait(gp, "GP", w_usedD.get(si - 2))
                        if si < 2:
                            S.wait(gp, "GP", T["rwt"])
                            S.wait(gp, "GP", T["cw"])
                        ins = gp.dma_start(
                            s_wg[par][:],
                            wg_r[e, fc].rearrange("(dt p) c -> p dt c", p=128))
                        S.inc(f"gpd{par}", ins, 16)
                        ins = gp.dma_start(
                            s_wu[par][:],
                            wu_r[e, fc].rearrange("(dt p) c -> p dt c", p=128))
                        S.inc(f"gpd{par}", ins, 16)
                        ins = gp.dma_start(
                            s_wd[par][:],
                            wd_i[e, fc * FC:(fc + 1) * FC, :].rearrange(
                                "(ft p) c -> p ft c", p=128))
                        wd_toks[si] = S.inc(f"gpd{par}", ins, 16)
                    block.gpsimd(f)

                d_prefetch(0)
                if len(steps) > 1:
                    d_prefetch(1)

                def f(a):
                    load_act_set(a, 18)
                block.scalar(f)

                for si, (e, fc) in enumerate(steps):
                    par = si % 2
                    for ft in range(NFT):
                        for h0, hw in chunks(T_C, 1024):
                            def f(pe, si=si, ft=ft, h0=h0, hw=hw, par=par):
                                S.wait(pe, "PE", wd_toks[si])
                                S.wait(pe, "PE", T["cw"])
                                S.wait(pe, "PE", T.get("sg"))
                                for dt in range(NDT):
                                    for q0, qw in chunks(hw, 512):
                                        ins = pe.matmul(
                                            p_g[:, q0:q0 + qw],
                                            s_wg[par][:, dt,
                                                      ft * 128:(ft + 1) * 128],
                                            xfT[:, dt, h0 + q0:h0 + q0 + qw],
                                            start=(dt == 0),
                                            stop=(dt == NDT - 1))
                                T["pg"] = S.inc("pe", ins)
                            block.tensor(f)

                            def f(a, hw=hw):
                                S.wait(a, "ACT", T["pg"])
                                S.wait(a, "ACT", T.get("hop"))
                                ins = a.activation(
                                    s_sg[:, 0:hw], p_g[:, 0:hw], AF.Silu)
                                T["sg"] = S.inc("act", ins)
                            block.scalar(f)

                            def f(pe, si=si, ft=ft, h0=h0, hw=hw, par=par):
                                S.wait(pe, "PE", T.get("hop"))
                                for dt in range(NDT):
                                    for q0, qw in chunks(hw, 512):
                                        ins = pe.matmul(
                                            p_u[:, q0:q0 + qw],
                                            s_wu[par][:, dt,
                                                      ft * 128:(ft + 1) * 128],
                                            xfT[:, dt, h0 + q0:h0 + q0 + qw],
                                            start=(dt == 0),
                                            stop=(dt == NDT - 1))
                                T["pu"] = S.inc("pe", ins)
                            block.tensor(f)

                            def f(v, ft=ft, h0=h0, hw=hw):
                                S.wait(v, "DVE", T["pu"])
                                S.wait(v, "DVE", T["sg"])
                                S.wait(v, "DVE", T.get(f"pd{NT_C - 1}"))
                                ins = v.tensor_tensor(
                                    s_h[:, ft, h0:h0 + hw], s_sg[:, 0:hw],
                                    p_u[:, 0:hw], ALU.mult)
                                T["hop"] = S.inc("dve", ins)
                            block.vector(f)

                    for tt in range(NT_C):
                        dp = tt % 2

                        def f(pe, si=si, tt=tt, dp=dp, par=par):
                            S.wait(pe, "PE", T["hop"])
                            S.wait(pe, "PE", T.get(f"accup{(tt - 2) % NT_C}"))
                            for ft in range(NFT):
                                for h0, hw in chunks(D, 512):
                                    ins = pe.matmul(
                                        p_d[dp][:, h0:h0 + hw],
                                        s_h[:, ft, tt * 128:(tt + 1) * 128],
                                        s_wd[par][:, ft, h0:h0 + hw],
                                        start=(ft == 0), stop=(ft == NFT - 1))
                            T[f"pd{tt}"] = S.inc("pe", ins)
                        block.tensor(f)

                        def f(v, e=e, tt=tt, dp=dp):
                            S.wait(v, "DVE", T[f"pd{tt}"])
                            S.wait(v, "DVE", T["acc0"])
                            S.wait(v, "DVE", T["cw"])
                            ins = v.scalar_tensor_tensor(
                                s_acc[:, tt, :], p_d[dp][:],
                                s_cw[:, tt, e:e + 1], s_acc[:, tt, :],
                                ALU.mult, ALU.add)
                            T[f"accup{tt}"] = S.inc("dve", ins)
                        block.vector(f)
                    w_usedD[si] = T[f"accup{NT_C - 1}"]
                    if si + 2 < len(steps):
                        d_prefetch(si + 2)

                for tt in range(NT_C):
                    def f(sync, tt=tt):
                        S.wait(sync, "SP", T["h1sp_all"])
                        S.wait(sync, "SP", T.get("dout"))
                        ins = sync.dma_start(
                            s_h1t[:], h1sp[tt * 128:(tt + 1) * 128, :])
                        T["h1t"] = S.inc("sph1t", ins, 16)
                    block.sync(f)

                    def f(v, tt=tt):
                        S.wait(v, "DVE", T["h1t"])
                        S.wait(v, "DVE", T[f"accup{tt}"])
                        S.wait(v, "DVE", T.get("outdma"))
                        ins = v.tensor_tensor(
                            s_do[:], s_h1t[:], s_acc[:, tt, :], ALU.add)
                        T["dout"] = S.inc("dve", ins)
                    block.vector(f)

                    def f(sync, tt=tt):
                        S.wait(sync, "SP", T["dout"])
                        ins = sync.dma_start(
                            out_t[tt * 128:(tt + 1) * 128, :], s_do[:])
                        T["outdma"] = S.inc("spout", ins, 16)
                    block.sync(f)

                def f(sync):
                    for nm in S.n:
                        if nm.startswith("sp") or nm.startswith("gp"):
                            if S.n[nm] > 0:
                                sync.wait_ge(S.h[nm], S.n[nm])
                block.sync(f)

    for cm in reversed(ctx):
        cm.__exit__(None, None, None)
    return nc



def _rope_tables(T):
    inv = (1.0 / np.power(np.float32(10000.0),
                          np.arange(0, HD, 2, dtype=np.float32) / np.float32(HD))
           ).astype(np.float32)
    freqs = np.arange(T, dtype=np.float32)[:, None] * inv[None, :]
    emb = np.concatenate([freqs, freqs], axis=-1)
    cos_td = np.cos(emb).astype(np.float32)
    sin_td = np.sin(emb).astype(np.float32)
    rows = np.arange(128) % HD
    cosk = cos_td.T[rows, :]
    sgn = np.where((np.arange(128) % HD) < 32, -1.0, 1.0).astype(np.float32)
    s1k = sin_td.T[rows, :] * sgn[:, None]
    return np.ascontiguousarray(cosk), np.ascontiguousarray(s1k)


def _perm_bd():
    p = np.zeros((128, 128), np.float32)
    for b in (0, 64):
        for m in range(HD):
            p[b + (m + 32) % HD, b + m] = 1.0
    return p


def make_in_maps(inputs, T_B, T_C, n_cores):
    x = np.asarray(inputs["L0_attn_x"], np.float32)
    cosk, s1k = _rope_tables(T_B)
    idn = np.eye(128, dtype=ml_dtypes.bfloat16)
    perm = _perm_bd()

    def wr(w):
        w = np.asarray(w, np.float32)
        return np.ascontiguousarray(w.reshape(D, NHG, 128).transpose(1, 0, 2))

    shared = {
        "idn": idn, "permbd": perm,
        "wq_r": wr(inputs["L0_attn_Wq"]),
        "wk_r": wr(inputs["L0_attn_Wk"]),
        "wv_r": np.ascontiguousarray(np.asarray(inputs["L0_attn_Wv"], np.float32).reshape(D, NHG // 2, 256).transpose(1, 0, 2)),
        "bq": np.asarray(inputs["L0_attn_bq"], np.float32),
        "bk": np.asarray(inputs["L0_attn_bk"], np.float32),
        "bv": np.asarray(inputs["L0_attn_bv"], np.float32),
        "wo": np.asarray(inputs["L0_attn_Wo"], np.float32),
        "bo": np.asarray(inputs["L0_attn_bo"], np.float32),
        "g1": np.asarray(inputs["L0_attn_g1"], np.float32),
        "be1": np.asarray(inputs["L0_attn_be1"], np.float32),
        "g2": np.asarray(inputs["L0_attn_g2"], np.float32),
        "be2": np.asarray(inputs["L0_attn_be2"], np.float32),
        "w1_r": np.ascontiguousarray(
            np.asarray(inputs["L0_attn_W1"], np.float32)
            .reshape(D, DFF // FC, FC).transpose(1, 0, 2)),
        "b1_r": np.ascontiguousarray(np.asarray(inputs["L0_attn_b1"], np.float32).reshape(DFF // FC, NFT, 128).transpose(0, 2, 1)),
        "w2": np.asarray(inputs["L0_attn_W2"], np.float32),
        "b2": np.asarray(inputs["L0_attn_b2"], np.float32),
        "rwt": np.ascontiguousarray(
            np.asarray(inputs["L0_ffn_router_w"], np.float32).T),
        "wg_r": np.ascontiguousarray(
            np.asarray(inputs["L0_ffn_w_gate"], np.float32)
            .reshape(NE, D, DFF // FC, FC).transpose(0, 2, 1, 3)),
        "wu_r": np.ascontiguousarray(
            np.asarray(inputs["L0_ffn_w_up"], np.float32)
            .reshape(NE, D, DFF // FC, FC).transpose(0, 2, 1, 3)),
        "wd": np.asarray(inputs["L0_ffn_w_down"], np.float32),
    }
    maps = []
    cpb = T_B // T_C
    for c in range(n_cores):
        b = c // cpb
        q0 = (c % cpb) * T_C
        xb = np.roll(x[b], -q0, axis=0)
        m = dict(shared)
        m["x_b"] = np.ascontiguousarray(xb)
        m["cosk"] = np.ascontiguousarray(np.roll(cosk, -q0, axis=1))
        m["s1k"] = np.ascontiguousarray(np.roll(s1k, -q0, axis=1))
        maps.append(m)
    return maps


def kernel(**inputs):
    T_B, T_C = 8192, 2048
    n_cores = 8
    nc = build_program(T_B, T_C)
    maps = make_in_maps(inputs, T_B, T_C, n_cores)
    res = run_bass_kernel_spmd(nc, maps, core_ids=list(range(n_cores)))
    x = inputs["L0_attn_x"]
    B = x.shape[0]
    out = np.empty((B, T_B, D), np.float32)
    cpb = T_B // T_C
    for c in range(n_cores):
        b = c // cpb
        q0 = (c % cpb) * T_C
        out[b, q0:q0 + T_C, :] = res.results[c]["out"]
    return out


# version 14
# speedup vs baseline: 1.0618x; 1.0618x over previous
from contextlib import ExitStack
import numpy as np
import ml_dtypes
import concourse.bass as bass
import concourse.mybir as mybir
from concourse.bass_utils import run_bass_kernel_spmd

F32 = mybir.dt.float32
BF16 = mybir.dt.bfloat16
AF = mybir.ActivationFunctionType
ALU = mybir.AluOpType
AX = mybir.AxisListType

D = 1024
NH = 16
HD = 64
NHG = 8
DFF = 4096
NE = 8
NDT = D // 128
FC = 512
NFT = FC // 128


def chunks(total, size):
    return [(i, min(size, total - i)) for i in range(0, total, size)]


class Sems:
    def __init__(self, nc, names):
        self.h = {}
        self.n = {}
        self.lastw = {}
        self.cms = []
        for name in names:
            cm = nc.semaphore(name)
            self.cms.append(cm)
            self.h[name] = cm.__enter__()
            self.n[name] = 0

    def inc(self, name, instr, amount=1):
        self.n[name] += amount
        instr.then_inc(self.h[name], amount)
        return (name, self.n[name])

    def wait(self, eng, who, tok):
        if tok is None:
            return
        name, val = tok
        assert val <= self.n[name], f"wait on future {name}: {val} > {self.n[name]}"
        key = (who, name)
        if self.lastw.get(key, 0) >= val:
            return
        eng.wait_ge(self.h[name], val)
        self.lastw[key] = val

    def close(self):
        for cm in reversed(self.cms):
            cm.__exit__(None, None, None)


def build_program(T_B, T_C, debug_taps=False):
    NT_B = T_B // 128
    NT_C = T_C // 128
    BLK = min(1024, T_B)
    NBLK = T_B // BLK
    NBT = BLK // 128
    QC = min(1024, T_C)
    NQC = T_C // QC

    nc = bass.Bass("TRN2", target_bir_lowering=False, debug=False)

    din = {}
    def DI(name, shape, dtype=F32):
        din[name] = nc.dram_tensor(name, shape, dtype, kind="ExternalInput").ap()
        return din[name]

    x_b = DI("x_b", [T_B, D])
    cosk = DI("cosk", [128, T_B])
    s1k = DI("s1k", [128, T_B])
    idn_i = DI("idn", [128, 128], BF16)
    perm_i = DI("permbd", [128, 128])
    wq_r = DI("wq_r", [NHG, D, 128])
    wk_r = DI("wk_r", [NHG, D, 128])
    wv_r = DI("wv_r", [NHG // 2, D, 256])
    bq_i = DI("bq", [D])
    bk_i = DI("bk", [D])
    bv_i = DI("bv", [D])
    wo_i = DI("wo", [D, D])
    bo_i = DI("bo", [D])
    g1_i = DI("g1", [D])
    be1_i = DI("be1", [D])
    g2_i = DI("g2", [D])
    be2_i = DI("be2", [D])
    w1_r = DI("w1_r", [DFF // FC, D, FC])
    b1_i = DI("b1_r", [DFF // FC, 128, FC // 128])
    w2_i = DI("w2", [DFF, D])
    b2_i = DI("b2", [D])
    rwt_i = DI("rwt", [D, NE])
    wg_r = DI("wg_r", [NE, DFF // FC, D, FC])
    wu_r = DI("wu_r", [NE, DFF // FC, D, FC])
    wd_i = DI("wd", [NE, DFF, D])

    out_t = nc.dram_tensor("out", [T_C, D], F32, kind="ExternalOutput").ap()
    spill = nc.dram_tensor("xn1t_spill", [NDT, 128, T_B], BF16, kind="Internal").ap()
    h1sp = nc.dram_tensor("h1_spill", [T_C, D], F32, kind="Internal").ap()
    vsp = nc.dram_tensor("v_spill", [128, T_B // 128, 128], BF16, kind="Internal").ap()

    taps = {}
    if debug_taps:
        for name, shape, dt in [
            ("tap_xn1", [128, D], BF16), ("tap_khat", [65, T_B], BF16),
            ("tap_vbuf", [128, 65], BF16), ("tap_qhat", [65, T_C], BF16),
            ("tap_attnT", [128, T_C], BF16), ("tap_xz", [128, D], F32),
            ("tap_xn2T", [128, T_C], BF16), ("tap_h1", [128, D], F32),
            ("tap_cw", [128, NE], F32),
        ]:
            taps[name] = nc.dram_tensor(name, shape, dt, kind="ExternalOutput").ap()

    ctx = []
    def ent(cm):
        ctx.append(cm)
        return cm.__enter__()

    sb = lambda n, s, d: ent(nc.sbuf_tensor(n, s, d))

    S = Sems(nc, ["sprow", "sptap", "spx0", "spx1", "spx2", "spx3", "spst0", "spst1",
              "spxb0", "spxb1", "sph1s", "sph1t", "spout", "spvs", "spvr",
              "gpw", "gpw10", "gpw11", "gpd0", "gpd1",
              "pe", "act", "dve", "pool"])

    s_idn = sb("s_idn", [128, 128], BF16)
    s_perm = sb("s_perm", [128, 128], F32)
    s_eps = sb("s_eps", [128, 1], F32)
    s_one1 = sb("s_one1", [1, 128], F32)
    s_ones64 = sb("s_ones64", [64, 1], BF16)
    s_bones = sb("s_bones", [128, 2], BF16)
    s_ln102 = sb("s_ln102", [1, 1], F32)
    s_mm1 = [sb("s_mm1a", [128, 1], F32), sb("s_mm1b", [128, 1], F32)]
    s_mm2 = [sb("s_mm2a", [128, 1], F32), sb("s_mm2b", [128, 1], F32)]
    s_ss = [sb("s_ssa", [128, 1], F32), sb("s_ssb", [128, 1], F32)]
    s_rstd = [sb("s_rstda", [128, 1], F32), sb("s_rstdb", [128, 1], F32)]
    s_cc = [sb("s_cca", [128, D], F32), sb("s_ccb", [128, D], F32)]
    s_cc2 = [sb("s_cc2a", [128, D], F32), sb("s_cc2b", [128, D], F32)]
    s_yy = [sb("s_yya", [128, D], F32), sb("s_yyb", [128, D], F32)]

    block = ent(nc.Block())
    T = {}

    def f(v):
        v.memset(s_eps[:], 1e-5)
        v.memset(s_one1[:], 1.0)
        v.memset(s_ones64[:], 1.0)
        ins = v.memset(s_bones[:], 0.0)
        t = S.inc("dve", ins)
        S.wait(v, "DVE", t)
        v.memset(s_bones[0:64, 0:1], 1.0)
        v.memset(s_bones[64:128, 1:2], 1.0)
        ins = v.memset(s_ln102[:], float(np.log(1.05)))
        T["init_dve"] = S.inc("dve", ins)
    block.vector(f)

    def load_act_set(a, set_id):
        a.add_instruction(mybir.InstLoadActFuncSet(
            name=nc.get_next_instruction_name(),
            act_func_set_id=set_id, ins=[], outs=[]))

    def f(a):
        load_act_set(a, 6)
    block.scalar(f)

    def f(sync):
        ins = sync.dma_start(s_idn[:], idn_i)
        S.inc("sprow", ins, 16)
        ins = sync.dma_start(s_perm[:], perm_i)
        T["init_sp"] = S.inc("sprow", ins, 16)
    block.sync(f)

    def emit_ln(src_ap, pre_toks, tag, par=0):
        def f(v, pre=list(pre_toks)):
            S.wait(v, "DVE", T.get(f"ln_guard{par}"))
            for t in pre:
                S.wait(v, "DVE", t)
            ins = v.reduce_sum(s_mm1[par][:], src_ap, axis=AX.X)
            T[tag + "_sum"] = S.inc("dve", ins)
        block.vector(f)

        def f(a):
            S.wait(a, "ACT", T[tag + "_sum"])
            ins = a.mul(s_mm2[par][:], s_mm1[par][:], -1.0 / D)
            T[tag + "_mn"] = S.inc("act", ins)
        block.scalar(f)

        def f(v):
            S.wait(v, "DVE", T[tag + "_mn"])
            ins = v.tensor_scalar(s_cc[par][:], src_ap, s_mm2[par][:], None, ALU.add)
            T[tag + "_c"] = S.inc("dve", ins)
        block.vector(f)

        def f(a):
            S.wait(a, "ACT", T[tag + "_c"])
            ins = a.activation(s_cc2[par][:], s_cc[par][:], AF.Square,
                               accum_out=s_ss[par][:])
            t = S.inc("act", ins)
            S.wait(a, "ACT", t)
            ins = a.activation(s_ss[par][:], s_ss[par][:], AF.Ln, bias=s_eps[:],
                               scale=1.0 / D)
            t = S.inc("act", ins)
            S.wait(a, "ACT", t)
            ins = a.activation(s_rstd[par][:], s_ss[par][:], AF.Exp, scale=-0.5)
            T[tag + "_rstd"] = S.inc("act", ins)
        block.scalar(f)
        return T[tag + "_rstd"], T[tag + "_c"]

    def emit_ln2(src_ap, gb_ap, bb_ap, out_ap, pre_toks, tag, par=0):
        t_r, _ = emit_ln(src_ap, pre_toks, tag + "a", par)

        def f(v):
            S.wait(v, "DVE", t_r)
            ins = v.tensor_scalar(s_yy[par][:], s_cc[par][:], s_rstd[par][:],
                                  None, ALU.mult)
            T[tag + "_y"] = S.inc("dve", ins)
        block.vector(f)

        t_r2, _ = emit_ln(s_yy[par][:], [T[tag + "_y"]], tag + "b", par)

        def f(v):
            S.wait(v, "DVE", t_r2)
            ins = v.scalar_tensor_tensor(
                s_cc2[par][:], s_cc[par][:], s_rstd[par][:], gb_ap,
                ALU.mult, ALU.mult)
            t = S.inc("dve", ins)
            S.wait(v, "DVE", t)
            S.wait(v, "DVE", T.get("tapguard"))
            ins = v.tensor_tensor(out_ap, s_cc2[par][:], bb_ap, ALU.add)
            T[tag + "_out"] = S.inc("dve", ins)
            T[f"ln_guard{par}"] = T[tag + "_out"]
        block.vector(f)
        return T[tag + "_out"]

    def emit_bcast(row_ap, dst_ap, p_bc, row_tok, tag):
        for half in range(2):
            def fpe(pe, half=half):
                S.wait(pe, "PE", row_tok)
                S.wait(pe, "PE", T["init_sp"])
                S.wait(pe, "PE", T["init_dve"])
                S.wait(pe, "PE", T.get("bc_ev"))
                ins = pe.matmul(
                    p_bc[:, 0:512], s_one1[:],
                    row_ap[:, half * 512:(half + 1) * 512],
                    start=True, stop=True)
                T["bc_mm"] = S.inc("pe", ins)
            block.tensor(fpe)

            def fv(v, half=half):
                S.wait(v, "DVE", T["bc_mm"])
                ins = v.tensor_copy(
                    dst_ap[:, half * 512:(half + 1) * 512], p_bc[:, 0:512])
                T["bc_ev"] = S.inc("dve", ins)
            block.vector(fv)
        T[tag + "_tok"] = T["bc_ev"]
        T["bcrow_used"] = T["bc_mm"]
        return T["bc_ev"]

    with ExitStack() as _es:
        s_x0 = _es.enter_context(nc.sbuf_tensor("a1_x0", [128, D], F32))
        s_x1 = _es.enter_context(nc.sbuf_tensor("a1_x1", [128, D], F32))
        s_x2 = _es.enter_context(nc.sbuf_tensor("a1_x2", [128, D], F32))
        s_x3 = _es.enter_context(nc.sbuf_tensor("a1_x3", [128, D], F32))
        s_xn1 = [_es.enter_context(nc.sbuf_tensor("a1_xn1a", [128, D], BF16)),
                 _es.enter_context(nc.sbuf_tensor("a1_xn1b", [128, D], BF16)),
                 _es.enter_context(nc.sbuf_tensor("a1_xn1c", [128, D], BF16)),
                 _es.enter_context(nc.sbuf_tensor("a1_xn1d", [128, D], BF16))]
        s_g1b = _es.enter_context(nc.sbuf_tensor("a1_g1b", [128, D], F32))
        s_be1b = _es.enter_context(nc.sbuf_tensor("a1_be1b", [128, D], F32))
        s_row = _es.enter_context(nc.sbuf_tensor("a1_row", [1, D], F32))
        s_st0 = _es.enter_context(nc.sbuf_tensor("a1_st0", [128, NDT, BLK], BF16))
        s_st1 = _es.enter_context(nc.sbuf_tensor("a1_st1", [128, NDT, BLK], BF16))
        p_bc = _es.enter_context(nc.psum_tensor("a1_bc", [128, 512], F32))
        p_tr0 = _es.enter_context(nc.psum_tensor("a1_tr0", [128, 128], BF16))
        p_tr1 = _es.enter_context(nc.psum_tensor("a1_tr1", [128, 128], BF16))
        s_x = [s_x0, s_x1, s_x2, s_x3]
        s_st = [s_st0, s_st1]
        p_tr = [p_tr0, p_tr1]
        for nm in ("c", "c2", "y"):
            pass
        extra = []
        for k in range(2):
            extra.append((
                _es.enter_context(nc.sbuf_tensor(f"a1mm1{k}", [128, 1], F32)),
                _es.enter_context(nc.sbuf_tensor(f"a1mm2{k}", [128, 1], F32)),
                _es.enter_context(nc.sbuf_tensor(f"a1ss{k}", [128, 1], F32)),
                _es.enter_context(nc.sbuf_tensor(f"a1rstd{k}", [128, 1], F32)),
                _es.enter_context(nc.sbuf_tensor(f"a1cc{k}", [128, D], F32)),
                _es.enter_context(nc.sbuf_tensor(f"a1cc2{k}", [128, D], F32)),
                _es.enter_context(nc.sbuf_tensor(f"a1yy{k}", [128, D], F32)),
            ))
        for (m1, m2, ss, rs, cc, cc2, yy) in extra:
            s_mm1.append(m1); s_mm2.append(m2); s_ss.append(ss)
            s_rstd.append(rs); s_cc.append(cc); s_cc2.append(cc2)
            s_yy.append(yy)

        for name, src, dst in [("g1b", g1_i, s_g1b), ("be1b", be1_i, s_be1b)]:
            def f(sync, src=src, name=name):
                S.wait(sync, "SP", T.get("bcrow_used"))
                ins = sync.dma_start(
                    s_row[:], src.rearrange("(one d) -> one d", one=1))
                T[name + "_row"] = S.inc("sprow", ins, 16)
            block.sync(f)
            emit_bcast(s_row, dst, p_bc, T[name + "_row"], name)

        stage_free = [None, None]
        xuse = [None, None, None, None]
        trcp = [None, None]
        G = 4

        tokS = {}
        for g0 in range(0, NT_B, G):
            tiles = list(range(g0, g0 + G))

            for i in tiles:
                par = i % 4

                def f(sync, i=i, par=par):
                    S.wait(sync, "SP", xuse[par])
                    ins = sync.dma_start(
                        s_x[par][:], x_b[i * 128:(i + 1) * 128, :])
                    T[f"a1x{i}"] = S.inc(f"spx{par}", ins, 16)
                block.sync(f)

            def f(v, tiles=tiles):
                S.wait(v, "DVE", tokS.get("mn2"))
                for i in tiles:
                    par = i % 4
                    S.wait(v, "DVE", T[f"a1x{i}"])
                    ins = v.reduce_sum(s_mm1[par][:], s_x[par][:], axis=AX.X)
                tokS["sum"] = S.inc("dve", ins)
            block.vector(f)

            def f(a, tiles=tiles):
                S.wait(a, "ACT", tokS["sum"])
                S.wait(a, "ACT", tokS.get("c2"))
                for i in tiles:
                    par = i % 4
                    ins = a.mul(s_mm2[par][:], s_mm1[par][:], -1.0 / D)
                tokS["mn"] = S.inc("act", ins)
            block.scalar(f)

            def f(v, tiles=tiles):
                S.wait(v, "DVE", tokS["mn"])
                S.wait(v, "DVE", tokS.get("xn1"))
                for i in tiles:
                    par = i % 4
                    ins = v.tensor_scalar(
                        s_cc[par][:], s_x[par][:], s_mm2[par][:], None, ALU.add)
                tokS["c"] = S.inc("dve", ins)
            block.vector(f)

            def f(a, tiles=tiles):
                S.wait(a, "ACT", tokS["c"])
                S.wait(a, "ACT", tokS.get("xn1"))
                for i in tiles:
                    par = i % 4
                    ins = a.activation(s_cc2[par][:], s_cc[par][:], AF.Square,
                                       accum_out=s_ss[par][:])
                tokS["sq"] = S.inc("act", ins)
                S.wait(a, "ACT", tokS["sq"])
                for i in tiles:
                    par = i % 4
                    ins = a.activation(s_ss[par][:], s_ss[par][:], AF.Ln,
                                       bias=s_eps[:], scale=1.0 / D)
                t = S.inc("act", ins)
                S.wait(a, "ACT", t)
                for i in tiles:
                    par = i % 4
                    ins = a.activation(s_rstd[par][:], s_ss[par][:], AF.Exp,
                                       scale=-0.5)
                tokS["rstd"] = S.inc("act", ins)
            block.scalar(f)

            def f(v, tiles=tiles):
                S.wait(v, "DVE", tokS["rstd"])
                S.wait(v, "DVE", tokS.get("c2"))
                for i in tiles:
                    par = i % 4
                    ins = v.tensor_scalar(
                        s_yy[par][:], s_cc[par][:], s_rstd[par][:], None,
                        ALU.mult)
                tokS["y"] = S.inc("dve", ins)
                S.wait(v, "DVE", tokS["y"])
                for i in tiles:
                    par = i % 4
                    ins = v.reduce_sum(s_mm1[par][:], s_yy[par][:], axis=AX.X)
                tokS["sum2"] = S.inc("dve", ins)
            block.vector(f)

            def f(a, tiles=tiles):
                S.wait(a, "ACT", tokS["sum2"])
                for i in tiles:
                    par = i % 4
                    ins = a.mul(s_mm2[par][:], s_mm1[par][:], -1.0 / D)
                tokS["mn2"] = S.inc("act", ins)
            block.scalar(f)

            def f(v, tiles=tiles):
                S.wait(v, "DVE", tokS["mn2"])
                for i in tiles:
                    par = i % 4
                    ins = v.tensor_scalar(
                        s_cc[par][:], s_yy[par][:], s_mm2[par][:], None,
                        ALU.add)
                tokS["c2"] = S.inc("dve", ins)
            block.vector(f)

            def f(a, tiles=tiles):
                S.wait(a, "ACT", tokS["c2"])
                for i in tiles:
                    par = i % 4
                    ins = a.activation(s_cc2[par][:], s_cc[par][:], AF.Square,
                                       accum_out=s_ss[par][:])
                t = S.inc("act", ins)
                S.wait(a, "ACT", t)
                for i in tiles:
                    par = i % 4
                    ins = a.activation(s_ss[par][:], s_ss[par][:], AF.Ln,
                                       bias=s_eps[:], scale=1.0 / D)
                t = S.inc("act", ins)
                S.wait(a, "ACT", t)
                for i in tiles:
                    par = i % 4
                    ins = a.activation(s_rstd[par][:], s_ss[par][:], AF.Exp,
                                       scale=-0.5)
                tokS["rstd2"] = S.inc("act", ins)
            block.scalar(f)

            def f(v, tiles=tiles):
                S.wait(v, "DVE", tokS["rstd2"])
                S.wait(v, "DVE", T["g1b_tok"])
                S.wait(v, "DVE", T["be1b_tok"])
                S.wait(v, "DVE", T.get("a1_xn1_used"))
                for i in tiles:
                    par = i % 4
                    ins = v.scalar_tensor_tensor(
                        s_cc2[par][:], s_cc[par][:], s_rstd[par][:], s_g1b[:],
                        ALU.mult, ALU.mult)
                t = S.inc("dve", ins)
                S.wait(v, "DVE", t)
                S.wait(v, "DVE", T.get("tapguard"))
                for i in tiles:
                    par = i % 4
                    ins = v.tensor_tensor(
                        s_xn1[par][:], s_cc2[par][:], s_be1b[:], ALU.add)
                tokS["xn1"] = S.inc("dve", ins)
            block.vector(f)

            if debug_taps and g0 == 0:
                def f(sync):
                    S.wait(sync, "SP", tokS["xn1"])
                    S.wait(sync, "SP", T.get("lasttap"))
                    ins = sync.dma_start(taps["tap_xn1"], s_xn1[0][:])
                    T["tapguard"] = S.inc("sptap", ins, 16)
                    T["lasttap"] = T["tapguard"]
                block.sync(f)

            for i in tiles:
                par = i % 4
                sp = (i // NBT) % 2
                for j in range(NDT):
                    tp = j % 2

                    def f(pe, i=i, j=j, tp=tp, par=par):
                        S.wait(pe, "PE", tokS["xn1"])
                        S.wait(pe, "PE", T["init_sp"])
                        S.wait(pe, "PE", trcp[tp])
                        ins = pe.transpose(
                            p_tr[tp][:], s_xn1[par][:, j * 128:(j + 1) * 128],
                            s_idn[:])
                        T["a1t"] = S.inc("pe", ins)
                    block.tensor(f)

                    def f(a, i=i, j=j, tp=tp, sp=sp):
                        S.wait(a, "ACT", T["a1t"])
                        S.wait(a, "ACT", stage_free[sp])
                        c0 = (i % NBT) * 128
                        ins = a.copy(s_st[sp][:, j, c0:c0 + 128], p_tr[tp][:])
                        trcp[tp] = S.inc("act", ins)
                    block.scalar(f)
                xuse[par] = tokS["c"]
                T["a1_xn1_used"] = T["a1t"]

                if (i + 1) % NBT == 0:
                    blk = i // NBT

                    def f(sync, blk=blk, sp=sp):
                        S.wait(sync, "SP", trcp[0])
                        S.wait(sync, "SP", trcp[1])
                        ins = sync.dma_start(
                            spill[:, :, blk * BLK:(blk + 1) * BLK].rearrange(
                                "dt p c -> p dt c"),
                            s_st[sp][:])
                        T[f"a1sp{blk}"] = S.inc(f"spst{sp}", ins, 16)
                        stage_free[sp] = T[f"a1sp{blk}"]
                    block.sync(f)

        spill_done = T[f"a1sp{NBLK - 1}"]
        spill_done2 = T[f"a1sp{NBLK - 2}"] if NBLK >= 2 else None
        for lst in (s_mm1, s_mm2, s_ss, s_rstd, s_cc, s_cc2, s_yy):
            lst.pop(); lst.pop()

    attnT_cm = nc.sbuf_tensor("attnT", [128, NDT, T_C], BF16)
    attnT = attnT_cm.__enter__()
    with ExitStack() as _es:
        qhat = _es.enter_context(nc.sbuf_tensor("qhat", [65, 2, T_C], BF16))
        khat = _es.enter_context(nc.sbuf_tensor("khat", [65, 2, T_B], BF16))
        vbuf = _es.enter_context(nc.sbuf_tensor("vbuf", [128, NT_B, 2, 65], BF16))
        s_wq = _es.enter_context(nc.sbuf_tensor("s_wq", [128, NDT, 128], BF16))
        s_wk = _es.enter_context(nc.sbuf_tensor("s_wk", [128, NDT, 128], BF16))
        s_wvp = _es.enter_context(nc.sbuf_tensor("s_wvp", [128, NDT, 256], BF16))
        s_vst = _es.enter_context(nc.sbuf_tensor("s_vst", [128, NBT, 128], BF16))
        s_vrd0 = _es.enter_context(nc.sbuf_tensor("s_vrd0", [128, NBT, 128], BF16))
        s_vrd1 = _es.enter_context(nc.sbuf_tensor("s_vrd1", [128, NBT, 128], BF16))
        s_vrd = [s_vrd0, s_vrd1]
        s_bqb = _es.enter_context(nc.sbuf_tensor("s_bqb", [128, 1], F32))
        s_bkb = _es.enter_context(nc.sbuf_tensor("s_bkb", [128, 1], F32))
        s_xb0 = _es.enter_context(nc.sbuf_tensor("s_xb0", [128, NDT, BLK], BF16))
        s_xb1 = _es.enter_context(nc.sbuf_tensor("s_xb1", [128, NDT, BLK], BF16))
        s_ck0 = _es.enter_context(nc.sbuf_tensor("s_ck0", [128, BLK], F32))
        s_ck1 = _es.enter_context(nc.sbuf_tensor("s_ck1", [128, BLK], F32))
        s_sk0 = _es.enter_context(nc.sbuf_tensor("s_sk0", [128, BLK], F32))
        s_sk1 = _es.enter_context(nc.sbuf_tensor("s_sk1", [128, BLK], F32))
        s_bvrow = _es.enter_context(nc.sbuf_tensor("s_bvrow", [1, 256], F32))
        s_bvbb = _es.enter_context(nc.sbuf_tensor("s_bvbb", [128, 256], F32))
        s_rmaxv = _es.enter_context(nc.sbuf_tensor("s_rmaxv", [128, 1], F32))
        s_t1v = _es.enter_context(nc.sbuf_tensor("s_t1v", [128, 1], F32))
        s_t1b = _es.enter_context(nc.sbuf_tensor("s_t1b", [128, 1], BF16))
        s_ktmp = s_yy
        s_u = s_cc
        s_w = s_cc2
        s_sq2 = _es.enter_context(nc.sbuf_tensor("s_sq2", [64, 512], BF16))
        s_rmax = _es.enter_context(nc.sbuf_tensor("s_rmax", [1, 2], F32))
        s_t11 = _es.enter_context(nc.sbuf_tensor("s_t11", [1, 2], F32))
        s_bias2 = _es.enter_context(nc.sbuf_tensor("s_bias2", [1, 2], F32))
        s_lq = _es.enter_context(nc.sbuf_tensor("s_lq", [1, 512], F32))
        s_brow = _es.enter_context(nc.sbuf_tensor("s_brow", [1, 512], F32))
        s_p0 = _es.enter_context(nc.sbuf_tensor("s_p0", [128, QC], BF16))
        s_p1 = _es.enter_context(nc.sbuf_tensor("s_p1", [128, QC], BF16))
        s_tacc = s_cc2[0][0:65, 0:QC]
        s_rl = s_cc[0][0:1, 0:QC]
        s_xb = [s_xb0, s_xb1]
        s_ck = [s_ck0, s_ck1]
        s_sk = [s_sk0, s_sk1]
        s_p = [s_p0, s_p1]
        xb_use = [None, None]
        ck_use = [None, None]

        def f(v):
            S.wait(v, "DVE", spill_done)
            S.wait(v, "DVE", spill_done2)
            ins = v.memset(vbuf[:, :, :, 64:65], 1.0)
            T["init_vbuf"] = S.inc("dve", ins)
        block.vector(f)

        for hg in range(NHG):
            def f(gp, hg=hg):
                S.wait(gp, "GP", spill_done)
                S.wait(gp, "GP", T.get("tapguard"))
                S.wait(gp, "GP", T.get("hat"))
                srcs = [
                    (s_wq[:], wq_r[hg].rearrange("(dt p) c -> p dt c", p=128)),
                    (s_wk[:], wk_r[hg].rearrange("(dt p) c -> p dt c", p=128)),
                    (s_bqb[:], bq_i[hg * 128:(hg + 1) * 128].rearrange(
                        "(p one) -> p one", one=1)),
                    (s_bkb[:], bk_i[hg * 128:(hg + 1) * 128].rearrange(
                        "(p one) -> p one", one=1)),
                ]
                if hg % 2 == 0:
                    srcs.append(
                        (s_wvp[:],
                         wv_r[hg // 2].rearrange("(dt p) c -> p dt c", p=128)))
                for dst, src in srcs:
                    ins = gp.dma_start(dst, src)
                    T["wslices"] = S.inc("gpw", ins, 16)
            block.gpsimd(f)

            with ExitStack() as _es:
                p_k0 = _es.enter_context(nc.psum_tensor(f"a2_k0{hg}", [128, BLK], F32))
                p_k1 = _es.enter_context(nc.psum_tensor(f"a2_k1{hg}", [128, BLK], F32))
                p_rot = _es.enter_context(nc.psum_tensor(f"a2_rot{hg}", [128, 512], F32))
                p_vn0 = _es.enter_context(nc.psum_tensor(f"a2_vn0{hg}", [128, 256], F32))
                p_vn1 = _es.enter_context(nc.psum_tensor(f"a2_vn1{hg}", [128, 256], F32))
                p_nrm = _es.enter_context(nc.psum_tensor(f"a2_nrm{hg}", [1, 512], F32))
                p_kp = [p_k0, p_k1]
                p_vn = [p_vn0, p_vn1]
                vnT = [None, None]
                pkT = {}
                ktmpT = {}
                rotT = {}
                wopT = {}
                tuT = {}
                hatT = {}

                def f(v):
                    S.wait(v, "DVE", T.get("bias2"))
                    ins = v.memset(s_rmaxv[:], 0.0)
                    T["rmax0"] = S.inc("dve", ins)
                block.vector(f)

                if hg % 2 == 0:
                    def f(sync, hg=hg):
                        S.wait(sync, "SP", T.get("bvrow_used"))
                        ins = sync.dma_start(
                            s_bvrow[:],
                            bv_i[hg * 128:hg * 128 + 256].rearrange(
                                "(one c) -> one c", one=1))
                        T["bvrow"] = S.inc("sprow", ins, 16)
                    block.sync(f)

                    def f(pe):
                        S.wait(pe, "PE", T["bvrow"])
                        S.wait(pe, "PE", T["init_sp"])
                        S.wait(pe, "PE", T.get("wop"))
                        S.wait(pe, "PE", T.get("bvbb"))
                        for h0, hw in chunks(256, 512):
                            ins = pe.matmul(p_rot[:, 0:256], s_one1[:],
                                            s_bvrow[:], start=True, stop=True)
                        t = S.inc("pe", ins)
                        T["bvrow_used"] = t
                        T["bvmm"] = t
                    block.tensor(f)

                    def f(v):
                        S.wait(v, "DVE", T["bvmm"])
                        ins = v.tensor_copy(s_bvbb[:], p_rot[:, 0:256])
                        T["bvbb"] = S.inc("dve", ins)
                    block.vector(f)

                blocks = [("kv", b) for b in range(NBLK)] + \
                         [("q", b) for b in range(T_C // BLK if T_C >= BLK else 1)]
                for bi, (kind, b) in enumerate(blocks):
                    par = bi % 2
                    w = BLK if kind == "kv" or T_C >= BLK else T_C
                    c0 = b * BLK if kind == "kv" else b * w

                    def f(sync, par=par, c0=c0, w=w):
                        S.wait(sync, "SP", spill_done)
                        S.wait(sync, "SP", spill_done2)
                        S.wait(sync, "SP", xb_use[par])
                        S.wait(sync, "SP", ck_use[par])
                        ins = sync.dma_start(
                            s_xb[par][:, :, 0:w],
                            spill[:, :, c0:c0 + w].rearrange("dt p c -> p dt c"))
                        S.inc(f"spxb{par}", ins, 16)
                        ins = sync.dma_start(s_ck[par][:, 0:w], cosk[:, c0:c0 + w])
                        S.inc(f"spxb{par}", ins, 16)
                        ins = sync.dma_start(s_sk[par][:, 0:w], s1k[:, c0:c0 + w])
                        T["ckt"] = S.inc(f"spxb{par}", ins, 16)
                    block.sync(f)
                    xt = ct = T["ckt"]

                    wsel = s_wk if kind == "kv" else s_wq
                    bsel = s_bkb if kind == "kv" else s_bqb
                    dhat = khat if kind == "kv" else qhat

                    def f(pe, xt=xt, wsel=wsel, par=par, w=w):
                        S.wait(pe, "PE", T.get("attn_done"))
                        S.wait(pe, "PE", xt)
                        S.wait(pe, "PE", T["wslices"])
                        S.wait(pe, "PE", ktmpT.get(par))
                        for dt in range(NDT):
                            for h0, hw in chunks(w, 512):
                                ins = pe.matmul(
                                    p_kp[par][:, h0:h0 + hw],
                                    wsel[:, dt, :],
                                    s_xb[par][:, dt, h0:h0 + hw],
                                    start=(dt == 0), stop=(dt == NDT - 1))
                        pkT[par] = S.inc("pe", ins)
                    block.tensor(f)

                    def f(a, bsel=bsel, par=par, w=w):
                        S.wait(a, "ACT", pkT[par])
                        S.wait(a, "ACT", rotT.get(par))
                        S.wait(a, "ACT", T.get("nrm_red"))
                        S.wait(a, "ACT", tuT.get(par))
                        ins = a.activation(
                            s_ktmp[par][:, 0:w], p_kp[par][:, 0:w], AF.Identity,
                            bias=bsel[:])
                        ktmpT[par] = S.inc("act", ins)
                    block.scalar(f)

                    if kind == "kv" and hg % 2 == 0:
                        for j in range(NBT):
                            vp = j % 2

                            def f(pe, j=j, vp=vp, xt=xt, par=par):
                                S.wait(pe, "PE", xt)
                                S.wait(pe, "PE", T["wslices"])
                                S.wait(pe, "PE", vnT[vp])
                                for dt in range(NDT):
                                    ins = pe.matmul(
                                        p_vn[vp][:],
                                        s_xb[par][:, dt,
                                                  j * 128:(j + 1) * 128],
                                        s_wvp[:, dt, :],
                                        start=(dt == 0), stop=(dt == NDT - 1))
                                T["pv"] = S.inc("pe", ins)
                            block.tensor(f)

                            def f(v, j=j, vp=vp, b=b):
                                S.wait(v, "DVE", T["pv"])
                                S.wait(v, "DVE", T["bvbb"])
                                S.wait(v, "DVE", T.get("tapguard2"))
                                S.wait(v, "DVE", T["init_vbuf"])
                                S.wait(v, "DVE", T.get(f"vspw{b}"))
                                tt = b * NBT + j
                                ins = v.tensor_tensor(
                                    vbuf[:, tt, :, 0:64],
                                    p_vn[vp][:, 0:128].rearrange(
                                        "p (h c) -> p h c", h=2),
                                    s_bvbb[:, 0:128].rearrange(
                                        "p (h c) -> p h c", h=2),
                                    ALU.add)
                                t = S.inc("dve", ins)
                                ins = v.tensor_tensor(
                                    s_vst[:, j, :],
                                    p_vn[vp][:, 128:256],
                                    s_bvbb[:, 128:256], ALU.add)
                                vnT[vp] = S.inc("dve", ins)
                                T["vbuf_wr"] = vnT[vp]
                                T["vst_wr"] = vnT[vp]
                            block.vector(f)

                        def f(sync, b=b):
                            S.wait(sync, "SP", T["vst_wr"])
                            ins = sync.dma_start(
                                vsp[:, b * NBT:(b + 1) * NBT, :], s_vst[:])
                            T[f"vspw{b}"] = S.inc("spvs", ins, 16)
                        block.sync(f)

                    if kind == "kv" and hg % 2 == 1:
                        vrp = b % 2

                        def f(sync, b=b, vrp=vrp):
                            S.wait(sync, "SP", T[f"vspw{b}"])
                            S.wait(sync, "SP", T.get(f"vrd_used{vrp}"))
                            ins = sync.dma_start(
                                s_vrd[vrp][:], vsp[:, b * NBT:(b + 1) * NBT, :])
                            T["vrd"] = S.inc("spvr", ins, 16)
                        block.sync(f)

                        def f(v, b=b, vrp=vrp):
                            S.wait(v, "DVE", T["vrd"])
                            S.wait(v, "DVE", T.get("tapguard2"))
                            S.wait(v, "DVE", T["init_vbuf"])
                            for j in range(NBT):
                                tt = b * NBT + j
                                ins = v.tensor_copy(
                                    vbuf[:, tt, :, 0:64],
                                    s_vrd[vrp][:, j, :].rearrange(
                                        "p (h c) -> p h c", h=2))
                            T[f"vrd_used{vrp}"] = S.inc("dve", ins)
                            T["vbuf_wr"] = T[f"vrd_used{vrp}"]
                        block.vector(f)

                    for h0, hw in chunks(w, 512):
                        def f(pe, h0=h0, hw=hw, par=par):
                            S.wait(pe, "PE", ktmpT[par])
                            S.wait(pe, "PE", T.get("wop"))
                            ins = pe.matmul(
                                p_rot[:, 0:hw], s_perm[:],
                                s_ktmp[par][:, h0:h0 + hw],
                                start=True, stop=True)
                            rotT[par] = S.inc("pe", ins)
                            T["rot_roll"] = rotT[par]
                        block.tensor(f)

                        def f(v, h0=h0, hw=hw, par=par, ct=ct):
                            S.wait(v, "DVE", rotT[par])
                            S.wait(v, "DVE", ct)
                            ins = v.tensor_tensor(
                                s_w[par][:, h0:h0 + hw], p_rot[:, 0:hw],
                                s_sk[par][:, h0:h0 + hw], ALU.mult)
                            T["wop"] = S.inc("dve", ins)
                            wopT[par] = T["wop"]
                        block.vector(f)

                    def f(v, par=par, ct=ct, dhat=dhat, c0=c0, w=w, kind=kind):
                        S.wait(v, "DVE", ktmpT[par])
                        S.wait(v, "DVE", T.get("tapguard2"))
                        S.wait(v, "DVE", ct)
                        ins = v.tensor_tensor(
                            s_u[par][:, 0:w], s_ktmp[par][:, 0:w],
                            s_ck[par][:, 0:w], ALU.mult)
                        tuT[par] = S.inc("dve", ins)
                        S.wait(v, "DVE", tuT[par])
                        S.wait(v, "DVE", wopT[par])
                        ins = v.tensor_tensor(
                            dhat[0:64, 0, c0:c0 + w],
                            s_u[par][0:64, 0:w], s_w[par][0:64, 0:w], ALU.add)
                        S.inc("dve", ins)
                        ins = v.tensor_tensor(
                            dhat[0:64, 1, c0:c0 + w],
                            s_u[par][64:128, 0:w], s_w[par][64:128, 0:w],
                            ALU.add)
                        hatT[par] = S.inc("dve", ins)
                        T["hat"] = hatT[par]
                        if kind == "kv":
                            S.wait(v, "DVE", T.get("nrm_red"))
                            ins = v.reduce_max(
                                s_t1v[:], s_ktmp[par][:, 0:w], axis=AX.X,
                                apply_absolute_value=True)
                            t = S.inc("dve", ins)
                            S.wait(v, "DVE", t)
                            ins = v.tensor_tensor(
                                s_rmaxv[:], s_rmaxv[:], s_t1v[:], ALU.max)
                            T["nrm_red"] = S.inc("dve", ins)
                    block.vector(f)
                    if kind == "q" or hg % 2 == 1:
                        xb_use[par] = pkT[par]
                    else:
                        xb_use[par] = T["pv"]
                    ck_use[par] = hatT[par]

                def f(v):
                    S.wait(v, "DVE", T["nrm_red"])
                    ins = v.tensor_tensor(
                        s_t1b[:], s_rmaxv[:], s_rmaxv[:], ALU.mult)
                    T["rmx_bf"] = S.inc("dve", ins)
                block.vector(f)

                def f(pe):
                    S.wait(pe, "PE", T["rmx_bf"])
                    S.wait(pe, "PE", T.get("q2mm"))
                    for h01 in range(2):
                        ins = pe.matmul(
                            p_nrm[:, h01:h01 + 1],
                            s_bones[:, h01:h01 + 1], s_t1b[:],
                            start=True, stop=True)
                    T["mk2"] = S.inc("pe", ins)
                block.tensor(f)

                def f(a):
                    S.wait(a, "ACT", T["mk2"])
                    ins = a.activation(s_t11[:], p_nrm[:, 0:2], AF.Ln)
                    t = S.inc("act", ins)
                    S.wait(a, "ACT", t)
                    ins = a.activation(
                        s_bias2[:], s_t11[:], AF.Identity, scale=0.5,
                        bias=s_ln102[:])
                    T["bias2"] = S.inc("act", ins)
                block.scalar(f)

                for h01 in range(2):
                    for c0, cw in chunks(T_C, 512):
                        def f(v, h01=h01, c0=c0, cw=cw):
                            S.wait(v, "DVE", T["hat"])
                            S.wait(v, "DVE", T.get("q2mm"))
                            ins = v.tensor_tensor(
                                s_sq2[:, 0:cw],
                                qhat[0:64, h01, c0:c0 + cw],
                                qhat[0:64, h01, c0:c0 + cw], ALU.mult)
                            T["sq2"] = S.inc("dve", ins)
                        block.vector(f)

                        def f(pe, c0=c0, cw=cw):
                            S.wait(pe, "PE", T["sq2"])
                            S.wait(pe, "PE", T["bias2"])
                            S.wait(pe, "PE", T.get("lnq"))
                            for h0, hw in chunks(cw, 512):
                                ins = pe.matmul(
                                    p_nrm[:, h0:h0 + hw], s_ones64[:],
                                    s_sq2[:, h0:h0 + hw],
                                    start=True, stop=True)
                            T["q2mm"] = S.inc("pe", ins)
                        block.tensor(f)

                        def f(a, h01=h01, c0=c0, cw=cw):
                            S.wait(a, "ACT", T["q2mm"])
                            S.wait(a, "ACT", T["bias2"])
                            ins = a.activation(s_lq[:, 0:cw], p_nrm[:, 0:cw], AF.Ln)
                            T["lnq"] = S.inc("act", ins)
                            S.wait(a, "ACT", T["lnq"])
                            ins = a.activation(
                                s_brow[:, 0:cw], s_lq[:, 0:cw], AF.Exp,
                                scale=0.5, bias=s_bias2[:, h01:h01 + 1])
                            t = S.inc("act", ins)
                            S.wait(a, "ACT", t)
                            ins = a.mul(
                                qhat[64:65, h01, c0:c0 + cw], s_brow[:, 0:cw],
                                -1.0)
                            T["brow"] = S.inc("act", ins)
                        block.scalar(f)

                def f(v):
                    S.wait(v, "DVE", T["hat"])
                    S.wait(v, "DVE", T["brow"])
                    ins = v.memset(khat[64:65, :, :], 1.0)
                    T["khat_ones"] = S.inc("dve", ins)
                block.vector(f)

                if debug_taps and hg == 0:
                    def f(sync):
                        S.wait(sync, "SP", T["khat_ones"])
                        S.wait(sync, "SP", T["vbuf_wr"])
                        for name, srcap in [
                            ("tap_khat", khat[:, 0, :]),
                            ("tap_qhat", qhat[:, 0, :]),
                            ("tap_vbuf", vbuf[:, 0, 0, :]),
                        ]:
                            S.wait(sync, "SP", T.get("lasttap"))
                            ins = sync.dma_start(taps[name], srcap)
                            T["tapguard2"] = S.inc("sptap", ins, 16)
                            T["lasttap"] = T["tapguard2"]
                    block.sync(f)

            with ExitStack() as _es:
                p_s0 = _es.enter_context(nc.psum_tensor(f"b_s0{hg}", [128, QC], F32))
                p_s1 = _es.enter_context(nc.psum_tensor(f"b_s1{hg}", [128, QC], F32))
                p_acc = _es.enter_context(nc.psum_tensor(f"b_acc{hg}", [65, QC], F32))
                p_bl = _es.enter_context(nc.psum_tensor(f"b_bl{hg}", [64, QC], F32))
                p_s = [p_s0, p_s1]
                smm_par = {}
                exp_par = {}
                pv_par = {}
                pending = [None]

                def emit_s(h01, qc, kt):
                    par = kt % 2

                    def f(pe, h01=h01, kt=kt, par=par, qc=qc):
                        S.wait(pe, "PE", T["khat_ones"])
                        S.wait(pe, "PE", exp_par.get(par))
                        for h0, hw in chunks(QC, 512):
                            ins = pe.matmul(
                                p_s[par][:, h0:h0 + hw],
                                khat[0:65, h01, kt * 128:(kt + 1) * 128],
                                qhat[0:65, h01,
                                     qc * QC + h0:qc * QC + h0 + hw],
                                start=True, stop=True)
                        smm_par[par] = S.inc("pe", ins)
                    block.tensor(f)

                for h01 in range(2):
                    for qc in range(NQC):
                        emit_s(h01, qc, 0)
                        for kt in range(NT_B):
                            par = kt % 2
                            if kt + 1 < NT_B:
                                emit_s(h01, qc, kt + 1)
                            if kt == 4 and pending[0] is not None:
                                pending[0]()
                                pending[0] = None

                            def f(a, kt=kt, par=par):
                                S.wait(a, "ACT", smm_par[par])
                                S.wait(a, "ACT", pv_par.get(par))
                                ins = a.activation(
                                    s_p[par][:], p_s[par][:], AF.Exp,
                                    scale=0.125)
                                exp_par[par] = S.inc("act", ins)
                            block.scalar(f)

                            def f(pe, h01=h01, kt=kt, par=par):
                                S.wait(pe, "PE", exp_par[par])
                                if kt == 0:
                                    S.wait(pe, "PE", T.get("vbuf_wr"))
                                    S.wait(pe, "PE", T.get("tacc"))
                                for h0, hw in chunks(QC, 512):
                                    ins = pe.matmul(
                                        p_acc[:, h0:h0 + hw],
                                        vbuf[:, kt, h01, :],
                                        s_p[par][:, h0:h0 + hw],
                                        start=(kt == 0), stop=(kt == NT_B - 1))
                                pv_par[par] = S.inc("pe", ins)
                                T["pvlast"] = pv_par[par]
                            block.tensor(f)

                        def f(a):
                            S.wait(a, "ACT", T["pvlast"])
                            S.wait(a, "ACT", T["hat"])
                            S.wait(a, "ACT", T.get("attnT_wr"))
                            ins = a.copy(s_tacc, p_acc[:])
                            T["tacc"] = S.inc("act", ins)
                        block.scalar(f)

                        def evac(h01=h01, hg=hg, qc=qc, tacc_tok=None):
                            def f(v):
                                S.wait(v, "DVE", T["tacc"])
                                S.wait(v, "DVE", T["hat"])
                                S.wait(v, "DVE", T.get("bl"))
                                ins = v.reciprocal(s_rl, s_cc2[0][64:65, 0:QC])
                                T["rl"] = S.inc("dve", ins)
                            block.vector(f)

                            def f(pe):
                                S.wait(pe, "PE", T["rl"])
                                S.wait(pe, "PE", T.get("attnT_wr"))
                                for h0, hw in chunks(QC, 512):
                                    ins = pe.matmul(
                                        p_bl[:, h0:h0 + hw], s_one1[:, 0:64],
                                        s_cc[0][0:1, h0:h0 + hw],
                                        start=True, stop=True)
                                T["bl"] = S.inc("pe", ins)
                            block.tensor(f)

                            def f(v):
                                S.wait(v, "DVE", T["bl"])
                                ins = v.tensor_tensor(
                                    attnT[h01 * 64:(h01 + 1) * 64, hg,
                                          qc * QC:(qc + 1) * QC],
                                    s_cc2[0][0:64, 0:QC], p_bl[:], ALU.mult)
                                T["attnT_wr"] = S.inc("dve", ins)
                            block.vector(f)
                        pending[0] = evac

                if pending[0] is not None:
                    pending[0]()
                    pending[0] = None

            T["attn_done"] = T["attnT_wr"]

        attnT_done = T["attn_done"]

    if debug_taps:
        def f(sync):
            S.wait(sync, "SP", attnT_done)
            S.wait(sync, "SP", T.get("lasttap"))
            ins = sync.dma_start(taps["tap_attnT"], attnT[:, 0, :])
            T["tapguard5"] = S.inc("sptap", ins, 16)
            T["lasttap"] = T["tapguard5"]
        block.sync(f)

    xz_cm = nc.sbuf_tensor("xz", [128, NT_C, D], F32, side="right")
    xz = xz_cm.__enter__()
    xn2T_cm = nc.sbuf_tensor("xn2T", [128, NDT, T_C], BF16, side="right")
    xn2T = xn2T_cm.__enter__()
    if True:
        with ExitStack() as _es:
            s_g2b = _es.enter_context(nc.sbuf_tensor("c_g2b", [128, D], F32))
            s_be2b = _es.enter_context(nc.sbuf_tensor("c_be2b", [128, D], F32))
            s_bob = _es.enter_context(nc.sbuf_tensor("c_bob", [128, D], F32))
            s_crow = _es.enter_context(nc.sbuf_tensor("c_row", [1, D], F32))
            c_x0 = _es.enter_context(nc.sbuf_tensor("c_x0", [128, D], F32))
            c_x1 = _es.enter_context(nc.sbuf_tensor("c_x1", [128, D], F32))
            s_z = _es.enter_context(nc.sbuf_tensor("c_z", [128, D], F32))
            s_xn2 = _es.enter_context(nc.sbuf_tensor("c_xn2", [128, D], BF16))
            s_x = [c_x0, c_x1]
            s_wo = _es.enter_context(nc.sbuf_tensor("c_wo", [128, NDT, D], BF16))
            p_o = _es.enter_context(nc.psum_tensor("c_o", [128, D], F32))
            p_bc2 = _es.enter_context(nc.psum_tensor("c_bc", [128, 512], F32))
            p_tr0 = _es.enter_context(nc.psum_tensor("c_tr0", [128, 128], BF16))
            p_tr1 = _es.enter_context(nc.psum_tensor("c_tr1", [128, 128], BF16))
            p_tr = [p_tr0, p_tr1]

            def f(gp):
                S.wait(gp, "GP", attnT_done)
                ins = gp.dma_start(
                    s_wo[:], wo_i.rearrange("(dt p) c -> p dt c", p=128))
                T["wo"] = S.inc("gpw", ins, 16)
            block.gpsimd(f)

            for name, src, dst in [
                ("g2b", g2_i, s_g2b), ("be2b", be2_i, s_be2b),
                ("bob", bo_i, s_bob),
            ]:
                def f(sync, src=src, name=name):
                    S.wait(sync, "SP", attnT_done)
                    S.wait(sync, "SP", T.get("bcrow_used"))
                    ins = sync.dma_start(
                        s_crow[:], src.rearrange("(one d) -> one d", one=1))
                    T[name + "_row"] = S.inc("sprow", ins, 16)
                block.sync(f)
                emit_bcast(s_crow, dst, p_bc2, T[name + "_row"], name)

            xuse = [None, None]
            trcp = [None, None]
            for tt in range(NT_C):
                par = tt % 2

                def f(sync, tt=tt, par=par):
                    S.wait(sync, "SP", attnT_done)
                    S.wait(sync, "SP", xuse[par])
                    ins = sync.dma_start(
                        s_x[par][:], x_b[tt * 128:(tt + 1) * 128, :])
                    T["cx"] = S.inc(f"spx{par}", ins, 16)
                block.sync(f)
                cx = T["cx"]

                def f(pe, tt=tt):
                    S.wait(pe, "PE", attnT_done)
                    S.wait(pe, "PE", T["wo"])
                    S.wait(pe, "PE", T.get("z1"))
                    for dt in range(NDT):
                        for h0, hw in chunks(D, 512):
                            ins = pe.matmul(
                                p_o[:, h0:h0 + hw],
                                attnT[:, dt, tt * 128:(tt + 1) * 128],
                                s_wo[:, dt, h0:h0 + hw],
                                start=(dt == 0), stop=(dt == NDT - 1))
                    T["po"] = S.inc("pe", ins)
                block.tensor(f)

                t_r, _ = emit_ln(s_x[par][:], [cx], f"cy{tt}", par)

                def f(v, tt=tt, par=par):
                    S.wait(v, "DVE", t_r)
                    ins = v.tensor_scalar(
                        s_yy[par][:], s_cc[par][:], s_rstd[par][:], None,
                        ALU.mult)
                    t = S.inc("dve", ins)
                    S.wait(v, "DVE", t)
                    S.wait(v, "DVE", T["po"])
                    S.wait(v, "DVE", T["bob_tok"])
                    ins = v.tensor_tensor(s_z[:], p_o[:], s_yy[par][:], ALU.add)
                    T["z1"] = S.inc("dve", ins)
                    S.wait(v, "DVE", T["z1"])
                    ins = v.tensor_tensor(s_z[:], s_z[:], s_bob[:], ALU.add)
                    T["z2"] = S.inc("dve", ins)
                    S.wait(v, "DVE", T["z2"])
                    ins = v.tensor_tensor(
                        xz[:, tt, :], s_x[par][:], s_z[:], ALU.add)
                    T[f"xz{tt}"] = S.inc("dve", ins)
                    xuse[par] = T[f"xz{tt}"]
                    T[f"ln_guard{par}"] = T[f"xz{tt}"]
                block.vector(f)

                t_r2, _ = emit_ln(s_z[:], [T["z2"]], f"cn{tt}", par)

                def f(v, tt=tt, par=par):
                    S.wait(v, "DVE", t_r2)
                    S.wait(v, "DVE", T["g2b_tok"])
                    S.wait(v, "DVE", T["be2b_tok"])
                    S.wait(v, "DVE", T.get("xn2T_cp"))
                    ins = v.scalar_tensor_tensor(
                        s_cc2[par][:], s_cc[par][:], s_rstd[par][:], s_g2b[:],
                        ALU.mult, ALU.mult)
                    t = S.inc("dve", ins)
                    S.wait(v, "DVE", t)
                    ins = v.tensor_tensor(s_xn2[:], s_cc2[par][:], s_be2b[:],
                                          ALU.add)
                    T["xn2"] = S.inc("dve", ins)
                    T[f"ln_guard{par}"] = T["xn2"]
                block.vector(f)

                for j in range(NDT):
                    tp = j % 2

                    def f(pe, tt=tt, j=j, tp=tp):
                        S.wait(pe, "PE", T["xn2"])
                        S.wait(pe, "PE", trcp[tp])
                        ins = pe.transpose(
                            p_tr[tp][:], s_xn2[:, j * 128:(j + 1) * 128],
                            s_idn[:])
                        T["ctr"] = S.inc("pe", ins)
                    block.tensor(f)

                    def f(a, tt=tt, j=j, tp=tp):
                        S.wait(a, "ACT", T["ctr"])
                        ins = a.copy(
                            xn2T[:, j, tt * 128:(tt + 1) * 128], p_tr[tp][:])
                        trcp[tp] = S.inc("act", ins)
                        T["xn2T_cp"] = trcp[tp]
                    block.scalar(f)
            T["xn2T_done"] = T["xn2T_cp"]

            if debug_taps:
                def f(sync):
                    S.wait(sync, "SP", T["xn2T_done"])
                    S.wait(sync, "SP", T["xz0"])
                    S.wait(sync, "SP", T.get("lasttap"))
                    ins = sync.dma_start(taps["tap_xz"], xz[:, 0, :])
                    T["lasttap"] = S.inc("sptap", ins, 16)
                    S.wait(sync, "SP", T.get("lasttap"))
                    ins = sync.dma_start(taps["tap_xn2T"], xn2T[:, 0, :])
                    T["tapguard3"] = S.inc("sptap", ins, 16)
                    T["lasttap"] = T["tapguard3"]
                block.sync(f)

        attnT_cm.__exit__(None, None, None)

        with ExitStack() as _es:
            s_b2b = _es.enter_context(nc.sbuf_tensor("c_b2b", [128, D], F32))
            s_crow2 = _es.enter_context(nc.sbuf_tensor("c_row2", [1, D], F32))

            s_w1a = _es.enter_context(nc.sbuf_tensor("c_w1a", [128, NDT, FC], BF16))
            s_w1b = _es.enter_context(nc.sbuf_tensor("c_w1b", [128, NDT, FC], BF16))
            s_w2a = _es.enter_context(nc.sbuf_tensor("c_w2a", [128, NFT, D], BF16))
            s_w2b = _es.enter_context(nc.sbuf_tensor("c_w2b", [128, NFT, D], BF16))
            s_b1 = _es.enter_context(nc.sbuf_tensor("c_b1", [128, NFT], F32))
            s_b1x = _es.enter_context(nc.sbuf_tensor("c_b1x", [128, NFT], F32))
            s_gel = _es.enter_context(nc.sbuf_tensor("c_gel", [128, NFT, T_C], BF16))
            p_a0 = _es.enter_context(nc.psum_tensor("c_a0", [128, 1024], F32))
            p_a1 = _es.enter_context(nc.psum_tensor("c_a1", [128, 1024], F32))
            p_b0 = _es.enter_context(nc.psum_tensor("c_pb0", [128, D], F32))
            p_b1 = _es.enter_context(nc.psum_tensor("c_pb1", [128, D], F32))
            p_ap = [p_a0, p_a1]
            p_bp = [p_b0, p_b1]

            def f(sync):
                S.wait(sync, "SP", T["xn2T_done"])
                S.wait(sync, "SP", T.get("tapguard5"))
                S.wait(sync, "SP", T.get("bcrow_used"))
                ins = sync.dma_start(
                    s_crow2[:], b2_i.rearrange("(one d) -> one d", one=1))
                T["b2b_row"] = S.inc("sprow", ins, 16)
            block.sync(f)
            emit_bcast(s_crow2, s_b2b, p_a0, T["b2b_row"], "b2b")
            s_w1 = [s_w1a, s_w1b]
            s_w2 = [s_w2a, s_w2b]
            s_b1p = [s_b1, s_b1x]
            NFC_MLP = DFF // FC
            w_toks = {}
            w_used = {}

            def mlp_prefetch(fc):
                par = fc % 2

                def f(gp, fc=fc, par=par):
                    S.wait(gp, "GP", w_used.get(fc - 2))
                    if fc < 2:
                        S.wait(gp, "GP", T["xn2T_done"])
                    ins = gp.dma_start(
                        s_w1[par][:],
                        w1_r[fc].rearrange("(dt p) c -> p dt c", p=128))
                    S.inc(f"gpw1{par}", ins, 16)
                    ins = gp.dma_start(
                        s_w2[par][:],
                        w2_i[fc * FC:(fc + 1) * FC, :].rearrange(
                            "(ft p) c -> p ft c", p=128))
                    S.inc(f"gpw1{par}", ins, 16)
                    ins = gp.dma_start(
                        s_b1p[par][:, 0:NFT], b1_i[fc])
                    w_toks[fc] = S.inc(f"gpw1{par}", ins, 16)
                block.gpsimd(f)

            mlp_prefetch(0)
            if NFC_MLP > 1:
                mlp_prefetch(1)

            def f(a):
                load_act_set(a, 10)
            block.scalar(f)

            for fc in range(NFC_MLP):
                par = fc % 2
                for ft in range(NFT):
                    for h0, hw in chunks(T_C, 1024):
                        ap = (ft * (T_C // 1024) + h0 // 1024) % 2

                        def f(pe, fc=fc, ft=ft, h0=h0, hw=hw, par=par, ap=ap):
                            S.wait(pe, "PE", T["xn2T_done"])
                            S.wait(pe, "PE", w_toks[fc])
                            S.wait(pe, "PE", T.get("b2b_tok"))
                            S.wait(pe, "PE", T.get(f"gel{ap}"))
                            for dt in range(NDT):
                                for q0, qw in chunks(hw, 512):
                                    ins = pe.matmul(
                                        p_ap[ap][:, q0:q0 + qw],
                                        s_w1[par][:, dt, ft * 128:(ft + 1) * 128],
                                        xn2T[:, dt, h0 + q0:h0 + q0 + qw],
                                        start=(dt == 0), stop=(dt == NDT - 1))
                            T["pa"] = S.inc("pe", ins)
                        block.tensor(f)

                        def f(a, fc=fc, ft=ft, h0=h0, hw=hw, par=par, ap=ap):
                            S.wait(a, "ACT", T["pa"])
                            ins = a.activation(
                                s_gel[:, ft, h0:h0 + hw], p_ap[ap][:, 0:hw],
                                AF.Gelu, bias=s_b1p[par][:, ft:ft + 1])
                            T[f"gel{ap}"] = S.inc("act", ins)
                            T["gel"] = T[f"gel{ap}"]
                        block.scalar(f)

                for tt in range(NT_C):
                    bp = tt % 2

                    def f(pe, fc=fc, tt=tt, par=par, bp=bp):
                        S.wait(pe, "PE", T["gel"])
                        S.wait(pe, "PE", T.get(f"xzup{bp}"))
                        for ft in range(NFT):
                            for h0, hw in chunks(D, 512):
                                ins = pe.matmul(
                                    p_bp[bp][:, h0:h0 + hw],
                                    s_gel[:, ft, tt * 128:(tt + 1) * 128],
                                    s_w2[par][:, ft, h0:h0 + hw],
                                    start=(ft == 0), stop=(ft == NFT - 1))
                        T["pb"] = S.inc("pe", ins)
                    block.tensor(f)

                    def f(v, fc=fc, tt=tt, bp=bp):
                        S.wait(v, "DVE", T["pb"])
                        S.wait(v, "DVE", T.get("tapguard3"))
                        S.wait(v, "DVE", T[f"xz{tt}"])
                        ins = v.tensor_tensor(
                            xz[:, tt, :], xz[:, tt, :], p_bp[bp][:], ALU.add)
                        T[f"xz{tt}"] = S.inc("dve", ins)
                        if fc == NFC_MLP - 1:
                            S.wait(v, "DVE", T[f"xz{tt}"])
                            S.wait(v, "DVE", T["b2b_tok"])
                            ins = v.tensor_tensor(
                                xz[:, tt, :], xz[:, tt, :], s_b2b[:], ALU.add)
                            T[f"xz{tt}"] = S.inc("dve", ins)
                        T[f"xzup{bp}"] = T[f"xz{tt}"]
                        T["xzup"] = T[f"xz{tt}"]
                    block.vector(f)
                w_used[fc] = T["xzup"]
                if fc + 2 < NFC_MLP:
                    mlp_prefetch(fc + 2)

        xn2T_cm.__exit__(None, None, None)

        with ExitStack() as _es:
            xfT = _es.enter_context(nc.sbuf_tensor("d_xfT", [128, NDT, T_C], BF16))
            s_cw = _es.enter_context(nc.sbuf_tensor("d_cw", [128, NT_C, NE], F32))
            s_xf = _es.enter_context(nc.sbuf_tensor("d_xf", [128, D], BF16))
            s_rwt = _es.enter_context(nc.sbuf_tensor("d_rwt", [128, NDT, NE], BF16))
            s_rr = _es.enter_context(nc.sbuf_tensor("d_rr", [128, NT_C, NE], F32))
            s_er = _es.enter_context(nc.sbuf_tensor("d_er", [128, NT_C, NE], F32))
            s_mk1 = _es.enter_context(nc.sbuf_tensor("d_msk1", [128, NT_C, NE], F32))
            s_mk2 = _es.enter_context(nc.sbuf_tensor("d_msk2", [128, NT_C, NE], F32))
            s_m1 = _es.enter_context(nc.sbuf_tensor("d_m1", [128, NT_C], F32))
            s_m2 = _es.enter_context(nc.sbuf_tensor("d_m2", [128, NT_C], F32))
            s_den = _es.enter_context(nc.sbuf_tensor("d_den", [128, NT_C], F32))
            s_rden = _es.enter_context(nc.sbuf_tensor("d_rden", [128, NT_C], F32))
            s_negbig = _es.enter_context(nc.sbuf_tensor("d_negbig", [128, 1], F32))
            def f(gp):
                S.wait(gp, "GP", w_used.get(NFC_MLP - 1))
                ins = gp.dma_start(
                    s_rwt[:], rwt_i.rearrange("(dt p) e -> p dt e", p=128))
                T["rwt"] = S.inc("gpw", ins, 16)
            block.gpsimd(f)

            def f(v):
                S.wait(v, "DVE", w_used.get(NFC_MLP - 1))
                ins = v.memset(s_negbig[:], -1e30)
                T["negbig"] = S.inc("dve", ins)
            block.vector(f)

            def f(a):
                load_act_set(a, 6)
            block.scalar(f)

            with ExitStack() as _es:
                p_tr0 = _es.enter_context(nc.psum_tensor("d_tr0", [128, 128], BF16))
                p_tr1 = _es.enter_context(nc.psum_tensor("d_tr1", [128, 128], BF16))
                p_lg = _es.enter_context(nc.psum_tensor("d_lg", [128, NE], F32))
                p_tr = [p_tr0, p_tr1]
                trcp = [None, None]
                for tt in range(NT_C):
                    par = tt % 2
                    t_r, _ = emit_ln(xz[:, tt, :], [T[f"xz{tt}"]], f"dl{tt}", par)

                    def f(v, tt=tt, par=par):
                        S.wait(v, "DVE", t_r)
                        S.wait(v, "DVE", T.get("xfT_cp"))
                        ins = v.tensor_scalar(
                            s_xf[:], s_cc[par][:], s_rstd[par][:], None,
                            ALU.mult)
                        T["xf"] = S.inc("dve", ins)
                        T[f"ln_guard{par}"] = T["xf"]
                    block.vector(f)

                    for j in range(NDT):
                        tp = j % 2

                        def f(pe, tt=tt, j=j, tp=tp):
                            S.wait(pe, "PE", T["xf"])
                            S.wait(pe, "PE", trcp[tp])
                            ins = pe.transpose(
                                p_tr[tp][:], s_xf[:, j * 128:(j + 1) * 128],
                                s_idn[:])
                            T["dtr"] = S.inc("pe", ins)
                        block.tensor(f)

                        def f(a, tt=tt, j=j, tp=tp):
                            S.wait(a, "ACT", T["dtr"])
                            ins = a.copy(
                                xfT[:, j, tt * 128:(tt + 1) * 128], p_tr[tp][:])
                            trcp[tp] = S.inc("act", ins)
                            T["xfT_cp"] = trcp[tp]
                        block.scalar(f)

                    def f(pe, tt=tt):
                        S.wait(pe, "PE", T["xfT_cp"])
                        S.wait(pe, "PE", T["rwt"])
                        S.wait(pe, "PE", T.get("rr_ev"))
                        for dt in range(NDT):
                            ins = pe.matmul(
                                p_lg[:], xfT[:, dt, tt * 128:(tt + 1) * 128],
                                s_rwt[:, dt, :],
                                start=(dt == 0), stop=(dt == NDT - 1))
                        T["lg"] = S.inc("pe", ins)
                    block.tensor(f)

                    def f(a, tt=tt):
                        S.wait(a, "ACT", T["lg"])
                        ins = a.copy(s_rr[:, tt, :], p_lg[:])
                        T["rr_ev"] = S.inc("act", ins)
                        T[f"rr{tt}"] = T["rr_ev"]
                    block.scalar(f)

                    def f(sync, tt=tt):
                        S.wait(sync, "SP", T[f"xz{tt}"])
                        ins = sync.dma_start(
                            h1sp[tt * 128:(tt + 1) * 128, :], xz[:, tt, :])
                        T["h1sp_all"] = S.inc("sph1s", ins, 16)
                    block.sync(f)

                if debug_taps:
                    def f(sync):
                        S.wait(sync, "SP", T["xz0"])
                        S.wait(sync, "SP", T.get("lasttap"))
                        ins = sync.dma_start(taps["tap_h1"], xz[:, 0, :])
                        T["tapguard4"] = S.inc("sptap", ins, 16)
                        T["lasttap"] = T["tapguard4"]
                    block.sync(f)

            def f(v):
                for tt in range(NT_C):
                    S.wait(v, "DVE", T[f"rr{tt}"])
                ins = v.reduce_max(
                    s_m1[:].rearrange("p (t one) -> p t one", one=1),
                    s_rr[:], axis=AX.X)
                T["rmx"] = S.inc("dve", ins)
            block.vector(f)

            def f(a):
                S.wait(a, "ACT", T["rmx"])
                for tt in range(NT_C):
                    ins = a.mul(s_m2[:, tt:tt + 1], s_m1[:, tt:tt + 1], -1.0)
                    t = S.inc("act", ins)
                    S.wait(a, "ACT", t)
                    ins = a.activation(
                        s_er[:, tt, :], s_rr[:, tt, :], AF.Exp,
                        bias=s_m2[:, tt:tt + 1])
                    t = S.inc("act", ins)
                T["er"] = t
            block.scalar(f)

            def f(v):
                S.wait(v, "DVE", T["er"])
                ins = v.reduce_max(
                    s_m1[:].rearrange("p (t one) -> p t one", one=1),
                    s_er[:], axis=AX.X)
                t = S.inc("dve", ins)
                S.wait(v, "DVE", t)
                for tt in range(NT_C):
                    ins = v.tensor_scalar(
                        s_mk1[:, tt, :], s_er[:, tt, :], s_m1[:, tt:tt + 1],
                        None, ALU.is_ge)
                t = S.inc("dve", ins)
                S.wait(v, "DVE", t)
                S.wait(v, "DVE", T["negbig"])
                ins = v.scalar_tensor_tensor(
                    s_mk2[:], s_mk1[:], s_negbig[:], s_er[:], ALU.mult, ALU.add)
                t = S.inc("dve", ins)
                S.wait(v, "DVE", t)
                ins = v.reduce_max(
                    s_m2[:].rearrange("p (t one) -> p t one", one=1),
                    s_mk2[:], axis=AX.X)
                t = S.inc("dve", ins)
                S.wait(v, "DVE", t)
                for tt in range(NT_C):
                    ins = v.tensor_scalar(
                        s_mk2[:, tt, :], s_mk2[:, tt, :], s_m2[:, tt:tt + 1],
                        None, ALU.is_ge)
                t = S.inc("dve", ins)
                S.wait(v, "DVE", t)
                ins = v.tensor_tensor(s_den[:], s_m1[:], s_m2[:], ALU.add)
                t = S.inc("dve", ins)
                S.wait(v, "DVE", t)
                ins = v.reciprocal(s_rden[:], s_den[:])
                t = S.inc("dve", ins)
                S.wait(v, "DVE", t)
                ins = v.tensor_tensor(s_m1[:], s_m1[:], s_rden[:], ALU.mult)
                t = S.inc("dve", ins)
                ins = v.tensor_tensor(s_m2[:], s_m2[:], s_rden[:], ALU.mult)
                t = S.inc("dve", ins)
                S.wait(v, "DVE", t)
                for tt in range(NT_C):
                    ins = v.tensor_scalar(
                        s_mk1[:, tt, :], s_mk1[:, tt, :], s_m1[:, tt:tt + 1],
                        None, ALU.mult)
                    ins = v.tensor_scalar(
                        s_mk2[:, tt, :], s_mk2[:, tt, :], s_m2[:, tt:tt + 1],
                        None, ALU.mult)
                t = S.inc("dve", ins)
                S.wait(v, "DVE", t)
                ins = v.tensor_tensor(s_cw[:], s_mk1[:], s_mk2[:], ALU.add)
                T["cw"] = S.inc("dve", ins)
            block.vector(f)

            if debug_taps:
                def f(sync):
                    S.wait(sync, "SP", T["cw"])
                    S.wait(sync, "SP", T.get("lasttap"))
                    ins = sync.dma_start(taps["tap_cw"], s_cw[:, 0, :])
                    T["lasttap"] = S.inc("sptap", ins, 16)
                block.sync(f)

            with ExitStack() as _es:
                s_acc = xz
                s_wg0 = _es.enter_context(nc.sbuf_tensor("d_wg0", [128, NDT, FC], BF16))
                s_wg1 = _es.enter_context(nc.sbuf_tensor("d_wg1", [128, NDT, FC], BF16))
                s_wu0 = _es.enter_context(nc.sbuf_tensor("d_wu0", [128, NDT, FC], BF16))
                s_wu1 = _es.enter_context(nc.sbuf_tensor("d_wu1", [128, NDT, FC], BF16))
                s_wd0 = _es.enter_context(nc.sbuf_tensor("d_wd0", [128, NFT, D], BF16))
                s_wd1 = _es.enter_context(nc.sbuf_tensor("d_wd1", [128, NFT, D], BF16))
                s_h = _es.enter_context(nc.sbuf_tensor("d_h", [128, NFT, T_C], BF16))
                s_sg = _es.enter_context(nc.sbuf_tensor("d_sg", [128, 1024], F32))
                s_do = _es.enter_context(nc.sbuf_tensor("d_out", [128, D], F32))
                s_h1t = _es.enter_context(nc.sbuf_tensor("d_h1t", [128, D], F32))
                p_g = _es.enter_context(nc.psum_tensor("d_g", [128, 1024], F32))
                p_u = _es.enter_context(nc.psum_tensor("d_u", [128, 1024], F32))
                p_d0 = _es.enter_context(nc.psum_tensor("d_d0", [128, D], F32))
                p_d1 = _es.enter_context(nc.psum_tensor("d_d1", [128, D], F32))
                s_wg = [s_wg0, s_wg1]
                s_wu = [s_wu0, s_wu1]
                s_wd = [s_wd0, s_wd1]
                p_d = [p_d0, p_d1]

                def f(v):
                    S.wait(v, "DVE", T["cw"])
                    S.wait(v, "DVE", T["h1sp_all"])
                    S.wait(v, "DVE", T.get("tapguard4"))
                    for tt in range(NT_C):
                        ins = v.memset(s_acc[:, tt, :], 0.0)
                    T["acc0"] = S.inc("dve", ins)
                block.vector(f)

                NFC_D = DFF // FC
                steps = [(e, fc) for e in range(NE) for fc in range(NFC_D)]
                wd_toks = {}
                w_usedD = {}

                def d_prefetch(si):
                    e, fc = steps[si]
                    par = si % 2

                    def f(gp, e=e, fc=fc, si=si, par=par):
                        S.wait(gp, "GP", w_usedD.get(si - 2))
                        if si < 2:
                            S.wait(gp, "GP", T["rwt"])
                            S.wait(gp, "GP", T["cw"])
                        ins = gp.dma_start(
                            s_wg[par][:],
                            wg_r[e, fc].rearrange("(dt p) c -> p dt c", p=128))
                        S.inc(f"gpd{par}", ins, 16)
                        ins = gp.dma_start(
                            s_wu[par][:],
                            wu_r[e, fc].rearrange("(dt p) c -> p dt c", p=128))
                        S.inc(f"gpd{par}", ins, 16)
                        ins = gp.dma_start(
                            s_wd[par][:],
                            wd_i[e, fc * FC:(fc + 1) * FC, :].rearrange(
                                "(ft p) c -> p ft c", p=128))
                        wd_toks[si] = S.inc(f"gpd{par}", ins, 16)
                    block.gpsimd(f)

                d_prefetch(0)
                if len(steps) > 1:
                    d_prefetch(1)

                def f(a):
                    load_act_set(a, 18)
                block.scalar(f)

                for si, (e, fc) in enumerate(steps):
                    par = si % 2
                    for ft in range(NFT):
                        for h0, hw in chunks(T_C, 1024):
                            def f(pe, si=si, ft=ft, h0=h0, hw=hw, par=par):
                                S.wait(pe, "PE", wd_toks[si])
                                S.wait(pe, "PE", T["cw"])
                                S.wait(pe, "PE", T.get("sg"))
                                for dt in range(NDT):
                                    for q0, qw in chunks(hw, 512):
                                        ins = pe.matmul(
                                            p_g[:, q0:q0 + qw],
                                            s_wg[par][:, dt,
                                                      ft * 128:(ft + 1) * 128],
                                            xfT[:, dt, h0 + q0:h0 + q0 + qw],
                                            start=(dt == 0),
                                            stop=(dt == NDT - 1))
                                T["pg"] = S.inc("pe", ins)
                            block.tensor(f)

                            def f(a, hw=hw):
                                S.wait(a, "ACT", T["pg"])
                                S.wait(a, "ACT", T.get("hop"))
                                ins = a.activation(
                                    s_sg[:, 0:hw], p_g[:, 0:hw], AF.Silu)
                                T["sg"] = S.inc("act", ins)
                            block.scalar(f)

                            def f(pe, si=si, ft=ft, h0=h0, hw=hw, par=par):
                                S.wait(pe, "PE", T.get("hop"))
                                for dt in range(NDT):
                                    for q0, qw in chunks(hw, 512):
                                        ins = pe.matmul(
                                            p_u[:, q0:q0 + qw],
                                            s_wu[par][:, dt,
                                                      ft * 128:(ft + 1) * 128],
                                            xfT[:, dt, h0 + q0:h0 + q0 + qw],
                                            start=(dt == 0),
                                            stop=(dt == NDT - 1))
                                T["pu"] = S.inc("pe", ins)
                            block.tensor(f)

                            def f(v, ft=ft, h0=h0, hw=hw):
                                S.wait(v, "DVE", T["pu"])
                                S.wait(v, "DVE", T["sg"])
                                S.wait(v, "DVE", T.get(f"pd{NT_C - 1}"))
                                ins = v.tensor_tensor(
                                    s_h[:, ft, h0:h0 + hw], s_sg[:, 0:hw],
                                    p_u[:, 0:hw], ALU.mult)
                                T["hop"] = S.inc("dve", ins)
                            block.vector(f)

                    for tt in range(NT_C):
                        dp = tt % 2

                        def f(pe, si=si, tt=tt, dp=dp, par=par):
                            S.wait(pe, "PE", T["hop"])
                            S.wait(pe, "PE", T.get(f"accup{(tt - 2) % NT_C}"))
                            for ft in range(NFT):
                                for h0, hw in chunks(D, 512):
                                    ins = pe.matmul(
                                        p_d[dp][:, h0:h0 + hw],
                                        s_h[:, ft, tt * 128:(tt + 1) * 128],
                                        s_wd[par][:, ft, h0:h0 + hw],
                                        start=(ft == 0), stop=(ft == NFT - 1))
                            T[f"pd{tt}"] = S.inc("pe", ins)
                        block.tensor(f)

                        def f(v, e=e, tt=tt, dp=dp):
                            S.wait(v, "DVE", T[f"pd{tt}"])
                            S.wait(v, "DVE", T["acc0"])
                            S.wait(v, "DVE", T["cw"])
                            ins = v.scalar_tensor_tensor(
                                s_acc[:, tt, :], p_d[dp][:],
                                s_cw[:, tt, e:e + 1], s_acc[:, tt, :],
                                ALU.mult, ALU.add)
                            T[f"accup{tt}"] = S.inc("dve", ins)
                        block.vector(f)
                    w_usedD[si] = T[f"accup{NT_C - 1}"]
                    if si + 2 < len(steps):
                        d_prefetch(si + 2)

                for tt in range(NT_C):
                    def f(sync, tt=tt):
                        S.wait(sync, "SP", T["h1sp_all"])
                        S.wait(sync, "SP", T.get("dout"))
                        ins = sync.dma_start(
                            s_h1t[:], h1sp[tt * 128:(tt + 1) * 128, :])
                        T["h1t"] = S.inc("sph1t", ins, 16)
                    block.sync(f)

                    def f(v, tt=tt):
                        S.wait(v, "DVE", T["h1t"])
                        S.wait(v, "DVE", T[f"accup{tt}"])
                        S.wait(v, "DVE", T.get("outdma"))
                        ins = v.tensor_tensor(
                            s_do[:], s_h1t[:], s_acc[:, tt, :], ALU.add)
                        T["dout"] = S.inc("dve", ins)
                    block.vector(f)

                    def f(sync, tt=tt):
                        S.wait(sync, "SP", T["dout"])
                        ins = sync.dma_start(
                            out_t[tt * 128:(tt + 1) * 128, :], s_do[:])
                        T["outdma"] = S.inc("spout", ins, 16)
                    block.sync(f)

                def f(sync):
                    for nm in S.n:
                        if nm.startswith("sp") or nm.startswith("gp"):
                            if S.n[nm] > 0:
                                sync.wait_ge(S.h[nm], S.n[nm])
                block.sync(f)

    for cm in reversed(ctx):
        cm.__exit__(None, None, None)
    return nc



def _rope_tables(T):
    inv = (1.0 / np.power(np.float32(10000.0),
                          np.arange(0, HD, 2, dtype=np.float32) / np.float32(HD))
           ).astype(np.float32)
    freqs = np.arange(T, dtype=np.float32)[:, None] * inv[None, :]
    emb = np.concatenate([freqs, freqs], axis=-1)
    cos_td = np.cos(emb).astype(np.float32)
    sin_td = np.sin(emb).astype(np.float32)
    rows = np.arange(128) % HD
    cosk = cos_td.T[rows, :]
    sgn = np.where((np.arange(128) % HD) < 32, -1.0, 1.0).astype(np.float32)
    s1k = sin_td.T[rows, :] * sgn[:, None]
    return np.ascontiguousarray(cosk), np.ascontiguousarray(s1k)


def _perm_bd():
    p = np.zeros((128, 128), np.float32)
    for b in (0, 64):
        for m in range(HD):
            p[b + (m + 32) % HD, b + m] = 1.0
    return p


def make_in_maps(inputs, T_B, T_C, n_cores):
    x = np.asarray(inputs["L0_attn_x"], np.float32)
    cosk, s1k = _rope_tables(T_B)
    idn = np.eye(128, dtype=ml_dtypes.bfloat16)
    perm = _perm_bd()

    def wr(w):
        w = np.asarray(w, np.float32)
        return np.ascontiguousarray(w.reshape(D, NHG, 128).transpose(1, 0, 2))

    shared = {
        "idn": idn, "permbd": perm,
        "wq_r": wr(inputs["L0_attn_Wq"]),
        "wk_r": wr(inputs["L0_attn_Wk"]),
        "wv_r": np.ascontiguousarray(np.asarray(inputs["L0_attn_Wv"], np.float32).reshape(D, NHG // 2, 256).transpose(1, 0, 2)),
        "bq": np.asarray(inputs["L0_attn_bq"], np.float32),
        "bk": np.asarray(inputs["L0_attn_bk"], np.float32),
        "bv": np.asarray(inputs["L0_attn_bv"], np.float32),
        "wo": np.asarray(inputs["L0_attn_Wo"], np.float32),
        "bo": np.asarray(inputs["L0_attn_bo"], np.float32),
        "g1": np.asarray(inputs["L0_attn_g1"], np.float32),
        "be1": np.asarray(inputs["L0_attn_be1"], np.float32),
        "g2": np.asarray(inputs["L0_attn_g2"], np.float32),
        "be2": np.asarray(inputs["L0_attn_be2"], np.float32),
        "w1_r": np.ascontiguousarray(
            np.asarray(inputs["L0_attn_W1"], np.float32)
            .reshape(D, DFF // FC, FC).transpose(1, 0, 2)),
        "b1_r": np.ascontiguousarray(np.asarray(inputs["L0_attn_b1"], np.float32).reshape(DFF // FC, NFT, 128).transpose(0, 2, 1)),
        "w2": np.asarray(inputs["L0_attn_W2"], np.float32),
        "b2": np.asarray(inputs["L0_attn_b2"], np.float32),
        "rwt": np.ascontiguousarray(
            np.asarray(inputs["L0_ffn_router_w"], np.float32).T),
        "wg_r": np.ascontiguousarray(
            np.asarray(inputs["L0_ffn_w_gate"], np.float32)
            .reshape(NE, D, DFF // FC, FC).transpose(0, 2, 1, 3)),
        "wu_r": np.ascontiguousarray(
            np.asarray(inputs["L0_ffn_w_up"], np.float32)
            .reshape(NE, D, DFF // FC, FC).transpose(0, 2, 1, 3)),
        "wd": np.asarray(inputs["L0_ffn_w_down"], np.float32),
    }
    maps = []
    cpb = T_B // T_C
    for c in range(n_cores):
        b = c // cpb
        q0 = (c % cpb) * T_C
        xb = np.roll(x[b], -q0, axis=0)
        m = dict(shared)
        m["x_b"] = np.ascontiguousarray(xb)
        m["cosk"] = np.ascontiguousarray(np.roll(cosk, -q0, axis=1))
        m["s1k"] = np.ascontiguousarray(np.roll(s1k, -q0, axis=1))
        maps.append(m)
    return maps


def kernel(**inputs):
    T_B, T_C = 8192, 2048
    n_cores = 8
    nc = build_program(T_B, T_C)
    maps = make_in_maps(inputs, T_B, T_C, n_cores)
    res = run_bass_kernel_spmd(nc, maps, core_ids=list(range(n_cores)))
    x = inputs["L0_attn_x"]
    B = x.shape[0]
    out = np.empty((B, T_B, D), np.float32)
    cpb = T_B // T_C
    for c in range(n_cores):
        b = c // cpb
        q0 = (c % cpb) * T_C
        out[b, q0:q0 + T_C, :] = res.results[c]["out"]
    return out


# version 15
# speedup vs baseline: 1.1400x; 1.0737x over previous
from contextlib import ExitStack
import numpy as np
import ml_dtypes
import concourse.bass as bass
import concourse.mybir as mybir
from concourse.bass_utils import run_bass_kernel_spmd

F32 = mybir.dt.float32
BF16 = mybir.dt.bfloat16
AF = mybir.ActivationFunctionType
ALU = mybir.AluOpType
AX = mybir.AxisListType

D = 1024
NH = 16
HD = 64
NHG = 8
DFF = 4096
NE = 8
NDT = D // 128
FC = 512
NFT = FC // 128


def chunks(total, size):
    return [(i, min(size, total - i)) for i in range(0, total, size)]


class Sems:
    def __init__(self, nc, names):
        self.h = {}
        self.n = {}
        self.lastw = {}
        self.cms = []
        for name in names:
            cm = nc.semaphore(name)
            self.cms.append(cm)
            self.h[name] = cm.__enter__()
            self.n[name] = 0

    def inc(self, name, instr, amount=1):
        self.n[name] += amount
        instr.then_inc(self.h[name], amount)
        return (name, self.n[name])

    def wait(self, eng, who, tok):
        if tok is None:
            return
        name, val = tok
        assert val <= self.n[name], f"wait on future {name}: {val} > {self.n[name]}"
        key = (who, name)
        if self.lastw.get(key, 0) >= val:
            return
        eng.wait_ge(self.h[name], val)
        self.lastw[key] = val

    def close(self):
        for cm in reversed(self.cms):
            cm.__exit__(None, None, None)


def build_program(T_B, T_C, debug_taps=False):
    NT_B = T_B // 128
    NT_C = T_C // 128
    BLK = min(1024, T_B)
    NBLK = T_B // BLK
    NBT = BLK // 128
    QC = min(1024, T_C)
    NQC = T_C // QC

    nc = bass.Bass("TRN2", target_bir_lowering=False, debug=False)

    din = {}
    def DI(name, shape, dtype=F32):
        din[name] = nc.dram_tensor(name, shape, dtype, kind="ExternalInput").ap()
        return din[name]

    x_b = DI("x_b", [T_B, D])
    cosk = DI("cosk", [128, T_B])
    s1k = DI("s1k", [128, T_B])
    idn_i = DI("idn", [128, 128], BF16)
    perm_i = DI("permbd", [128, 128])
    wq_r = DI("wq_r", [NHG, D, 128])
    wk_r = DI("wk_r", [NHG, D, 128])
    wv_r = DI("wv_r", [NHG // 2, D, 256])
    bq_i = DI("bq", [D])
    bk_i = DI("bk", [D])
    bv_i = DI("bv", [D])
    wo_i = DI("wo", [D, D])
    bo_i = DI("bo", [D])
    g1_i = DI("g1", [D])
    be1_i = DI("be1", [D])
    g2_i = DI("g2", [D])
    be2_i = DI("be2", [D])
    w1_r = DI("w1_r", [DFF // FC, D, FC])
    b1_i = DI("b1_r", [DFF // FC, 128, FC // 128])
    w2_i = DI("w2", [DFF, D])
    b2_i = DI("b2", [D])
    rwt_i = DI("rwt", [D, NE])
    wg_r = DI("wg_r", [NE, DFF // FC, D, FC])
    wu_r = DI("wu_r", [NE, DFF // FC, D, FC])
    wd_i = DI("wd", [NE, DFF, D])

    out_t = nc.dram_tensor("out", [T_C, D], F32, kind="ExternalOutput").ap()
    spill = nc.dram_tensor("xn1t_spill", [NDT, 128, T_B], BF16, kind="Internal").ap()
    h1sp = nc.dram_tensor("h1_spill", [T_C, D], F32, kind="Internal").ap()
    vsp = nc.dram_tensor("v_spill", [128, T_B // 128, 128], BF16, kind="Internal").ap()

    taps = {}
    if debug_taps:
        for name, shape, dt in [
            ("tap_xn1", [128, D], BF16), ("tap_khat", [65, T_B], BF16),
            ("tap_vbuf", [128, 65], BF16), ("tap_qhat", [65, T_C], BF16),
            ("tap_attnT", [128, T_C], BF16), ("tap_xz", [128, D], F32),
            ("tap_xn2T", [128, T_C], BF16), ("tap_h1", [128, D], F32),
            ("tap_cw", [128, NE], F32),
        ]:
            taps[name] = nc.dram_tensor(name, shape, dt, kind="ExternalOutput").ap()

    ctx = []
    def ent(cm):
        ctx.append(cm)
        return cm.__enter__()

    sb = lambda n, s, d: ent(nc.sbuf_tensor(n, s, d))

    S = Sems(nc, ["sprow", "sptap", "spx0", "spx1", "spx2", "spx3",
              "spx4", "spx5", "spx6", "spx7", "spst0", "spst1",
              "spxb0", "spxb1", "sph1s", "sph1t", "spout", "spvs", "spvr",
              "gpw", "gpw10", "gpw11", "gpd0", "gpd1",
              "pe", "act", "dve", "pool"])

    s_idn = sb("s_idn", [128, 128], BF16)
    s_perm = sb("s_perm", [128, 128], F32)
    s_eps = sb("s_eps", [128, 1], F32)
    s_one1 = sb("s_one1", [1, 128], F32)
    s_ones64 = sb("s_ones64", [64, 1], BF16)
    s_bones = sb("s_bones", [128, 2], BF16)
    s_ln102 = sb("s_ln102", [1, 1], F32)
    s_mm1 = [sb("s_mm1a", [128, 1], F32), sb("s_mm1b", [128, 1], F32)]
    s_mm2 = [sb("s_mm2a", [128, 1], F32), sb("s_mm2b", [128, 1], F32)]
    s_ss = [sb("s_ssa", [128, 1], F32), sb("s_ssb", [128, 1], F32)]
    s_rstd = [sb("s_rstda", [128, 1], F32), sb("s_rstdb", [128, 1], F32)]
    s_cc = [sb("s_cca", [128, D], F32), sb("s_ccb", [128, D], F32)]
    s_cc2 = [sb("s_cc2a", [128, D], F32), sb("s_cc2b", [128, D], F32)]
    s_yy = [sb("s_yya", [128, D], F32), sb("s_yyb", [128, D], F32)]

    block = ent(nc.Block())
    T = {}

    def f(v):
        v.memset(s_eps[:], 1e-5)
        v.memset(s_one1[:], 1.0)
        v.memset(s_ones64[:], 1.0)
        ins = v.memset(s_bones[:], 0.0)
        t = S.inc("dve", ins)
        S.wait(v, "DVE", t)
        v.memset(s_bones[0:64, 0:1], 1.0)
        v.memset(s_bones[64:128, 1:2], 1.0)
        ins = v.memset(s_ln102[:], float(np.log(1.05)))
        T["init_dve"] = S.inc("dve", ins)
    block.vector(f)

    def load_act_set(a, set_id):
        a.add_instruction(mybir.InstLoadActFuncSet(
            name=nc.get_next_instruction_name(),
            act_func_set_id=set_id, ins=[], outs=[]))

    def f(a):
        load_act_set(a, 6)
    block.scalar(f)

    def f(sync):
        ins = sync.dma_start(s_idn[:], idn_i)
        S.inc("sprow", ins, 16)
        ins = sync.dma_start(s_perm[:], perm_i)
        T["init_sp"] = S.inc("sprow", ins, 16)
    block.sync(f)

    def emit_ln(src_ap, pre_toks, tag, par=0):
        def f(v, pre=list(pre_toks)):
            S.wait(v, "DVE", T.get(f"ln_guard{par}"))
            for t in pre:
                S.wait(v, "DVE", t)
            ins = v.reduce_sum(s_mm1[par][:], src_ap, axis=AX.X)
            T[tag + "_sum"] = S.inc("dve", ins)
        block.vector(f)

        def f(a):
            S.wait(a, "ACT", T[tag + "_sum"])
            ins = a.mul(s_mm2[par][:], s_mm1[par][:], -1.0 / D)
            T[tag + "_mn"] = S.inc("act", ins)
        block.scalar(f)

        def f(v):
            S.wait(v, "DVE", T[tag + "_mn"])
            ins = v.tensor_scalar(s_cc[par][:], src_ap, s_mm2[par][:], None, ALU.add)
            T[tag + "_c"] = S.inc("dve", ins)
        block.vector(f)

        def f(a):
            S.wait(a, "ACT", T[tag + "_c"])
            ins = a.activation(s_cc2[par][:], s_cc[par][:], AF.Square,
                               accum_out=s_ss[par][:])
            t = S.inc("act", ins)
            S.wait(a, "ACT", t)
            ins = a.activation(s_ss[par][:], s_ss[par][:], AF.Ln, bias=s_eps[:],
                               scale=1.0 / D)
            t = S.inc("act", ins)
            S.wait(a, "ACT", t)
            ins = a.activation(s_rstd[par][:], s_ss[par][:], AF.Exp, scale=-0.5)
            T[tag + "_rstd"] = S.inc("act", ins)
        block.scalar(f)
        return T[tag + "_rstd"], T[tag + "_c"]

    def emit_ln2(src_ap, gb_ap, bb_ap, out_ap, pre_toks, tag, par=0):
        t_r, _ = emit_ln(src_ap, pre_toks, tag + "a", par)

        def f(v):
            S.wait(v, "DVE", t_r)
            ins = v.tensor_scalar(s_yy[par][:], s_cc[par][:], s_rstd[par][:],
                                  None, ALU.mult)
            T[tag + "_y"] = S.inc("dve", ins)
        block.vector(f)

        t_r2, _ = emit_ln(s_yy[par][:], [T[tag + "_y"]], tag + "b", par)

        def f(v):
            S.wait(v, "DVE", t_r2)
            ins = v.scalar_tensor_tensor(
                s_cc2[par][:], s_cc[par][:], s_rstd[par][:], gb_ap,
                ALU.mult, ALU.mult)
            t = S.inc("dve", ins)
            S.wait(v, "DVE", t)
            S.wait(v, "DVE", T.get("tapguard"))
            ins = v.tensor_tensor(out_ap, s_cc2[par][:], bb_ap, ALU.add)
            T[tag + "_out"] = S.inc("dve", ins)
            T[f"ln_guard{par}"] = T[tag + "_out"]
        block.vector(f)
        return T[tag + "_out"]

    def emit_bcast(row_ap, dst_ap, p_bc, row_tok, tag):
        for half in range(2):
            def fpe(pe, half=half):
                S.wait(pe, "PE", row_tok)
                S.wait(pe, "PE", T["init_sp"])
                S.wait(pe, "PE", T["init_dve"])
                S.wait(pe, "PE", T.get("bc_ev"))
                ins = pe.matmul(
                    p_bc[:, 0:512], s_one1[:],
                    row_ap[:, half * 512:(half + 1) * 512],
                    start=True, stop=True)
                T["bc_mm"] = S.inc("pe", ins)
            block.tensor(fpe)

            def fv(v, half=half):
                S.wait(v, "DVE", T["bc_mm"])
                ins = v.tensor_copy(
                    dst_ap[:, half * 512:(half + 1) * 512], p_bc[:, 0:512])
                T["bc_ev"] = S.inc("dve", ins)
            block.vector(fv)
        T[tag + "_tok"] = T["bc_ev"]
        T["bcrow_used"] = T["bc_mm"]
        return T["bc_ev"]

    with ExitStack() as _es:
        s_x0 = _es.enter_context(nc.sbuf_tensor("a1_x0", [128, D], F32))
        s_x1 = _es.enter_context(nc.sbuf_tensor("a1_x1", [128, D], F32))
        s_x2 = _es.enter_context(nc.sbuf_tensor("a1_x2", [128, D], F32))
        s_x3 = _es.enter_context(nc.sbuf_tensor("a1_x3", [128, D], F32))
        s_xn1 = [_es.enter_context(
            nc.sbuf_tensor(f"a1_xn1{k}", [128, D], BF16)) for k in range(8)]
        s_g1b = _es.enter_context(nc.sbuf_tensor("a1_g1b", [128, D], F32))
        s_be1b = _es.enter_context(nc.sbuf_tensor("a1_be1b", [128, D], F32))
        s_row = _es.enter_context(nc.sbuf_tensor("a1_row", [1, D], F32))
        s_st0 = _es.enter_context(nc.sbuf_tensor("a1_st0", [128, NDT, BLK], BF16))
        s_st1 = _es.enter_context(nc.sbuf_tensor("a1_st1", [128, NDT, BLK], BF16))
        p_bc = _es.enter_context(nc.psum_tensor("a1_bc", [128, 512], F32))
        p_tr0 = _es.enter_context(nc.psum_tensor("a1_tr0", [128, 128], BF16))
        p_tr1 = _es.enter_context(nc.psum_tensor("a1_tr1", [128, 128], BF16))
        s_x = [s_x0, s_x1, s_x2, s_x3] + [
            _es.enter_context(nc.sbuf_tensor(f"a1_x{k}", [128, D], F32))
            for k in range(4, 8)]
        s_st = [s_st0, s_st1]
        p_tr = [p_tr0, p_tr1]
        for nm in ("c", "c2", "y"):
            pass
        extra = []
        for k in range(6):
            extra.append((
                _es.enter_context(nc.sbuf_tensor(f"a1mm1{k}", [128, 1], F32)),
                _es.enter_context(nc.sbuf_tensor(f"a1mm2{k}", [128, 1], F32)),
                _es.enter_context(nc.sbuf_tensor(f"a1ss{k}", [128, 1], F32)),
                _es.enter_context(nc.sbuf_tensor(f"a1rstd{k}", [128, 1], F32)),
                _es.enter_context(nc.sbuf_tensor(f"a1cc{k}", [128, D], F32)),
                _es.enter_context(nc.sbuf_tensor(f"a1cc2{k}", [128, D], F32)),
                _es.enter_context(nc.sbuf_tensor(f"a1yy{k}", [128, D], F32)),
            ))
        for (m1, m2, ss, rs, cc, cc2, yy) in extra:
            s_mm1.append(m1); s_mm2.append(m2); s_ss.append(ss)
            s_rstd.append(rs); s_cc.append(cc); s_cc2.append(cc2)
            s_yy.append(yy)

        for name, src, dst in [("g1b", g1_i, s_g1b), ("be1b", be1_i, s_be1b)]:
            def f(sync, src=src, name=name):
                S.wait(sync, "SP", T.get("bcrow_used"))
                ins = sync.dma_start(
                    s_row[:], src.rearrange("(one d) -> one d", one=1))
                T[name + "_row"] = S.inc("sprow", ins, 16)
            block.sync(f)
            emit_bcast(s_row, dst, p_bc, T[name + "_row"], name)

        stage_free = [None, None]
        xuse = [None] * 8
        trcp = [None, None]
        G = 4
        NG = NT_B // G
        tokG = {}

        def tiles_of(g):
            return [(g * G + k, (g * G + k) % 8) for k in range(G)]

        def s_dma(g):
            for i, par in tiles_of(g):
                def f(sync, i=i, par=par):
                    S.wait(sync, "SP", xuse[par])
                    ins = sync.dma_start(
                        s_x[par][:], x_b[i * 128:(i + 1) * 128, :])
                    T[f"a1x{i}"] = S.inc(f"spx{par}", ins, 16)
                block.sync(f)

        def s1(g):
            def f(v, g=g):
                S.wait(v, "DVE", tokG.get(("mn2", g - 2)))
                for i, par in tiles_of(g):
                    S.wait(v, "DVE", T[f"a1x{i}"])
                    ins = v.reduce_sum(s_mm1[par][:], s_x[par][:], axis=AX.X)
                tokG[("sum", g)] = S.inc("dve", ins)
            block.vector(f)

        def s2(g):
            def f(a, g=g):
                S.wait(a, "ACT", tokG[("sum", g)])
                S.wait(a, "ACT", tokG.get(("c2", g - 2)))
                for i, par in tiles_of(g):
                    ins = a.mul(s_mm2[par][:], s_mm1[par][:], -1.0 / D)
                tokG[("mn", g)] = S.inc("act", ins)
            block.scalar(f)

        def s3(g):
            def f(v, g=g):
                S.wait(v, "DVE", tokG[("mn", g)])
                S.wait(v, "DVE", tokG.get(("xn1", g - 2)))
                for i, par in tiles_of(g):
                    ins = v.tensor_scalar(
                        s_cc[par][:], s_x[par][:], s_mm2[par][:], None, ALU.add)
                tokG[("c", g)] = S.inc("dve", ins)
            block.vector(f)

        def s4(g):
            def f(a, g=g):
                S.wait(a, "ACT", tokG[("c", g)])
                S.wait(a, "ACT", tokG.get(("xn1", g - 2)))
                for i, par in tiles_of(g):
                    ins = a.activation(s_cc2[par][:], s_cc[par][:], AF.Square,
                                       accum_out=s_ss[par][:])
                t = S.inc("act", ins)
                S.wait(a, "ACT", t)
                for i, par in tiles_of(g):
                    ins = a.activation(s_ss[par][:], s_ss[par][:], AF.Ln,
                                       bias=s_eps[:], scale=1.0 / D)
                t = S.inc("act", ins)
                S.wait(a, "ACT", t)
                for i, par in tiles_of(g):
                    ins = a.activation(s_rstd[par][:], s_ss[par][:], AF.Exp,
                                       scale=-0.5)
                tokG[("rstd", g)] = S.inc("act", ins)
            block.scalar(f)

        def s5(g):
            def f(v, g=g):
                S.wait(v, "DVE", tokG[("rstd", g)])
                S.wait(v, "DVE", tokG.get(("c2", g - 2)))
                for i, par in tiles_of(g):
                    ins = v.tensor_scalar(
                        s_yy[par][:], s_cc[par][:], s_rstd[par][:], None,
                        ALU.mult)
                t = S.inc("dve", ins)
                S.wait(v, "DVE", t)
                for i, par in tiles_of(g):
                    ins = v.reduce_sum(s_mm1[par][:], s_yy[par][:], axis=AX.X)
                tokG[("sum2", g)] = S.inc("dve", ins)
            block.vector(f)

        def s7(g):
            def f(a, g=g):
                S.wait(a, "ACT", tokG[("sum2", g)])
                for i, par in tiles_of(g):
                    ins = a.mul(s_mm2[par][:], s_mm1[par][:], -1.0 / D)
                tokG[("mn2", g)] = S.inc("act", ins)
            block.scalar(f)

        def s8(g):
            def f(v, g=g):
                S.wait(v, "DVE", tokG[("mn2", g)])
                for i, par in tiles_of(g):
                    ins = v.tensor_scalar(
                        s_cc[par][:], s_yy[par][:], s_mm2[par][:], None,
                        ALU.add)
                tokG[("c2", g)] = S.inc("dve", ins)
            block.vector(f)

        def s9(g):
            def f(a, g=g):
                S.wait(a, "ACT", tokG[("c2", g)])
                for i, par in tiles_of(g):
                    ins = a.activation(s_cc2[par][:], s_cc[par][:], AF.Square,
                                       accum_out=s_ss[par][:])
                t = S.inc("act", ins)
                S.wait(a, "ACT", t)
                for i, par in tiles_of(g):
                    ins = a.activation(s_ss[par][:], s_ss[par][:], AF.Ln,
                                       bias=s_eps[:], scale=1.0 / D)
                t = S.inc("act", ins)
                S.wait(a, "ACT", t)
                for i, par in tiles_of(g):
                    ins = a.activation(s_rstd[par][:], s_ss[par][:], AF.Exp,
                                       scale=-0.5)
                tokG[("rstd2", g)] = S.inc("act", ins)
            block.scalar(f)

        def s10(g):
            def f(v, g=g):
                S.wait(v, "DVE", tokG[("rstd2", g)])
                S.wait(v, "DVE", T["g1b_tok"])
                S.wait(v, "DVE", T["be1b_tok"])
                S.wait(v, "DVE", T.get("a1_xn1_used"))
                for i, par in tiles_of(g):
                    ins = v.scalar_tensor_tensor(
                        s_cc2[par][:], s_cc[par][:], s_rstd[par][:], s_g1b[:],
                        ALU.mult, ALU.mult)
                t = S.inc("dve", ins)
                S.wait(v, "DVE", t)
                S.wait(v, "DVE", T.get("tapguard"))
                for i, par in tiles_of(g):
                    ins = v.tensor_tensor(
                        s_xn1[par][:], s_cc2[par][:], s_be1b[:], ALU.add)
                tokG[("xn1", g)] = S.inc("dve", ins)
                for i, par in tiles_of(g):
                    xuse[par] = tokG[("c", g)]
            block.vector(f)

            if debug_taps and g == 0:
                def f(sync, g=g):
                    S.wait(sync, "SP", tokG[("xn1", g)])
                    S.wait(sync, "SP", T.get("lasttap"))
                    ins = sync.dma_start(taps["tap_xn1"], s_xn1[0][:])
                    T["tapguard"] = S.inc("sptap", ins, 16)
                    T["lasttap"] = T["tapguard"]
                block.sync(f)

        def s11(g):
            for i, par in tiles_of(g):
                sp = (i // NBT) % 2
                for j in range(NDT):
                    tp = j % 2

                    def f(pe, i=i, j=j, tp=tp, par=par, g=g):
                        S.wait(pe, "PE", tokG[("xn1", g)])
                        S.wait(pe, "PE", T["init_sp"])
                        S.wait(pe, "PE", trcp[tp])
                        ins = pe.transpose(
                            p_tr[tp][:], s_xn1[par][:, j * 128:(j + 1) * 128],
                            s_idn[:])
                        T["a1t"] = S.inc("pe", ins)
                    block.tensor(f)

                    def f(a, i=i, j=j, tp=tp, sp=sp):
                        S.wait(a, "ACT", T["a1t"])
                        S.wait(a, "ACT", stage_free[sp])
                        c0 = (i % NBT) * 128
                        ins = a.copy(s_st[sp][:, j, c0:c0 + 128], p_tr[tp][:])
                        trcp[tp] = S.inc("act", ins)
                    block.scalar(f)
                T["a1_xn1_used"] = T["a1t"]

                if (i + 1) % NBT == 0:
                    blk = i // NBT

                    def f(sync, blk=blk, sp=sp):
                        S.wait(sync, "SP", trcp[0])
                        S.wait(sync, "SP", trcp[1])
                        ins = sync.dma_start(
                            spill[:, :, blk * BLK:(blk + 1) * BLK].rearrange(
                                "dt p c -> p dt c"),
                            s_st[sp][:])
                        T[f"a1sp{blk}"] = S.inc(f"spst{sp}", ins, 16)
                        stage_free[sp] = T[f"a1sp{blk}"]
                    block.sync(f)

        s_dma(0)
        for g in range(NG + 1):
            a_ok = g < NG
            b = g - 1
            if a_ok:
                s1(g)
            if b >= 0:
                s7(b)
            if a_ok:
                s2(g)
            if b >= 0:
                s8(b)
            if a_ok:
                s3(g)
            if b >= 0:
                s9(b)
            if a_ok:
                s4(g)
            if b >= 0:
                s10(b)
            if a_ok:
                s5(g)
                if g + 1 < NG:
                    s_dma(g + 1)
            if b >= 0:
                s11(b)

        spill_done = T[f"a1sp{NBLK - 1}"]
        spill_done2 = T[f"a1sp{NBLK - 2}"] if NBLK >= 2 else None
        for lst in (s_mm1, s_mm2, s_ss, s_rstd, s_cc, s_cc2, s_yy):
            lst.pop(); lst.pop()

    attnT_cm = nc.sbuf_tensor("attnT", [128, NDT, T_C], BF16)
    attnT = attnT_cm.__enter__()
    with ExitStack() as _es:
        qhat = _es.enter_context(nc.sbuf_tensor("qhat", [65, 2, T_C], BF16))
        khat = _es.enter_context(nc.sbuf_tensor("khat", [65, 2, T_B], BF16))
        vbuf = _es.enter_context(nc.sbuf_tensor("vbuf", [128, NT_B, 2, 65], BF16))
        s_wq = _es.enter_context(nc.sbuf_tensor("s_wq", [128, NDT, 128], BF16))
        s_wk = _es.enter_context(nc.sbuf_tensor("s_wk", [128, NDT, 128], BF16))
        s_wvp = _es.enter_context(nc.sbuf_tensor("s_wvp", [128, NDT, 256], BF16))
        s_vst = _es.enter_context(nc.sbuf_tensor("s_vst", [128, NBT, 128], BF16))
        s_vrd0 = _es.enter_context(nc.sbuf_tensor("s_vrd0", [128, NBT, 128], BF16))
        s_vrd1 = _es.enter_context(nc.sbuf_tensor("s_vrd1", [128, NBT, 128], BF16))
        s_vrd = [s_vrd0, s_vrd1]
        s_bqb = _es.enter_context(nc.sbuf_tensor("s_bqb", [128, 1], F32))
        s_bkb = _es.enter_context(nc.sbuf_tensor("s_bkb", [128, 1], F32))
        s_xb0 = _es.enter_context(nc.sbuf_tensor("s_xb0", [128, NDT, BLK], BF16))
        s_xb1 = _es.enter_context(nc.sbuf_tensor("s_xb1", [128, NDT, BLK], BF16))
        s_ck0 = _es.enter_context(nc.sbuf_tensor("s_ck0", [128, BLK], F32))
        s_ck1 = _es.enter_context(nc.sbuf_tensor("s_ck1", [128, BLK], F32))
        s_sk0 = _es.enter_context(nc.sbuf_tensor("s_sk0", [128, BLK], F32))
        s_sk1 = _es.enter_context(nc.sbuf_tensor("s_sk1", [128, BLK], F32))
        s_bvrow = _es.enter_context(nc.sbuf_tensor("s_bvrow", [1, 256], F32))
        s_bvbb = _es.enter_context(nc.sbuf_tensor("s_bvbb", [128, 256], F32))
        s_rmaxv = _es.enter_context(nc.sbuf_tensor("s_rmaxv", [128, 1], F32))
        s_t1v = _es.enter_context(nc.sbuf_tensor("s_t1v", [128, 1], F32))
        s_t1b = _es.enter_context(nc.sbuf_tensor("s_t1b", [128, 1], BF16))
        s_ktmp = s_yy
        s_u = s_cc
        s_w = s_cc2
        s_sq2 = _es.enter_context(nc.sbuf_tensor("s_sq2", [64, 512], BF16))
        s_rmax = _es.enter_context(nc.sbuf_tensor("s_rmax", [1, 2], F32))
        s_t11 = _es.enter_context(nc.sbuf_tensor("s_t11", [1, 2], F32))
        s_bias2 = _es.enter_context(nc.sbuf_tensor("s_bias2", [1, 2], F32))
        s_lq = _es.enter_context(nc.sbuf_tensor("s_lq", [1, 512], F32))
        s_brow = _es.enter_context(nc.sbuf_tensor("s_brow", [1, 512], F32))
        s_p0 = _es.enter_context(nc.sbuf_tensor("s_p0", [128, QC], BF16))
        s_p1 = _es.enter_context(nc.sbuf_tensor("s_p1", [128, QC], BF16))
        s_tacc = s_cc2[0][0:65, 0:QC]
        s_rl = s_cc[0][0:1, 0:QC]
        s_xb = [s_xb0, s_xb1]
        s_ck = [s_ck0, s_ck1]
        s_sk = [s_sk0, s_sk1]
        s_p = [s_p0, s_p1]
        xb_use = [None, None]
        ck_use = [None, None]

        def f(v):
            S.wait(v, "DVE", spill_done)
            S.wait(v, "DVE", spill_done2)
            ins = v.memset(vbuf[:, :, :, 64:65], 1.0)
            T["init_vbuf"] = S.inc("dve", ins)
        block.vector(f)

        for hg in range(NHG):
            def f(gp, hg=hg):
                S.wait(gp, "GP", spill_done)
                S.wait(gp, "GP", T.get("tapguard"))
                S.wait(gp, "GP", T.get("hat"))
                srcs = [
                    (s_wq[:], wq_r[hg].rearrange("(dt p) c -> p dt c", p=128)),
                    (s_wk[:], wk_r[hg].rearrange("(dt p) c -> p dt c", p=128)),
                    (s_bqb[:], bq_i[hg * 128:(hg + 1) * 128].rearrange(
                        "(p one) -> p one", one=1)),
                    (s_bkb[:], bk_i[hg * 128:(hg + 1) * 128].rearrange(
                        "(p one) -> p one", one=1)),
                ]
                if hg % 2 == 0:
                    srcs.append(
                        (s_wvp[:],
                         wv_r[hg // 2].rearrange("(dt p) c -> p dt c", p=128)))
                for dst, src in srcs:
                    ins = gp.dma_start(dst, src)
                    T["wslices"] = S.inc("gpw", ins, 16)
            block.gpsimd(f)

            with ExitStack() as _es:
                p_k0 = _es.enter_context(nc.psum_tensor(f"a2_k0{hg}", [128, BLK], F32))
                p_k1 = _es.enter_context(nc.psum_tensor(f"a2_k1{hg}", [128, BLK], F32))
                p_rot = _es.enter_context(nc.psum_tensor(f"a2_rot{hg}", [128, 512], F32))
                p_vn0 = _es.enter_context(nc.psum_tensor(f"a2_vn0{hg}", [128, 256], F32))
                p_vn1 = _es.enter_context(nc.psum_tensor(f"a2_vn1{hg}", [128, 256], F32))
                p_nrm = _es.enter_context(nc.psum_tensor(f"a2_nrm{hg}", [1, 512], F32))
                p_kp = [p_k0, p_k1]
                p_vn = [p_vn0, p_vn1]
                vnT = [None, None]
                pkT = {}
                ktmpT = {}
                rotT = {}
                wopT = {}
                tuT = {}
                hatT = {}

                def f(v):
                    S.wait(v, "DVE", T.get("bias2"))
                    ins = v.memset(s_rmaxv[:], 0.0)
                    T["rmax0"] = S.inc("dve", ins)
                block.vector(f)

                if hg % 2 == 0:
                    def f(sync, hg=hg):
                        S.wait(sync, "SP", T.get("bvrow_used"))
                        ins = sync.dma_start(
                            s_bvrow[:],
                            bv_i[hg * 128:hg * 128 + 256].rearrange(
                                "(one c) -> one c", one=1))
                        T["bvrow"] = S.inc("sprow", ins, 16)
                    block.sync(f)

                    def f(pe):
                        S.wait(pe, "PE", T["bvrow"])
                        S.wait(pe, "PE", T["init_sp"])
                        S.wait(pe, "PE", T.get("wop"))
                        S.wait(pe, "PE", T.get("bvbb"))
                        for h0, hw in chunks(256, 512):
                            ins = pe.matmul(p_rot[:, 0:256], s_one1[:],
                                            s_bvrow[:], start=True, stop=True)
                        t = S.inc("pe", ins)
                        T["bvrow_used"] = t
                        T["bvmm"] = t
                    block.tensor(f)

                    def f(v):
                        S.wait(v, "DVE", T["bvmm"])
                        ins = v.tensor_copy(s_bvbb[:], p_rot[:, 0:256])
                        T["bvbb"] = S.inc("dve", ins)
                    block.vector(f)

                blocks = [("kv", b) for b in range(NBLK)] + \
                         [("q", b) for b in range(T_C // BLK if T_C >= BLK else 1)]
                for bi, (kind, b) in enumerate(blocks):
                    par = bi % 2
                    w = BLK if kind == "kv" or T_C >= BLK else T_C
                    c0 = b * BLK if kind == "kv" else b * w

                    def f(sync, par=par, c0=c0, w=w):
                        S.wait(sync, "SP", spill_done)
                        S.wait(sync, "SP", spill_done2)
                        S.wait(sync, "SP", xb_use[par])
                        S.wait(sync, "SP", ck_use[par])
                        ins = sync.dma_start(
                            s_xb[par][:, :, 0:w],
                            spill[:, :, c0:c0 + w].rearrange("dt p c -> p dt c"))
                        S.inc(f"spxb{par}", ins, 16)
                        ins = sync.dma_start(s_ck[par][:, 0:w], cosk[:, c0:c0 + w])
                        S.inc(f"spxb{par}", ins, 16)
                        ins = sync.dma_start(s_sk[par][:, 0:w], s1k[:, c0:c0 + w])
                        T["ckt"] = S.inc(f"spxb{par}", ins, 16)
                    block.sync(f)
                    xt = ct = T["ckt"]

                    wsel = s_wk if kind == "kv" else s_wq
                    bsel = s_bkb if kind == "kv" else s_bqb
                    dhat = khat if kind == "kv" else qhat

                    def f(pe, xt=xt, wsel=wsel, par=par, w=w):
                        S.wait(pe, "PE", T.get("attn_done"))
                        S.wait(pe, "PE", xt)
                        S.wait(pe, "PE", T["wslices"])
                        S.wait(pe, "PE", ktmpT.get(par))
                        for dt in range(NDT):
                            for h0, hw in chunks(w, 512):
                                ins = pe.matmul(
                                    p_kp[par][:, h0:h0 + hw],
                                    wsel[:, dt, :],
                                    s_xb[par][:, dt, h0:h0 + hw],
                                    start=(dt == 0), stop=(dt == NDT - 1))
                        pkT[par] = S.inc("pe", ins)
                    block.tensor(f)

                    def f(a, bsel=bsel, par=par, w=w):
                        S.wait(a, "ACT", pkT[par])
                        S.wait(a, "ACT", rotT.get(par))
                        S.wait(a, "ACT", T.get("nrm_red"))
                        S.wait(a, "ACT", tuT.get(par))
                        ins = a.activation(
                            s_ktmp[par][:, 0:w], p_kp[par][:, 0:w], AF.Identity,
                            bias=bsel[:])
                        ktmpT[par] = S.inc("act", ins)
                    block.scalar(f)

                    if kind == "kv" and hg % 2 == 0:
                        for j in range(NBT):
                            vp = j % 2

                            def f(pe, j=j, vp=vp, xt=xt, par=par):
                                S.wait(pe, "PE", xt)
                                S.wait(pe, "PE", T["wslices"])
                                S.wait(pe, "PE", vnT[vp])
                                for dt in range(NDT):
                                    ins = pe.matmul(
                                        p_vn[vp][:],
                                        s_xb[par][:, dt,
                                                  j * 128:(j + 1) * 128],
                                        s_wvp[:, dt, :],
                                        start=(dt == 0), stop=(dt == NDT - 1))
                                T["pv"] = S.inc("pe", ins)
                            block.tensor(f)

                            def f(v, j=j, vp=vp, b=b):
                                S.wait(v, "DVE", T["pv"])
                                S.wait(v, "DVE", T["bvbb"])
                                S.wait(v, "DVE", T.get("tapguard2"))
                                S.wait(v, "DVE", T["init_vbuf"])
                                S.wait(v, "DVE", T.get(f"vspw{b}"))
                                tt = b * NBT + j
                                ins = v.tensor_tensor(
                                    vbuf[:, tt, :, 0:64],
                                    p_vn[vp][:, 0:128].rearrange(
                                        "p (h c) -> p h c", h=2),
                                    s_bvbb[:, 0:128].rearrange(
                                        "p (h c) -> p h c", h=2),
                                    ALU.add)
                                t = S.inc("dve", ins)
                                ins = v.tensor_tensor(
                                    s_vst[:, j, :],
                                    p_vn[vp][:, 128:256],
                                    s_bvbb[:, 128:256], ALU.add)
                                vnT[vp] = S.inc("dve", ins)
                                T["vbuf_wr"] = vnT[vp]
                                T["vst_wr"] = vnT[vp]
                            block.vector(f)

                        def f(sync, b=b):
                            S.wait(sync, "SP", T["vst_wr"])
                            ins = sync.dma_start(
                                vsp[:, b * NBT:(b + 1) * NBT, :], s_vst[:])
                            T[f"vspw{b}"] = S.inc("spvs", ins, 16)
                        block.sync(f)

                    if kind == "kv" and hg % 2 == 1:
                        vrp = b % 2

                        def f(sync, b=b, vrp=vrp):
                            S.wait(sync, "SP", T[f"vspw{b}"])
                            S.wait(sync, "SP", T.get(f"vrd_used{vrp}"))
                            ins = sync.dma_start(
                                s_vrd[vrp][:], vsp[:, b * NBT:(b + 1) * NBT, :])
                            T["vrd"] = S.inc("spvr", ins, 16)
                        block.sync(f)

                        def f(v, b=b, vrp=vrp):
                            S.wait(v, "DVE", T["vrd"])
                            S.wait(v, "DVE", T.get("tapguard2"))
                            S.wait(v, "DVE", T["init_vbuf"])
                            for j in range(NBT):
                                tt = b * NBT + j
                                ins = v.tensor_copy(
                                    vbuf[:, tt, :, 0:64],
                                    s_vrd[vrp][:, j, :].rearrange(
                                        "p (h c) -> p h c", h=2))
                            T[f"vrd_used{vrp}"] = S.inc("dve", ins)
                            T["vbuf_wr"] = T[f"vrd_used{vrp}"]
                        block.vector(f)

                    for h0, hw in chunks(w, 512):
                        def f(pe, h0=h0, hw=hw, par=par):
                            S.wait(pe, "PE", ktmpT[par])
                            S.wait(pe, "PE", T.get("wop"))
                            ins = pe.matmul(
                                p_rot[:, 0:hw], s_perm[:],
                                s_ktmp[par][:, h0:h0 + hw],
                                start=True, stop=True)
                            rotT[par] = S.inc("pe", ins)
                            T["rot_roll"] = rotT[par]
                        block.tensor(f)

                        def f(v, h0=h0, hw=hw, par=par, ct=ct):
                            S.wait(v, "DVE", rotT[par])
                            S.wait(v, "DVE", ct)
                            ins = v.tensor_tensor(
                                s_w[par][:, h0:h0 + hw], p_rot[:, 0:hw],
                                s_sk[par][:, h0:h0 + hw], ALU.mult)
                            T["wop"] = S.inc("dve", ins)
                            wopT[par] = T["wop"]
                        block.vector(f)

                    def f(v, par=par, ct=ct, dhat=dhat, c0=c0, w=w, kind=kind):
                        S.wait(v, "DVE", ktmpT[par])
                        S.wait(v, "DVE", T.get("tapguard2"))
                        S.wait(v, "DVE", ct)
                        ins = v.tensor_tensor(
                            s_u[par][:, 0:w], s_ktmp[par][:, 0:w],
                            s_ck[par][:, 0:w], ALU.mult)
                        tuT[par] = S.inc("dve", ins)
                        S.wait(v, "DVE", tuT[par])
                        S.wait(v, "DVE", wopT[par])
                        ins = v.tensor_tensor(
                            dhat[0:64, 0, c0:c0 + w],
                            s_u[par][0:64, 0:w], s_w[par][0:64, 0:w], ALU.add)
                        S.inc("dve", ins)
                        ins = v.tensor_tensor(
                            dhat[0:64, 1, c0:c0 + w],
                            s_u[par][64:128, 0:w], s_w[par][64:128, 0:w],
                            ALU.add)
                        hatT[par] = S.inc("dve", ins)
                        T["hat"] = hatT[par]
                        if kind == "kv":
                            S.wait(v, "DVE", T.get("nrm_red"))
                            ins = v.reduce_max(
                                s_t1v[:], s_ktmp[par][:, 0:w], axis=AX.X,
                                apply_absolute_value=True)
                            t = S.inc("dve", ins)
                            S.wait(v, "DVE", t)
                            ins = v.tensor_tensor(
                                s_rmaxv[:], s_rmaxv[:], s_t1v[:], ALU.max)
                            T["nrm_red"] = S.inc("dve", ins)
                    block.vector(f)
                    if kind == "q" or hg % 2 == 1:
                        xb_use[par] = pkT[par]
                    else:
                        xb_use[par] = T["pv"]
                    ck_use[par] = hatT[par]

                def f(v):
                    S.wait(v, "DVE", T["nrm_red"])
                    ins = v.tensor_tensor(
                        s_t1b[:], s_rmaxv[:], s_rmaxv[:], ALU.mult)
                    T["rmx_bf"] = S.inc("dve", ins)
                block.vector(f)

                def f(pe):
                    S.wait(pe, "PE", T["rmx_bf"])
                    S.wait(pe, "PE", T.get("q2mm"))
                    for h01 in range(2):
                        ins = pe.matmul(
                            p_nrm[:, h01:h01 + 1],
                            s_bones[:, h01:h01 + 1], s_t1b[:],
                            start=True, stop=True)
                    T["mk2"] = S.inc("pe", ins)
                block.tensor(f)

                def f(a):
                    S.wait(a, "ACT", T["mk2"])
                    ins = a.activation(s_t11[:], p_nrm[:, 0:2], AF.Ln)
                    t = S.inc("act", ins)
                    S.wait(a, "ACT", t)
                    ins = a.activation(
                        s_bias2[:], s_t11[:], AF.Identity, scale=0.5,
                        bias=s_ln102[:])
                    T["bias2"] = S.inc("act", ins)
                block.scalar(f)

                for h01 in range(2):
                    for c0, cw in chunks(T_C, 512):
                        def f(v, h01=h01, c0=c0, cw=cw):
                            S.wait(v, "DVE", T["hat"])
                            S.wait(v, "DVE", T.get("q2mm"))
                            ins = v.tensor_tensor(
                                s_sq2[:, 0:cw],
                                qhat[0:64, h01, c0:c0 + cw],
                                qhat[0:64, h01, c0:c0 + cw], ALU.mult)
                            T["sq2"] = S.inc("dve", ins)
                        block.vector(f)

                        def f(pe, c0=c0, cw=cw):
                            S.wait(pe, "PE", T["sq2"])
                            S.wait(pe, "PE", T["bias2"])
                            S.wait(pe, "PE", T.get("lnq"))
                            for h0, hw in chunks(cw, 512):
                                ins = pe.matmul(
                                    p_nrm[:, h0:h0 + hw], s_ones64[:],
                                    s_sq2[:, h0:h0 + hw],
                                    start=True, stop=True)
                            T["q2mm"] = S.inc("pe", ins)
                        block.tensor(f)

                        def f(a, h01=h01, c0=c0, cw=cw):
                            S.wait(a, "ACT", T["q2mm"])
                            S.wait(a, "ACT", T["bias2"])
                            ins = a.activation(s_lq[:, 0:cw], p_nrm[:, 0:cw], AF.Ln)
                            T["lnq"] = S.inc("act", ins)
                            S.wait(a, "ACT", T["lnq"])
                            ins = a.activation(
                                s_brow[:, 0:cw], s_lq[:, 0:cw], AF.Exp,
                                scale=0.5, bias=s_bias2[:, h01:h01 + 1])
                            t = S.inc("act", ins)
                            S.wait(a, "ACT", t)
                            ins = a.mul(
                                qhat[64:65, h01, c0:c0 + cw], s_brow[:, 0:cw],
                                -1.0)
                            T["brow"] = S.inc("act", ins)
                        block.scalar(f)

                def f(v):
                    S.wait(v, "DVE", T["hat"])
                    S.wait(v, "DVE", T["brow"])
                    ins = v.memset(khat[64:65, :, :], 1.0)
                    T["khat_ones"] = S.inc("dve", ins)
                block.vector(f)

                if debug_taps and hg == 0:
                    def f(sync):
                        S.wait(sync, "SP", T["khat_ones"])
                        S.wait(sync, "SP", T["vbuf_wr"])
                        for name, srcap in [
                            ("tap_khat", khat[:, 0, :]),
                            ("tap_qhat", qhat[:, 0, :]),
                            ("tap_vbuf", vbuf[:, 0, 0, :]),
                        ]:
                            S.wait(sync, "SP", T.get("lasttap"))
                            ins = sync.dma_start(taps[name], srcap)
                            T["tapguard2"] = S.inc("sptap", ins, 16)
                            T["lasttap"] = T["tapguard2"]
                    block.sync(f)

            with ExitStack() as _es:
                p_s0 = _es.enter_context(nc.psum_tensor(f"b_s0{hg}", [128, QC], F32))
                p_s1 = _es.enter_context(nc.psum_tensor(f"b_s1{hg}", [128, QC], F32))
                p_acc = _es.enter_context(nc.psum_tensor(f"b_acc{hg}", [65, QC], F32))
                p_bl = _es.enter_context(nc.psum_tensor(f"b_bl{hg}", [64, QC], F32))
                p_s = [p_s0, p_s1]
                smm_par = {}
                exp_par = {}
                pv_par = {}
                pending = [None]

                def emit_s(h01, qc, kt):
                    par = kt % 2

                    def f(pe, h01=h01, kt=kt, par=par, qc=qc):
                        S.wait(pe, "PE", T["khat_ones"])
                        S.wait(pe, "PE", exp_par.get(par))
                        for h0, hw in chunks(QC, 512):
                            ins = pe.matmul(
                                p_s[par][:, h0:h0 + hw],
                                khat[0:65, h01, kt * 128:(kt + 1) * 128],
                                qhat[0:65, h01,
                                     qc * QC + h0:qc * QC + h0 + hw],
                                start=True, stop=True)
                        smm_par[par] = S.inc("pe", ins)
                    block.tensor(f)

                for h01 in range(2):
                    for qc in range(NQC):
                        emit_s(h01, qc, 0)
                        for kt in range(NT_B):
                            par = kt % 2
                            if kt + 1 < NT_B:
                                emit_s(h01, qc, kt + 1)
                            if kt == 4 and pending[0] is not None:
                                pending[0]()
                                pending[0] = None

                            def f(a, kt=kt, par=par):
                                S.wait(a, "ACT", smm_par[par])
                                S.wait(a, "ACT", pv_par.get(par))
                                ins = a.activation(
                                    s_p[par][:], p_s[par][:], AF.Exp,
                                    scale=0.125)
                                exp_par[par] = S.inc("act", ins)
                            block.scalar(f)

                            def f(pe, h01=h01, kt=kt, par=par):
                                S.wait(pe, "PE", exp_par[par])
                                if kt == 0:
                                    S.wait(pe, "PE", T.get("vbuf_wr"))
                                    S.wait(pe, "PE", T.get("tacc"))
                                for h0, hw in chunks(QC, 512):
                                    ins = pe.matmul(
                                        p_acc[:, h0:h0 + hw],
                                        vbuf[:, kt, h01, :],
                                        s_p[par][:, h0:h0 + hw],
                                        start=(kt == 0), stop=(kt == NT_B - 1))
                                pv_par[par] = S.inc("pe", ins)
                                T["pvlast"] = pv_par[par]
                            block.tensor(f)

                        def f(a):
                            S.wait(a, "ACT", T["pvlast"])
                            S.wait(a, "ACT", T["hat"])
                            S.wait(a, "ACT", T.get("attnT_wr"))
                            ins = a.copy(s_tacc, p_acc[:])
                            T["tacc"] = S.inc("act", ins)
                        block.scalar(f)

                        def evac(h01=h01, hg=hg, qc=qc, tacc_tok=None):
                            def f(v):
                                S.wait(v, "DVE", T["tacc"])
                                S.wait(v, "DVE", T["hat"])
                                S.wait(v, "DVE", T.get("bl"))
                                ins = v.reciprocal(s_rl, s_cc2[0][64:65, 0:QC])
                                T["rl"] = S.inc("dve", ins)
                            block.vector(f)

                            def f(pe):
                                S.wait(pe, "PE", T["rl"])
                                S.wait(pe, "PE", T.get("attnT_wr"))
                                for h0, hw in chunks(QC, 512):
                                    ins = pe.matmul(
                                        p_bl[:, h0:h0 + hw], s_one1[:, 0:64],
                                        s_cc[0][0:1, h0:h0 + hw],
                                        start=True, stop=True)
                                T["bl"] = S.inc("pe", ins)
                            block.tensor(f)

                            def f(v):
                                S.wait(v, "DVE", T["bl"])
                                ins = v.tensor_tensor(
                                    attnT[h01 * 64:(h01 + 1) * 64, hg,
                                          qc * QC:(qc + 1) * QC],
                                    s_cc2[0][0:64, 0:QC], p_bl[:], ALU.mult)
                                T["attnT_wr"] = S.inc("dve", ins)
                            block.vector(f)
                        pending[0] = evac

                if pending[0] is not None:
                    pending[0]()
                    pending[0] = None

            T["attn_done"] = T["attnT_wr"]

        attnT_done = T["attn_done"]

    if debug_taps:
        def f(sync):
            S.wait(sync, "SP", attnT_done)
            S.wait(sync, "SP", T.get("lasttap"))
            ins = sync.dma_start(taps["tap_attnT"], attnT[:, 0, :])
            T["tapguard5"] = S.inc("sptap", ins, 16)
            T["lasttap"] = T["tapguard5"]
        block.sync(f)

    xz_cm = nc.sbuf_tensor("xz", [128, NT_C, D], F32, side="right")
    xz = xz_cm.__enter__()
    xn2T_cm = nc.sbuf_tensor("xn2T", [128, NDT, T_C], BF16, side="right")
    xn2T = xn2T_cm.__enter__()
    if True:
        with ExitStack() as _es:
            s_g2b = _es.enter_context(nc.sbuf_tensor("c_g2b", [128, D], F32))
            s_be2b = _es.enter_context(nc.sbuf_tensor("c_be2b", [128, D], F32))
            s_bob = _es.enter_context(nc.sbuf_tensor("c_bob", [128, D], F32))
            s_crow = _es.enter_context(nc.sbuf_tensor("c_row", [1, D], F32))
            c_x0 = _es.enter_context(nc.sbuf_tensor("c_x0", [128, D], F32))
            c_x1 = _es.enter_context(nc.sbuf_tensor("c_x1", [128, D], F32))
            s_z = _es.enter_context(nc.sbuf_tensor("c_z", [128, D], F32))
            s_xn2 = _es.enter_context(nc.sbuf_tensor("c_xn2", [128, D], BF16))
            s_x = [c_x0, c_x1]
            s_wo = _es.enter_context(nc.sbuf_tensor("c_wo", [128, NDT, D], BF16))
            p_o = _es.enter_context(nc.psum_tensor("c_o", [128, D], F32))
            p_bc2 = _es.enter_context(nc.psum_tensor("c_bc", [128, 512], F32))
            p_tr0 = _es.enter_context(nc.psum_tensor("c_tr0", [128, 128], BF16))
            p_tr1 = _es.enter_context(nc.psum_tensor("c_tr1", [128, 128], BF16))
            p_tr = [p_tr0, p_tr1]

            def f(gp):
                S.wait(gp, "GP", attnT_done)
                ins = gp.dma_start(
                    s_wo[:], wo_i.rearrange("(dt p) c -> p dt c", p=128))
                T["wo"] = S.inc("gpw", ins, 16)
            block.gpsimd(f)

            for name, src, dst in [
                ("g2b", g2_i, s_g2b), ("be2b", be2_i, s_be2b),
                ("bob", bo_i, s_bob),
            ]:
                def f(sync, src=src, name=name):
                    S.wait(sync, "SP", attnT_done)
                    S.wait(sync, "SP", T.get("bcrow_used"))
                    ins = sync.dma_start(
                        s_crow[:], src.rearrange("(one d) -> one d", one=1))
                    T[name + "_row"] = S.inc("sprow", ins, 16)
                block.sync(f)
                emit_bcast(s_crow, dst, p_bc2, T[name + "_row"], name)

            xuse = [None, None]
            trcp = [None, None]
            for tt in range(NT_C):
                par = tt % 2

                def f(sync, tt=tt, par=par):
                    S.wait(sync, "SP", attnT_done)
                    S.wait(sync, "SP", xuse[par])
                    ins = sync.dma_start(
                        s_x[par][:], x_b[tt * 128:(tt + 1) * 128, :])
                    T["cx"] = S.inc(f"spx{par}", ins, 16)
                block.sync(f)
                cx = T["cx"]

                def f(pe, tt=tt):
                    S.wait(pe, "PE", attnT_done)
                    S.wait(pe, "PE", T["wo"])
                    S.wait(pe, "PE", T.get("z1"))
                    for dt in range(NDT):
                        for h0, hw in chunks(D, 512):
                            ins = pe.matmul(
                                p_o[:, h0:h0 + hw],
                                attnT[:, dt, tt * 128:(tt + 1) * 128],
                                s_wo[:, dt, h0:h0 + hw],
                                start=(dt == 0), stop=(dt == NDT - 1))
                    T["po"] = S.inc("pe", ins)
                block.tensor(f)

                t_r, _ = emit_ln(s_x[par][:], [cx], f"cy{tt}", par)

                def f(v, tt=tt, par=par):
                    S.wait(v, "DVE", t_r)
                    ins = v.tensor_scalar(
                        s_yy[par][:], s_cc[par][:], s_rstd[par][:], None,
                        ALU.mult)
                    t = S.inc("dve", ins)
                    S.wait(v, "DVE", t)
                    S.wait(v, "DVE", T["po"])
                    S.wait(v, "DVE", T["bob_tok"])
                    ins = v.tensor_tensor(s_z[:], p_o[:], s_yy[par][:], ALU.add)
                    T["z1"] = S.inc("dve", ins)
                    S.wait(v, "DVE", T["z1"])
                    ins = v.tensor_tensor(s_z[:], s_z[:], s_bob[:], ALU.add)
                    T["z2"] = S.inc("dve", ins)
                    S.wait(v, "DVE", T["z2"])
                    ins = v.tensor_tensor(
                        xz[:, tt, :], s_x[par][:], s_z[:], ALU.add)
                    T[f"xz{tt}"] = S.inc("dve", ins)
                    xuse[par] = T[f"xz{tt}"]
                    T[f"ln_guard{par}"] = T[f"xz{tt}"]
                block.vector(f)

                t_r2, _ = emit_ln(s_z[:], [T["z2"]], f"cn{tt}", par)

                def f(v, tt=tt, par=par):
                    S.wait(v, "DVE", t_r2)
                    S.wait(v, "DVE", T["g2b_tok"])
                    S.wait(v, "DVE", T["be2b_tok"])
                    S.wait(v, "DVE", T.get("xn2T_cp"))
                    ins = v.scalar_tensor_tensor(
                        s_cc2[par][:], s_cc[par][:], s_rstd[par][:], s_g2b[:],
                        ALU.mult, ALU.mult)
                    t = S.inc("dve", ins)
                    S.wait(v, "DVE", t)
                    ins = v.tensor_tensor(s_xn2[:], s_cc2[par][:], s_be2b[:],
                                          ALU.add)
                    T["xn2"] = S.inc("dve", ins)
                    T[f"ln_guard{par}"] = T["xn2"]
                block.vector(f)

                for j in range(NDT):
                    tp = j % 2

                    def f(pe, tt=tt, j=j, tp=tp):
                        S.wait(pe, "PE", T["xn2"])
                        S.wait(pe, "PE", trcp[tp])
                        ins = pe.transpose(
                            p_tr[tp][:], s_xn2[:, j * 128:(j + 1) * 128],
                            s_idn[:])
                        T["ctr"] = S.inc("pe", ins)
                    block.tensor(f)

                    def f(a, tt=tt, j=j, tp=tp):
                        S.wait(a, "ACT", T["ctr"])
                        ins = a.copy(
                            xn2T[:, j, tt * 128:(tt + 1) * 128], p_tr[tp][:])
                        trcp[tp] = S.inc("act", ins)
                        T["xn2T_cp"] = trcp[tp]
                    block.scalar(f)
            T["xn2T_done"] = T["xn2T_cp"]

            if debug_taps:
                def f(sync):
                    S.wait(sync, "SP", T["xn2T_done"])
                    S.wait(sync, "SP", T["xz0"])
                    S.wait(sync, "SP", T.get("lasttap"))
                    ins = sync.dma_start(taps["tap_xz"], xz[:, 0, :])
                    T["lasttap"] = S.inc("sptap", ins, 16)
                    S.wait(sync, "SP", T.get("lasttap"))
                    ins = sync.dma_start(taps["tap_xn2T"], xn2T[:, 0, :])
                    T["tapguard3"] = S.inc("sptap", ins, 16)
                    T["lasttap"] = T["tapguard3"]
                block.sync(f)

        attnT_cm.__exit__(None, None, None)

        with ExitStack() as _es:
            s_b2b = _es.enter_context(nc.sbuf_tensor("c_b2b", [128, D], F32))
            s_crow2 = _es.enter_context(nc.sbuf_tensor("c_row2", [1, D], F32))

            s_w1a = _es.enter_context(nc.sbuf_tensor("c_w1a", [128, NDT, FC], BF16))
            s_w1b = _es.enter_context(nc.sbuf_tensor("c_w1b", [128, NDT, FC], BF16))
            s_w2a = _es.enter_context(nc.sbuf_tensor("c_w2a", [128, NFT, D], BF16))
            s_w2b = _es.enter_context(nc.sbuf_tensor("c_w2b", [128, NFT, D], BF16))
            s_b1 = _es.enter_context(nc.sbuf_tensor("c_b1", [128, NFT], F32))
            s_b1x = _es.enter_context(nc.sbuf_tensor("c_b1x", [128, NFT], F32))
            s_gel = _es.enter_context(nc.sbuf_tensor("c_gel", [128, NFT, T_C], BF16))
            p_a0 = _es.enter_context(nc.psum_tensor("c_a0", [128, 1024], F32))
            p_a1 = _es.enter_context(nc.psum_tensor("c_a1", [128, 1024], F32))
            p_b0 = _es.enter_context(nc.psum_tensor("c_pb0", [128, D], F32))
            p_b1 = _es.enter_context(nc.psum_tensor("c_pb1", [128, D], F32))
            p_ap = [p_a0, p_a1]
            p_bp = [p_b0, p_b1]

            def f(sync):
                S.wait(sync, "SP", T["xn2T_done"])
                S.wait(sync, "SP", T.get("tapguard5"))
                S.wait(sync, "SP", T.get("bcrow_used"))
                ins = sync.dma_start(
                    s_crow2[:], b2_i.rearrange("(one d) -> one d", one=1))
                T["b2b_row"] = S.inc("sprow", ins, 16)
            block.sync(f)
            emit_bcast(s_crow2, s_b2b, p_a0, T["b2b_row"], "b2b")
            s_w1 = [s_w1a, s_w1b]
            s_w2 = [s_w2a, s_w2b]
            s_b1p = [s_b1, s_b1x]
            NFC_MLP = DFF // FC
            w_toks = {}
            w_used = {}

            def mlp_prefetch(fc):
                par = fc % 2

                def f(gp, fc=fc, par=par):
                    S.wait(gp, "GP", w_used.get(fc - 2))
                    if fc < 2:
                        S.wait(gp, "GP", T["xn2T_done"])
                    ins = gp.dma_start(
                        s_w1[par][:],
                        w1_r[fc].rearrange("(dt p) c -> p dt c", p=128))
                    S.inc(f"gpw1{par}", ins, 16)
                    ins = gp.dma_start(
                        s_w2[par][:],
                        w2_i[fc * FC:(fc + 1) * FC, :].rearrange(
                            "(ft p) c -> p ft c", p=128))
                    S.inc(f"gpw1{par}", ins, 16)
                    ins = gp.dma_start(
                        s_b1p[par][:, 0:NFT], b1_i[fc])
                    w_toks[fc] = S.inc(f"gpw1{par}", ins, 16)
                block.gpsimd(f)

            mlp_prefetch(0)
            if NFC_MLP > 1:
                mlp_prefetch(1)

            def f(a):
                load_act_set(a, 10)
            block.scalar(f)

            for fc in range(NFC_MLP):
                par = fc % 2
                for ft in range(NFT):
                    for h0, hw in chunks(T_C, 1024):
                        ap = (ft * (T_C // 1024) + h0 // 1024) % 2

                        def f(pe, fc=fc, ft=ft, h0=h0, hw=hw, par=par, ap=ap):
                            S.wait(pe, "PE", T["xn2T_done"])
                            S.wait(pe, "PE", w_toks[fc])
                            S.wait(pe, "PE", T.get("b2b_tok"))
                            S.wait(pe, "PE", T.get(f"gel{ap}"))
                            for dt in range(NDT):
                                for q0, qw in chunks(hw, 512):
                                    ins = pe.matmul(
                                        p_ap[ap][:, q0:q0 + qw],
                                        s_w1[par][:, dt, ft * 128:(ft + 1) * 128],
                                        xn2T[:, dt, h0 + q0:h0 + q0 + qw],
                                        start=(dt == 0), stop=(dt == NDT - 1))
                            T["pa"] = S.inc("pe", ins)
                        block.tensor(f)

                        def f(a, fc=fc, ft=ft, h0=h0, hw=hw, par=par, ap=ap):
                            S.wait(a, "ACT", T["pa"])
                            ins = a.activation(
                                s_gel[:, ft, h0:h0 + hw], p_ap[ap][:, 0:hw],
                                AF.Gelu, bias=s_b1p[par][:, ft:ft + 1])
                            T[f"gel{ap}"] = S.inc("act", ins)
                            T["gel"] = T[f"gel{ap}"]
                        block.scalar(f)

                for tt in range(NT_C):
                    bp = tt % 2

                    def f(pe, fc=fc, tt=tt, par=par, bp=bp):
                        S.wait(pe, "PE", T["gel"])
                        S.wait(pe, "PE", T.get(f"xzup{bp}"))
                        for ft in range(NFT):
                            for h0, hw in chunks(D, 512):
                                ins = pe.matmul(
                                    p_bp[bp][:, h0:h0 + hw],
                                    s_gel[:, ft, tt * 128:(tt + 1) * 128],
                                    s_w2[par][:, ft, h0:h0 + hw],
                                    start=(ft == 0), stop=(ft == NFT - 1))
                        T["pb"] = S.inc("pe", ins)
                    block.tensor(f)

                    def f(v, fc=fc, tt=tt, bp=bp):
                        S.wait(v, "DVE", T["pb"])
                        S.wait(v, "DVE", T.get("tapguard3"))
                        S.wait(v, "DVE", T[f"xz{tt}"])
                        ins = v.tensor_tensor(
                            xz[:, tt, :], xz[:, tt, :], p_bp[bp][:], ALU.add)
                        T[f"xz{tt}"] = S.inc("dve", ins)
                        if fc == NFC_MLP - 1:
                            S.wait(v, "DVE", T[f"xz{tt}"])
                            S.wait(v, "DVE", T["b2b_tok"])
                            ins = v.tensor_tensor(
                                xz[:, tt, :], xz[:, tt, :], s_b2b[:], ALU.add)
                            T[f"xz{tt}"] = S.inc("dve", ins)
                        T[f"xzup{bp}"] = T[f"xz{tt}"]
                        T["xzup"] = T[f"xz{tt}"]
                    block.vector(f)
                w_used[fc] = T["xzup"]
                if fc + 2 < NFC_MLP:
                    mlp_prefetch(fc + 2)

        xn2T_cm.__exit__(None, None, None)

        with ExitStack() as _es:
            xfT = _es.enter_context(nc.sbuf_tensor("d_xfT", [128, NDT, T_C], BF16))
            s_cw = _es.enter_context(nc.sbuf_tensor("d_cw", [128, NT_C, NE], F32))
            s_xf = _es.enter_context(nc.sbuf_tensor("d_xf", [128, D], BF16))
            s_rwt = _es.enter_context(nc.sbuf_tensor("d_rwt", [128, NDT, NE], BF16))
            s_rr = _es.enter_context(nc.sbuf_tensor("d_rr", [128, NT_C, NE], F32))
            s_er = _es.enter_context(nc.sbuf_tensor("d_er", [128, NT_C, NE], F32))
            s_mk1 = _es.enter_context(nc.sbuf_tensor("d_msk1", [128, NT_C, NE], F32))
            s_mk2 = _es.enter_context(nc.sbuf_tensor("d_msk2", [128, NT_C, NE], F32))
            s_m1 = _es.enter_context(nc.sbuf_tensor("d_m1", [128, NT_C], F32))
            s_m2 = _es.enter_context(nc.sbuf_tensor("d_m2", [128, NT_C], F32))
            s_den = _es.enter_context(nc.sbuf_tensor("d_den", [128, NT_C], F32))
            s_rden = _es.enter_context(nc.sbuf_tensor("d_rden", [128, NT_C], F32))
            s_negbig = _es.enter_context(nc.sbuf_tensor("d_negbig", [128, 1], F32))
            def f(gp):
                S.wait(gp, "GP", w_used.get(NFC_MLP - 1))
                ins = gp.dma_start(
                    s_rwt[:], rwt_i.rearrange("(dt p) e -> p dt e", p=128))
                T["rwt"] = S.inc("gpw", ins, 16)
            block.gpsimd(f)

            def f(v):
                S.wait(v, "DVE", w_used.get(NFC_MLP - 1))
                ins = v.memset(s_negbig[:], -1e30)
                T["negbig"] = S.inc("dve", ins)
            block.vector(f)

            def f(a):
                load_act_set(a, 6)
            block.scalar(f)

            with ExitStack() as _es:
                p_tr0 = _es.enter_context(nc.psum_tensor("d_tr0", [128, 128], BF16))
                p_tr1 = _es.enter_context(nc.psum_tensor("d_tr1", [128, 128], BF16))
                p_lg = _es.enter_context(nc.psum_tensor("d_lg", [128, NE], F32))
                p_tr = [p_tr0, p_tr1]
                trcp = [None, None]
                for tt in range(NT_C):
                    par = tt % 2
                    t_r, _ = emit_ln(xz[:, tt, :], [T[f"xz{tt}"]], f"dl{tt}", par)

                    def f(v, tt=tt, par=par):
                        S.wait(v, "DVE", t_r)
                        S.wait(v, "DVE", T.get("xfT_cp"))
                        ins = v.tensor_scalar(
                            s_xf[:], s_cc[par][:], s_rstd[par][:], None,
                            ALU.mult)
                        T["xf"] = S.inc("dve", ins)
                        T[f"ln_guard{par}"] = T["xf"]
                    block.vector(f)

                    for j in range(NDT):
                        tp = j % 2

                        def f(pe, tt=tt, j=j, tp=tp):
                            S.wait(pe, "PE", T["xf"])
                            S.wait(pe, "PE", trcp[tp])
                            ins = pe.transpose(
                                p_tr[tp][:], s_xf[:, j * 128:(j + 1) * 128],
                                s_idn[:])
                            T["dtr"] = S.inc("pe", ins)
                        block.tensor(f)

                        def f(a, tt=tt, j=j, tp=tp):
                            S.wait(a, "ACT", T["dtr"])
                            ins = a.copy(
                                xfT[:, j, tt * 128:(tt + 1) * 128], p_tr[tp][:])
                            trcp[tp] = S.inc("act", ins)
                            T["xfT_cp"] = trcp[tp]
                        block.scalar(f)

                    def f(pe, tt=tt):
                        S.wait(pe, "PE", T["xfT_cp"])
                        S.wait(pe, "PE", T["rwt"])
                        S.wait(pe, "PE", T.get("rr_ev"))
                        for dt in range(NDT):
                            ins = pe.matmul(
                                p_lg[:], xfT[:, dt, tt * 128:(tt + 1) * 128],
                                s_rwt[:, dt, :],
                                start=(dt == 0), stop=(dt == NDT - 1))
                        T["lg"] = S.inc("pe", ins)
                    block.tensor(f)

                    def f(a, tt=tt):
                        S.wait(a, "ACT", T["lg"])
                        ins = a.copy(s_rr[:, tt, :], p_lg[:])
                        T["rr_ev"] = S.inc("act", ins)
                        T[f"rr{tt}"] = T["rr_ev"]
                    block.scalar(f)

                    def f(sync, tt=tt):
                        S.wait(sync, "SP", T[f"xz{tt}"])
                        ins = sync.dma_start(
                            h1sp[tt * 128:(tt + 1) * 128, :], xz[:, tt, :])
                        T["h1sp_all"] = S.inc("sph1s", ins, 16)
                    block.sync(f)

                if debug_taps:
                    def f(sync):
                        S.wait(sync, "SP", T["xz0"])
                        S.wait(sync, "SP", T.get("lasttap"))
                        ins = sync.dma_start(taps["tap_h1"], xz[:, 0, :])
                        T["tapguard4"] = S.inc("sptap", ins, 16)
                        T["lasttap"] = T["tapguard4"]
                    block.sync(f)

            def f(v):
                for tt in range(NT_C):
                    S.wait(v, "DVE", T[f"rr{tt}"])
                ins = v.reduce_max(
                    s_m1[:].rearrange("p (t one) -> p t one", one=1),
                    s_rr[:], axis=AX.X)
                T["rmx"] = S.inc("dve", ins)
            block.vector(f)

            def f(a):
                S.wait(a, "ACT", T["rmx"])
                for tt in range(NT_C):
                    ins = a.mul(s_m2[:, tt:tt + 1], s_m1[:, tt:tt + 1], -1.0)
                    t = S.inc("act", ins)
                    S.wait(a, "ACT", t)
                    ins = a.activation(
                        s_er[:, tt, :], s_rr[:, tt, :], AF.Exp,
                        bias=s_m2[:, tt:tt + 1])
                    t = S.inc("act", ins)
                T["er"] = t
            block.scalar(f)

            def f(v):
                S.wait(v, "DVE", T["er"])
                ins = v.reduce_max(
                    s_m1[:].rearrange("p (t one) -> p t one", one=1),
                    s_er[:], axis=AX.X)
                t = S.inc("dve", ins)
                S.wait(v, "DVE", t)
                for tt in range(NT_C):
                    ins = v.tensor_scalar(
                        s_mk1[:, tt, :], s_er[:, tt, :], s_m1[:, tt:tt + 1],
                        None, ALU.is_ge)
                t = S.inc("dve", ins)
                S.wait(v, "DVE", t)
                S.wait(v, "DVE", T["negbig"])
                ins = v.scalar_tensor_tensor(
                    s_mk2[:], s_mk1[:], s_negbig[:], s_er[:], ALU.mult, ALU.add)
                t = S.inc("dve", ins)
                S.wait(v, "DVE", t)
                ins = v.reduce_max(
                    s_m2[:].rearrange("p (t one) -> p t one", one=1),
                    s_mk2[:], axis=AX.X)
                t = S.inc("dve", ins)
                S.wait(v, "DVE", t)
                for tt in range(NT_C):
                    ins = v.tensor_scalar(
                        s_mk2[:, tt, :], s_mk2[:, tt, :], s_m2[:, tt:tt + 1],
                        None, ALU.is_ge)
                t = S.inc("dve", ins)
                S.wait(v, "DVE", t)
                ins = v.tensor_tensor(s_den[:], s_m1[:], s_m2[:], ALU.add)
                t = S.inc("dve", ins)
                S.wait(v, "DVE", t)
                ins = v.reciprocal(s_rden[:], s_den[:])
                t = S.inc("dve", ins)
                S.wait(v, "DVE", t)
                ins = v.tensor_tensor(s_m1[:], s_m1[:], s_rden[:], ALU.mult)
                t = S.inc("dve", ins)
                ins = v.tensor_tensor(s_m2[:], s_m2[:], s_rden[:], ALU.mult)
                t = S.inc("dve", ins)
                S.wait(v, "DVE", t)
                for tt in range(NT_C):
                    ins = v.tensor_scalar(
                        s_mk1[:, tt, :], s_mk1[:, tt, :], s_m1[:, tt:tt + 1],
                        None, ALU.mult)
                    ins = v.tensor_scalar(
                        s_mk2[:, tt, :], s_mk2[:, tt, :], s_m2[:, tt:tt + 1],
                        None, ALU.mult)
                t = S.inc("dve", ins)
                S.wait(v, "DVE", t)
                ins = v.tensor_tensor(s_cw[:], s_mk1[:], s_mk2[:], ALU.add)
                T["cw"] = S.inc("dve", ins)
            block.vector(f)

            if debug_taps:
                def f(sync):
                    S.wait(sync, "SP", T["cw"])
                    S.wait(sync, "SP", T.get("lasttap"))
                    ins = sync.dma_start(taps["tap_cw"], s_cw[:, 0, :])
                    T["lasttap"] = S.inc("sptap", ins, 16)
                block.sync(f)

            with ExitStack() as _es:
                s_acc = xz
                s_wg0 = _es.enter_context(nc.sbuf_tensor("d_wg0", [128, NDT, FC], BF16))
                s_wg1 = _es.enter_context(nc.sbuf_tensor("d_wg1", [128, NDT, FC], BF16))
                s_wu0 = _es.enter_context(nc.sbuf_tensor("d_wu0", [128, NDT, FC], BF16))
                s_wu1 = _es.enter_context(nc.sbuf_tensor("d_wu1", [128, NDT, FC], BF16))
                s_wd0 = _es.enter_context(nc.sbuf_tensor("d_wd0", [128, NFT, D], BF16))
                s_wd1 = _es.enter_context(nc.sbuf_tensor("d_wd1", [128, NFT, D], BF16))
                s_h = _es.enter_context(nc.sbuf_tensor("d_h", [128, NFT, T_C], BF16))
                s_sg = _es.enter_context(nc.sbuf_tensor("d_sg", [128, 1024], F32))
                s_do = _es.enter_context(nc.sbuf_tensor("d_out", [128, D], F32))
                s_h1t = _es.enter_context(nc.sbuf_tensor("d_h1t", [128, D], F32))
                p_g = _es.enter_context(nc.psum_tensor("d_g", [128, 1024], F32))
                p_u = _es.enter_context(nc.psum_tensor("d_u", [128, 1024], F32))
                p_d0 = _es.enter_context(nc.psum_tensor("d_d0", [128, D], F32))
                p_d1 = _es.enter_context(nc.psum_tensor("d_d1", [128, D], F32))
                s_wg = [s_wg0, s_wg1]
                s_wu = [s_wu0, s_wu1]
                s_wd = [s_wd0, s_wd1]
                p_d = [p_d0, p_d1]

                def f(v):
                    S.wait(v, "DVE", T["cw"])
                    S.wait(v, "DVE", T["h1sp_all"])
                    S.wait(v, "DVE", T.get("tapguard4"))
                    for tt in range(NT_C):
                        ins = v.memset(s_acc[:, tt, :], 0.0)
                    T["acc0"] = S.inc("dve", ins)
                block.vector(f)

                NFC_D = DFF // FC
                steps = [(e, fc) for e in range(NE) for fc in range(NFC_D)]
                wd_toks = {}
                w_usedD = {}

                def d_prefetch(si):
                    e, fc = steps[si]
                    par = si % 2

                    def f(gp, e=e, fc=fc, si=si, par=par):
                        S.wait(gp, "GP", w_usedD.get(si - 2))
                        if si < 2:
                            S.wait(gp, "GP", T["rwt"])
                            S.wait(gp, "GP", T["cw"])
                        ins = gp.dma_start(
                            s_wg[par][:],
                            wg_r[e, fc].rearrange("(dt p) c -> p dt c", p=128))
                        S.inc(f"gpd{par}", ins, 16)
                        ins = gp.dma_start(
                            s_wu[par][:],
                            wu_r[e, fc].rearrange("(dt p) c -> p dt c", p=128))
                        S.inc(f"gpd{par}", ins, 16)
                        ins = gp.dma_start(
                            s_wd[par][:],
                            wd_i[e, fc * FC:(fc + 1) * FC, :].rearrange(
                                "(ft p) c -> p ft c", p=128))
                        wd_toks[si] = S.inc(f"gpd{par}", ins, 16)
                    block.gpsimd(f)

                d_prefetch(0)
                if len(steps) > 1:
                    d_prefetch(1)

                def f(a):
                    load_act_set(a, 18)
                block.scalar(f)

                for si, (e, fc) in enumerate(steps):
                    par = si % 2
                    for ft in range(NFT):
                        for h0, hw in chunks(T_C, 1024):
                            def f(pe, si=si, ft=ft, h0=h0, hw=hw, par=par):
                                S.wait(pe, "PE", wd_toks[si])
                                S.wait(pe, "PE", T["cw"])
                                S.wait(pe, "PE", T.get("sg"))
                                for dt in range(NDT):
                                    for q0, qw in chunks(hw, 512):
                                        ins = pe.matmul(
                                            p_g[:, q0:q0 + qw],
                                            s_wg[par][:, dt,
                                                      ft * 128:(ft + 1) * 128],
                                            xfT[:, dt, h0 + q0:h0 + q0 + qw],
                                            start=(dt == 0),
                                            stop=(dt == NDT - 1))
                                T["pg"] = S.inc("pe", ins)
                            block.tensor(f)

                            def f(a, hw=hw):
                                S.wait(a, "ACT", T["pg"])
                                S.wait(a, "ACT", T.get("hop"))
                                ins = a.activation(
                                    s_sg[:, 0:hw], p_g[:, 0:hw], AF.Silu)
                                T["sg"] = S.inc("act", ins)
                            block.scalar(f)

                            def f(pe, si=si, ft=ft, h0=h0, hw=hw, par=par):
                                S.wait(pe, "PE", T.get("hop"))
                                for dt in range(NDT):
                                    for q0, qw in chunks(hw, 512):
                                        ins = pe.matmul(
                                            p_u[:, q0:q0 + qw],
                                            s_wu[par][:, dt,
                                                      ft * 128:(ft + 1) * 128],
                                            xfT[:, dt, h0 + q0:h0 + q0 + qw],
                                            start=(dt == 0),
                                            stop=(dt == NDT - 1))
                                T["pu"] = S.inc("pe", ins)
                            block.tensor(f)

                            def f(v, ft=ft, h0=h0, hw=hw):
                                S.wait(v, "DVE", T["pu"])
                                S.wait(v, "DVE", T["sg"])
                                S.wait(v, "DVE", T.get(f"pd{NT_C - 1}"))
                                ins = v.tensor_tensor(
                                    s_h[:, ft, h0:h0 + hw], s_sg[:, 0:hw],
                                    p_u[:, 0:hw], ALU.mult)
                                T["hop"] = S.inc("dve", ins)
                            block.vector(f)

                    for tt in range(NT_C):
                        dp = tt % 2

                        def f(pe, si=si, tt=tt, dp=dp, par=par):
                            S.wait(pe, "PE", T["hop"])
                            S.wait(pe, "PE", T.get(f"accup{(tt - 2) % NT_C}"))
                            for ft in range(NFT):
                                for h0, hw in chunks(D, 512):
                                    ins = pe.matmul(
                                        p_d[dp][:, h0:h0 + hw],
                                        s_h[:, ft, tt * 128:(tt + 1) * 128],
                                        s_wd[par][:, ft, h0:h0 + hw],
                                        start=(ft == 0), stop=(ft == NFT - 1))
                            T[f"pd{tt}"] = S.inc("pe", ins)
                        block.tensor(f)

                        def f(v, e=e, tt=tt, dp=dp):
                            S.wait(v, "DVE", T[f"pd{tt}"])
                            S.wait(v, "DVE", T["acc0"])
                            S.wait(v, "DVE", T["cw"])
                            ins = v.scalar_tensor_tensor(
                                s_acc[:, tt, :], p_d[dp][:],
                                s_cw[:, tt, e:e + 1], s_acc[:, tt, :],
                                ALU.mult, ALU.add)
                            T[f"accup{tt}"] = S.inc("dve", ins)
                        block.vector(f)
                    w_usedD[si] = T[f"accup{NT_C - 1}"]
                    if si + 2 < len(steps):
                        d_prefetch(si + 2)

                for tt in range(NT_C):
                    def f(sync, tt=tt):
                        S.wait(sync, "SP", T["h1sp_all"])
                        S.wait(sync, "SP", T.get("dout"))
                        ins = sync.dma_start(
                            s_h1t[:], h1sp[tt * 128:(tt + 1) * 128, :])
                        T["h1t"] = S.inc("sph1t", ins, 16)
                    block.sync(f)

                    def f(v, tt=tt):
                        S.wait(v, "DVE", T["h1t"])
                        S.wait(v, "DVE", T[f"accup{tt}"])
                        S.wait(v, "DVE", T.get("outdma"))
                        ins = v.tensor_tensor(
                            s_do[:], s_h1t[:], s_acc[:, tt, :], ALU.add)
                        T["dout"] = S.inc("dve", ins)
                    block.vector(f)

                    def f(sync, tt=tt):
                        S.wait(sync, "SP", T["dout"])
                        ins = sync.dma_start(
                            out_t[tt * 128:(tt + 1) * 128, :], s_do[:])
                        T["outdma"] = S.inc("spout", ins, 16)
                    block.sync(f)

                def f(sync):
                    for nm in S.n:
                        if nm.startswith("sp") or nm.startswith("gp"):
                            if S.n[nm] > 0:
                                sync.wait_ge(S.h[nm], S.n[nm])
                block.sync(f)

    for cm in reversed(ctx):
        cm.__exit__(None, None, None)
    return nc



def _rope_tables(T):
    inv = (1.0 / np.power(np.float32(10000.0),
                          np.arange(0, HD, 2, dtype=np.float32) / np.float32(HD))
           ).astype(np.float32)
    freqs = np.arange(T, dtype=np.float32)[:, None] * inv[None, :]
    emb = np.concatenate([freqs, freqs], axis=-1)
    cos_td = np.cos(emb).astype(np.float32)
    sin_td = np.sin(emb).astype(np.float32)
    rows = np.arange(128) % HD
    cosk = cos_td.T[rows, :]
    sgn = np.where((np.arange(128) % HD) < 32, -1.0, 1.0).astype(np.float32)
    s1k = sin_td.T[rows, :] * sgn[:, None]
    return np.ascontiguousarray(cosk), np.ascontiguousarray(s1k)


def _perm_bd():
    p = np.zeros((128, 128), np.float32)
    for b in (0, 64):
        for m in range(HD):
            p[b + (m + 32) % HD, b + m] = 1.0
    return p


def make_in_maps(inputs, T_B, T_C, n_cores):
    x = np.asarray(inputs["L0_attn_x"], np.float32)
    cosk, s1k = _rope_tables(T_B)
    idn = np.eye(128, dtype=ml_dtypes.bfloat16)
    perm = _perm_bd()

    def wr(w):
        w = np.asarray(w, np.float32)
        return np.ascontiguousarray(w.reshape(D, NHG, 128).transpose(1, 0, 2))

    shared = {
        "idn": idn, "permbd": perm,
        "wq_r": wr(inputs["L0_attn_Wq"]),
        "wk_r": wr(inputs["L0_attn_Wk"]),
        "wv_r": np.ascontiguousarray(np.asarray(inputs["L0_attn_Wv"], np.float32).reshape(D, NHG // 2, 256).transpose(1, 0, 2)),
        "bq": np.asarray(inputs["L0_attn_bq"], np.float32),
        "bk": np.asarray(inputs["L0_attn_bk"], np.float32),
        "bv": np.asarray(inputs["L0_attn_bv"], np.float32),
        "wo": np.asarray(inputs["L0_attn_Wo"], np.float32),
        "bo": np.asarray(inputs["L0_attn_bo"], np.float32),
        "g1": np.asarray(inputs["L0_attn_g1"], np.float32),
        "be1": np.asarray(inputs["L0_attn_be1"], np.float32),
        "g2": np.asarray(inputs["L0_attn_g2"], np.float32),
        "be2": np.asarray(inputs["L0_attn_be2"], np.float32),
        "w1_r": np.ascontiguousarray(
            np.asarray(inputs["L0_attn_W1"], np.float32)
            .reshape(D, DFF // FC, FC).transpose(1, 0, 2)),
        "b1_r": np.ascontiguousarray(np.asarray(inputs["L0_attn_b1"], np.float32).reshape(DFF // FC, NFT, 128).transpose(0, 2, 1)),
        "w2": np.asarray(inputs["L0_attn_W2"], np.float32),
        "b2": np.asarray(inputs["L0_attn_b2"], np.float32),
        "rwt": np.ascontiguousarray(
            np.asarray(inputs["L0_ffn_router_w"], np.float32).T),
        "wg_r": np.ascontiguousarray(
            np.asarray(inputs["L0_ffn_w_gate"], np.float32)
            .reshape(NE, D, DFF // FC, FC).transpose(0, 2, 1, 3)),
        "wu_r": np.ascontiguousarray(
            np.asarray(inputs["L0_ffn_w_up"], np.float32)
            .reshape(NE, D, DFF // FC, FC).transpose(0, 2, 1, 3)),
        "wd": np.asarray(inputs["L0_ffn_w_down"], np.float32),
    }
    maps = []
    cpb = T_B // T_C
    for c in range(n_cores):
        b = c // cpb
        q0 = (c % cpb) * T_C
        xb = np.roll(x[b], -q0, axis=0)
        m = dict(shared)
        m["x_b"] = np.ascontiguousarray(xb)
        m["cosk"] = np.ascontiguousarray(np.roll(cosk, -q0, axis=1))
        m["s1k"] = np.ascontiguousarray(np.roll(s1k, -q0, axis=1))
        maps.append(m)
    return maps


def kernel(**inputs):
    T_B, T_C = 8192, 2048
    n_cores = 8
    nc = build_program(T_B, T_C)
    maps = make_in_maps(inputs, T_B, T_C, n_cores)
    res = run_bass_kernel_spmd(nc, maps, core_ids=list(range(n_cores)))
    x = inputs["L0_attn_x"]
    B = x.shape[0]
    out = np.empty((B, T_B, D), np.float32)
    cpb = T_B // T_C
    for c in range(n_cores):
        b = c // cpb
        q0 = (c % cpb) * T_C
        out[b, q0:q0 + T_C, :] = res.results[c]["out"]
    return out
